# version 24
# baseline (speedup 1.0000x reference)
import bisect
import math
from contextlib import ExitStack

import numpy as np
import concourse.bass as bass
import concourse.mybir as mybir
from concourse.bass_utils import run_bass_kernel_spmd

F32 = mybir.dt.float32
BF16 = mybir.dt.bfloat16
I32 = mybir.dt.int32
AF = mybir.ActivationFunctionType
OP = mybir.AluOpType

B, T, D = 8, 4096, 1024
DI = 2048
NH = 16
RD = 32
LAT = 256
IM = 2048
NE = 8
EPS = 1e-5
TS_MA = 256
TS_MB = 256
TWO_PI = 2.0 * math.pi

ENGINES = ("tensor", "vector", "scalar", "gpsimd", "sync")
STRICT_SAME_ENGINE = True
DMA_QUEUE = {"sync": "dq_sync", "gpsimd": "dq_gp", "scalar": "dq_sca"}


class _Op:
    __slots__ = ("engine", "fn", "eidx", "marked", "val", "dma", "dmaval",
                 "dmasem", "waits")

    def __init__(self, engine, fn, dma):
        self.engine, self.fn, self.dma = engine, fn, dma
        self.eidx, self.marked, self.val, self.dmaval = -1, False, None, None
        self.waits = []


DMA_POOL = 8


class Sched:

    def __init__(self, nc, es):
        self.nc = nc
        self.ops = {e: [] for e in ENGINES}
        self.sem = {}
        for e in ENGINES:
            self.sem[e] = es.enter_context(nc.semaphore("s_" + e))
        self.dma_pool = {}
        self.dma_uses = {}
        self.dma_prev = {}
        for q in set(DMA_QUEUE.values()):
            self.dma_pool[q] = [es.enter_context(nc.semaphore("%s_%d" % (q, i)))
                                for i in range(DMA_POOL)]
            self.dma_uses[q] = [0] * DMA_POOL
            self.dma_prev[q] = [None] * DMA_POOL
        self.sem_count = {e: 0 for e in ENGINES}
        self.dma_count = {q: 0 for q in set(DMA_QUEUE.values())}
        self.marked_idx = {e: [] for e in ENGINES}
        self.bufs = {}

    def _buf(self, name):
        return self.bufs.setdefault(name, {"w": None, "r": []})

    def _dep(self, op, producer):
        if producer is None or producer is op:
            return
        if producer.dma:
            op.waits.append(producer)
            return
        if producer.engine == op.engine and (producer.engine == "tensor"
                                             or not STRICT_SAME_ENGINE):
            return
        ml = self.marked_idx[producer.engine]
        lst = self.ops[producer.engine]
        i = bisect.bisect_left(ml, producer.eidx)
        if i < len(ml):
            op.waits.append(lst[ml[i]])
        else:
            producer.marked = True
            ml.append(producer.eidx)
            op.waits.append(producer)

    def op(self, engine, fn, reads=(), writes=(), dma=False):
        o = _Op(engine, fn, dma)
        o.eidx = len(self.ops[engine])
        if dma:
            q = DMA_QUEUE[engine]
            slot = self.dma_count[q] % DMA_POOL
            self.dma_count[q] += 1
            if self.dma_uses[q][slot] > 0:
                o.waits.append(("sem", self.dma_pool[q][slot],
                                16 * self.dma_uses[q][slot]))
            self.dma_uses[q][slot] += 1
            o.dmasem = self.dma_pool[q][slot]
            o.dmaval = 16 * self.dma_uses[q][slot]
        for r in reads:
            self._dep(o, self._buf(r)["w"])
        for w in writes:
            b = self._buf(w)
            self._dep(o, b["w"])
            for rd in b["r"]:
                self._dep(o, rd)
        self.ops[engine].append(o)
        for r in reads:
            self._buf(r)["r"].append(o)
        for w in writes:
            b = self._buf(w)
            b["w"] = o
            b["r"] = []
        return o

    def dma_join(self):
        tot = []
        for q in self.dma_pool:
            for i in range(DMA_POOL):
                if self.dma_uses[q][i]:
                    tot.append((self.dma_pool[q][i], 16 * self.dma_uses[q][i]))

        def f(e):
            inst = e.nop()
            for s, v in tot:
                e.wait_ge(s, v)
            return e.nop()
        o = _Op("sync", f, False)
        o.eidx = len(self.ops["sync"])
        self.ops["sync"].append(o)

    def flush(self, block, reset_bufs=True):
        for e in ENGINES:
            for o in self.ops[e]:
                if o.marked and not o.dma:
                    self.sem_count[e] += 1
                    o.val = self.sem_count[e]

        def resolve(w):
            if isinstance(w, tuple):
                return w[1], w[2]
            if w.dma:
                return w.dmasem, w.dmaval
            return self.sem[w.engine], w.val

        def emitter(ename):
            ops = self.ops[ename]

            def emit(eng):
                waited = {}
                for o in ops:
                    for w in o.waits:
                        semh, val = resolve(w)
                        assert val is not None, "unresolved wait"
                        if waited.get(semh.num, -1) < val:
                            eng.wait_ge(semh, val)
                            waited[semh.num] = val
                    inst = o.fn(eng)
                    if o.dma:
                        inst.then_inc(o.dmasem, 16)
                    elif o.marked:
                        inst.then_inc(self.sem[o.engine], 1)
            return emit

        for e in ENGINES:
            if self.ops[e]:
                getattr(block, e)(emitter(e))
        self.ops = {e: [] for e in ENGINES}
        self.marked_idx = {e: [] for e in ENGINES}
        if reset_bufs:
            nb = {}
            for k, v in self.bufs.items():
                if v["w"] is not None and v["w"].dma:
                    nb[k] = {"w": v["w"], "r": []}
            self.bufs = nb

    def mm_group(self, out, pairs, reads, writes, accum=False):
        n = len(pairs)

        def f(e):
            for i, (l, r) in enumerate(pairs):
                inst = e.matmul(out, l, r, start=(i == 0 and not accum),
                                stop=(i == n - 1))
            return inst
        return self.op("tensor", f, reads=reads, writes=writes)


def _stage(inp):
    c = lambda x: np.ascontiguousarray(np.asarray(x, dtype=np.float32))
    w = {}
    W_in = np.asarray(inp["L0_attn_W_in"], np.float32)
    w["wxT"] = c(W_in[6144:8192].T)
    w["wBT"] = c(W_in[8192:8208].T)
    w["wCT"] = c(W_in[8208:8224].T)
    w["wdtT"] = c(W_in[8224:8232].T)
    w["wz0T"] = c(W_in[0:2048].T)
    w["wx0T"] = c(W_in[2048:4096].T)
    cw = np.asarray(inp["L0_attn_conv_w"], np.float32)[:, 0, :]
    cb = np.asarray(inp["L0_attn_conv_b"], np.float32)
    w["convx"] = c(cw[:2048].reshape(16, 128, 4).transpose(1, 0, 2).reshape(128, 64))
    w["convxb"] = c(cb[:2048].reshape(16, 128).T)
    w["convB"] = c(cw[2048:2064])
    w["convBb"] = c(cb[2048:2064][:, None])
    w["convC"] = c(cw[2064:2080])
    w["convCb"] = c(cb[2064:2080][:, None])
    w["dtbias"] = c(np.asarray(inp["L0_attn_dt_bias"], np.float32)[:, None])
    w["arow"] = c(np.asarray(inp["L0_attn_A_log"], np.float32)[None, :])
    w["drow"] = c(np.asarray(inp["L0_attn_D_param"], np.float32)[None, :])
    w["gbrow"] = c(np.concatenate([np.asarray(inp["L0_attn_ln_gamma"], np.float32),
                                   np.asarray(inp["L0_attn_ln_beta"], np.float32)])[None, :])
    w["wgT"] = c(np.asarray(inp["L0_attn_W_gate"], np.float32).T)
    w["bg"] = c(np.asarray(inp["L0_attn_b_gate"], np.float32).reshape(16, 128).T)
    w["woT"] = c(np.asarray(inp["L0_attn_W_out"], np.float32).T)
    w["kvdT"] = c(np.asarray(inp["L0_ffn_kv_d"], np.float32).T)
    w["qdT"] = c(np.asarray(inp["L0_ffn_q_d"], np.float32).T)
    k_u = np.asarray(inp["L0_ffn_k_u"], np.float32).reshape(NH, 64, LAT)
    q_u = np.asarray(inp["L0_ffn_q_u"], np.float32).reshape(NH, 64, LAT)
    w["kuT"] = c(k_u[:, RD:, :].reshape(NH * RD, LAT).T)
    w["quT"] = c(q_u[:, RD:, :].reshape(NH * RD, LAT).T)
    w["vuT"] = c(np.asarray(inp["L0_ffn_v_u"], np.float32).T)
    rq = np.asarray(inp["L0_ffn_rope_q_w"], np.float32)
    rk = np.asarray(inp["L0_ffn_rope_k_w"], np.float32)
    perm = (np.arange(NH)[:, None] * RD + (np.arange(RD)[None, :] + RD // 2) % RD).ravel()
    w["rqT"] = c(rq.T)
    w["rqTp"] = c(rq.T[:, perm])
    w["rkT"] = c(rk.T)
    w["rkTp"] = c(rk.T[:, perm])
    w["owT"] = c(np.asarray(inp["L0_ffn_o_w"], np.float32).T)
    w["ln1c"] = c(np.asarray(inp["L0_ffn_ln1_w"], np.float32).reshape(8, 128).T)
    w["ln2c"] = c(np.asarray(inp["L0_ffn_ln2_w"], np.float32).reshape(8, 128).T)
    w["rwT"] = c(np.asarray(inp["L0_ffn_router_w"], np.float32).T)
    w["rbias"] = c(np.asarray(inp["L0_ffn_routing_bias"], np.float32)[None, :])
    w["moe_g"] = c(np.concatenate([np.asarray(inp["L0_ffn_shared_gate"], np.float32),
                                   np.asarray(inp["L0_ffn_routed_gate"], np.float32)], 0
                                  ).transpose(0, 2, 1).reshape(NE * D, IM))
    w["moe_u"] = c(np.concatenate([np.asarray(inp["L0_ffn_shared_up"], np.float32),
                                   np.asarray(inp["L0_ffn_routed_up"], np.float32)], 0
                                  ).transpose(0, 2, 1).reshape(NE * D, IM))
    w["moe_d"] = c(np.concatenate([np.asarray(inp["L0_ffn_shared_down"], np.float32),
                                   np.asarray(inp["L0_ffn_routed_down"], np.float32)], 0
                                  ).transpose(0, 2, 1).reshape(NE * IM, D))
    u = np.asarray(inp["L0_attn_u"], np.float32)
    in_maps = []
    for b in range(B):
        m = dict(w)
        m["u"] = c(u[b].T)
        in_maps.append(m)
    return in_maps


def fm(hbm, kch, t0, tt):
    return hbm[:, :].rearrange("(k p) t -> p k t", p=128)[:, :, t0:t0 + tt]


def t3(tile, kch):
    return tile[:, :].rearrange("p (k t) -> p k t", k=kch)


HBM_SHAPES = [
    ("u", [D, T]), ("wxT", [D, DI]), ("wBT", [D, 16]), ("wCT", [D, 16]),
    ("wdtT", [D, 8]), ("wz0T", [D, DI]), ("wx0T", [D, DI]),
    ("convx", [128, 64]), ("convxb", [128, 16]), ("convB", [16, 4]),
    ("convBb", [16, 1]), ("convC", [16, 4]), ("convCb", [16, 1]),
    ("dtbias", [8, 1]), ("arow", [1, 8]), ("drow", [1, 8]),
    ("gbrow", [1, 2 * DI]), ("wgT", [DI, DI]), ("bg", [128, 16]),
    ("woT", [2 * DI, D]),
    ("kvdT", [D, LAT]), ("qdT", [D, LAT]), ("kuT", [LAT, 512]),
    ("quT", [LAT, 512]), ("vuT", [LAT, D]), ("rqT", [LAT, 512]),
    ("rqTp", [LAT, 512]), ("rkT", [D, 512]), ("rkTp", [D, 512]),
    ("owT", [D, D]), ("ln1c", [128, 8]), ("ln2c", [128, 8]),
    ("rwT", [D, 7]), ("rbias", [1, 7]),
    ("moe_g", [NE * D, IM]), ("moe_u", [NE * D, IM]), ("moe_d", [NE * IM, D]),
]


class Ctx:
    pass


def build_program(dev=False, upto="FM"):
    nc = bass.Bass("TRN2", target_bir_lowering=False, detect_race_conditions=False)
    skind = "ExternalOutput" if dev else "Internal"
    cx = Ctx()
    cx.dev = dev
    cx.hbm = {}
    for name, shape in HBM_SHAPES:
        cx.hbm[name] = nc.dram_tensor(name, shape, F32, kind="ExternalInput")
    cx.lnu_h = nc.dram_tensor("lnu_h", [D, T], BF16, kind=skind)
    cx.yln_h = nc.dram_tensor("yln_h", [DI, T], BF16, kind=skind)
    cx.yg_h = nc.dram_tensor("yg_h", [DI, T], BF16, kind=skind)
    cx.x1_h = nc.dram_tensor("x1_h", [D, T], F32, kind=skind)
    cx.q_h = nc.dram_tensor("q_h", [128, (T // 128) * 1024], BF16, kind="Internal")
    cx.y1_h = nc.dram_tensor("y1_h", [D, T], F32, kind=skind)
    cx.y0_h = nc.dram_tensor("y0_h", [D, T], F32, kind="Internal")
    cx.out_fm = nc.dram_tensor("out_fm", [D, T], F32, kind="ExternalOutput")
    if dev:
        cx.dbg = {}
        for nm, shape, dt_ in (("d_xs", [128, 16 * TS_MA], BF16),
                               ("d_dtf", [8, 5 * TS_MA], F32),
                               ("d_yrow", [128, DI], F32),
                               ("d_lam", [128, 128], F32),
                               ("d_Mm", [128, 128], BF16),
                               ("d_Xtm", [128, 256], BF16),
                               ("d_srow", [1, 2048], F32),
                               ("d_dttm", [128, 8], F32),
                               ("d_hf", [16, 2048], F32),
                               ("d_Bs", [16, TS_MA], BF16),
                               ("d_Cs", [16, TS_MA], F32),
                               ("d_Csc", [16, 128], BF16),
                               ("d_eall", [16, 8], F32)):
            cx.dbg[nm] = nc.dram_tensor(nm, shape, dt_, kind="ExternalOutput")

    es = ExitStack()
    sch = Sched(nc, es)
    cx.nc, cx.sch = nc, sch

    def sb(name, shape, dt):
        return es.enter_context(nc.sbuf_tensor(name, shape, dt))
    cx.id128_bf = sb("id128_bf", [128, 128], BF16)
    cx.id128_f = sb("id128_f", [128, 128], F32)
    cx.id16_bf = sb("id16_bf", [16, 16], BF16)
    cx.id8_f = sb("id8_f", [8, 8], F32)
    cx.U_f = sb("U_f", [128, 128], F32)
    cx.maskS_f = sb("maskS_f", [128, 128], F32)
    cx.maskS_bf = sb("maskS_bf", [128, 128], BF16)
    cx.maskS2_f = sb("maskS2_f", [128, 256], F32)
    cx.ones_f = sb("ones_f", [128, 128], F32)
    cx.zrow_bf = sb("zrow_bf", [1, 128], BF16)
    cx.cvals = sb("cvals", [128, 2], F32)

    with es:
        with nc.Block() as blk:
            for t_, sq in ((cx.id128_bf, 128), (cx.id128_f, 128),
                           (cx.id16_bf, 16), (cx.id8_f, 8)):
                nm = t_.name
                sch.op("gpsimd", lambda e, t_=t_: e.memset(t_[:, :], 0.0), writes=(nm,))
                sch.op("gpsimd", lambda e, t_=t_, sq=sq: e.affine_select(
                    out=t_[:, :], in_=t_[:, :], compare_op=OP.not_equal,
                    fill=1.0, base=0, pattern=[[-1, sq]], channel_multiplier=1),
                    reads=(nm,), writes=(nm,))
            sch.op("gpsimd", lambda e: e.memset(cx.U_f[:, :], 0.0), writes=("U_f",))
            sch.op("gpsimd", lambda e: e.affine_select(
                out=cx.U_f[:, :], in_=cx.U_f[:, :], compare_op=OP.is_gt,
                fill=1.0, base=0, pattern=[[-1, 128]], channel_multiplier=1),
                reads=("U_f",), writes=("U_f",))
            for mt in (cx.maskS_f, cx.maskS_bf):
                nm = mt.name
                sch.op("gpsimd", lambda e, mt=mt: e.memset(mt[:, :], 0.0), writes=(nm,))
                sch.op("gpsimd", lambda e, mt=mt: e.affine_select(
                    out=mt[:, :], in_=mt[:, :], compare_op=OP.is_ge,
                    fill=-1e30, base=0, pattern=[[1, 128]], channel_multiplier=-1),
                    reads=(nm,), writes=(nm,))
            sch.op("gpsimd", lambda e: e.memset(cx.maskS2_f[:, :], 0.0),
                   writes=("maskS2_f",))
            for off in (0, 128):
                sch.op("gpsimd", lambda e, off=off: e.affine_select(
                    out=cx.maskS2_f[:, off:off + 128],
                    in_=cx.maskS2_f[:, off:off + 128],
                    compare_op=OP.is_ge, fill=-1e30, base=0,
                    pattern=[[1, 128]], channel_multiplier=-1),
                    reads=("maskS2_f",), writes=("maskS2_f",))
            sch.op("gpsimd", lambda e: e.memset(cx.ones_f[:, :], 1.0), writes=("ones_f",))
            sch.op("gpsimd", lambda e: e.memset(cx.zrow_bf[:, :], 0.0), writes=("zrow_bf",))
            sch.op("gpsimd", lambda e: (e.memset(cx.cvals[:, 0:1], EPS),
                                        e.memset(cx.cvals[:, 1:2], math.pi / 2))[-1],
                   writes=("cvals",))
            _phase_MA(cx, nc, sch)
            sch.dma_join()
            sch.flush(blk)
        if upto != "MA":
            with nc.Block() as blk:
                _phase_MG(cx, nc, sch)
                sch.dma_join()
                sch.flush(blk)
        if upto not in ("MA", "MG"):
            with nc.Block() as blk:
                _phase_MB(cx, nc, sch)
                sch.dma_join()
                sch.flush(blk)
        if upto in ("F", "FM"):
            _phase_F(cx, nc, sch)
        if upto == "FM":
            with nc.Block() as blk:
                _phase_FM(cx, nc, sch)
                sch.dma_join()
                sch.flush(blk)
        with nc.Block() as blk:
            def fin(e):
                for q in sch.dma_pool:
                    for i in range(DMA_POOL):
                        if sch.dma_uses[q][i]:
                            e.wait_ge(sch.dma_pool[q][i], 16 * sch.dma_uses[q][i])
                return e.nop()
            blk.sync(fin)
    return nc


def _phase_MA(cx, nc, sch):
    TT = TS_MA
    ntile = T // TT
    nch = TT // 128
    hbm = cx.hbm
    es = ExitStack()
    sb = lambda n, s, d: es.enter_context(nc.sbuf_tensor(n, s, d))
    ps = lambda n, s, d: es.enter_context(nc.psum_tensor(n, s, d))

    wx = sb("wx", [128, 8 * DI], BF16)
    wB = sb("wB", [128, 8 * 16], BF16)
    wC = sb("wC", [128, 8 * 16], BF16)
    wdt = sb("wdt", [128, 8 * 8], BF16)
    convx = sb("convx_s", [128, 64], F32)
    convxb = sb("convxb_s", [128, 16], F32)
    convB = sb("convB_s", [16, 4], F32)
    convBb = sb("convBb_s", [16, 1], F32)
    convC = sb("convC_s", [16, 4], F32)
    convCb = sb("convCb_s", [16, 1], F32)
    dtb = sb("dtb", [8, 1], F32)
    rows8 = sb("rows8", [1, 32], F32)
    gbrow = sb("gbrow_s", [1, 2 * DI], F32)
    A_bc = sb("A_bc", [128, 8], F32)
    idD = sb("idD", [128, 8 * 128], BF16)
    gam_bc = sb("gam_bc", [128, DI], BF16)
    bet_bc = sb("bet_bc", [128, DI], BF16)

    u_t = sb("u_t", [128, 8 * TT], F32)
    sq_t = sb("sq_t", [128, 8 * TT], F32)
    lnu_t = sb("lnu_t", [128, 8 * TT], BF16)
    rows = sb("rows", [1, 5 * TT], F32)
    xconv = sb("xconv", [128, 16 * (TT + 3)], F32)
    Bconv = sb("Bconv", [16, TT + 3], F32)
    Cconv = sb("Cconv", [16, TT + 3], F32)
    cscr = sb("cscr", [128, 2 * TT], F32)
    xs_t = sb("xs_t", [128, 16 * TT], BF16)
    Bs_bf = sb("Bs_bf", [16, TT], BF16)
    Cs_f = sb("Cs_f", [16, TT], F32)
    Cs_bf = sb("Cs_bf", [16, TT], BF16)
    dtf = sb("dtf", [8, 5 * TT], F32)
    dt_tm = [sb("dt_tm0", [128, 8], F32), sb("dt_tm1", [128, 8], F32)]
    dtA_tm = [sb("dtA_tm0", [128, 8], F32), sb("dtA_tm1", [128, 8], F32)]
    ns_tm = [sb("ns_tm0", [128, 8], F32), sb("ns_tm1", [128, 8], F32)]
    e_all = [sb("e_all0", [16, 8], F32), sb("e_all1", [16, 8], F32)]
    prod = [sb("prod0", [128, 8 * 128], F32), sb("prod1", [128, 8 * 128], F32)]
    srow = [sb("srow0", [1, 2 * 1024], F32), sb("srow1", [1, 2 * 1024], F32)]
    lam = [sb("lam0", [128, 128], F32), sb("lam1", [128, 128], F32)]
    Mm = [sb("Mm0", [128, 128], BF16), sb("Mm1", [128, 128], BF16)]
    X_tm = [sb("X_tm0", [128, 256], BF16), sb("X_tm1", [128, 256], BF16)]
    B_tm = sb("B_tm", [128, 16], BF16)
    Bw = [sb("Bw0", [128, 16], BF16), sb("Bw1", [128, 16], BF16)]
    wcol = [sb("wcol0", [128, 1], F32), sb("wcol1", [128, 1], F32)]
    G_sb = [sb("G_sb0", [128, 128], F32), sb("G_sb1", [128, 128], F32)]
    Csc_all = [sb("Csc_all0", [16, 1024], BF16), sb("Csc_all1", [16, 1024], BF16)]
    h_f = [sb("h_f0", [16, 1024], F32), sb("h_f1", [16, 1024], F32)]
    h_bf = [sb("h_bf0", [16, 1024], BF16), sb("h_bf1", [16, 1024], BF16)]
    h_tmp = sb("h_tmp", [16, 1024], F32)
    y_row = [sb("y_row0", [128, DI], F32), sb("y_row1", [128, DI], F32)]
    ycols = sb("ycols", [128, 8], F32)
    yln_bf = sb("yln_bf", [128, DI], BF16)
    scr2 = sb("scr2", [128, DI], BF16)
    yln_c = sb("yln_c", [128, 16 * 128], BF16)
    sq_y = sb("sq_y", [128, DI], F32)

    pxin = ps("pxin", [128, 512], F32)
    pdt = ps("pdt", [128, 512], F32)
    pgs = ps("pgs", [128, 256], F32)
    pym = ps("pym", [128, 384], F32)
    px = ps("px", [128, 272], BF16)
    ph = ps("ph", [16, 1024], F32)
    p8 = ps("p8", [128, 512], F32)

    with es:
        for dst, srcn, kch, width in ((wx, "wxT", 8, DI), (wB, "wBT", 8, 16),
                                      (wC, "wCT", 8, 16), (wdt, "wdtT", 8, 8)):
            s = hbm[srcn][:, :].rearrange("(k p) o -> p k o", p=128)
            sch.op("gpsimd", lambda e, d=t3(dst, kch), s=s: e.dma_start(d, s),
                   writes=(dst.name,), dma=True)
        for dst, srcn in ((convx, "convx"), (convxb, "convxb"), (convB, "convB"),
                          (convBb, "convBb"), (convC, "convC"), (convCb, "convCb"),
                          (dtb, "dtbias"), (gbrow, "gbrow")):
            sch.op("sync", lambda e, d=dst, s=hbm[srcn]: e.dma_start(d[:, :], s[:, :]),
                   writes=(dst.name,), dma=True)
        sch.op("sync", lambda e: e.dma_start(rows8[0:1, 0:8], hbm["arow"][:, :]),
               writes=("rows8",), dma=True)
        sch.op("sync", lambda e: e.dma_start(rows8[0:1, 8:16], hbm["drow"][:, :]),
               writes=("rows8",), dma=True)

        sch.op("scalar", lambda e: e.activation(rows8[0:1, 16:24], rows8[0:1, 0:8],
                                                AF.Exp, scale=1.0),
               reads=("rows8",), writes=("rows8",))
        sch.op("vector", lambda e: e.tensor_scalar(rows8[0:1, 16:24], rows8[0:1, 16:24],
                                                   -1.0, None, OP.mult),
               reads=("rows8",), writes=("rows8",))
        sch.op("tensor", lambda e: e.matmul(p8[:, 0:16], cx.ones_f[0:1, 0:128],
                                            rows8[0:1, 8:24], start=True, stop=True),
               reads=("rows8", "ones_f"), writes=("p8",))
        sch.op("scalar", lambda e: e.copy(A_bc[:, :], p8[:, 8:16]),
               reads=("p8",), writes=("A_bc",))
        for h in range(8):
            sch.op("vector",
                   lambda e, h=h: e.tensor_scalar(idD[:, bass.ts(h, 128)],
                                                  cx.id128_f[:, :],
                                                  p8[:, h:h + 1], None, OP.mult),
                   reads=("p8", "id128_f"), writes=("idD",))
        for j in range(8):
            dst = gam_bc if j < 4 else bet_bc
            jj = j % 4
            sch.op("tensor",
                   lambda e, j=j: e.matmul(p8[:, :], cx.ones_f[0:1, 0:128],
                                           gbrow[0:1, bass.ts(j, 512)],
                                           start=True, stop=True),
                   reads=("gbrow_s", "ones_f"), writes=("p8",))
            sch.op("scalar",
                   lambda e, d=dst, jj=jj: e.copy(d[:, bass.ts(jj, 512)], p8[:, :]),
                   reads=("p8",), writes=(dst.name,))
        sch.op("gpsimd", lambda e: (e.memset(xconv[:, :], 0.0),
                                    e.memset(Bconv[:, :], 0.0),
                                    e.memset(Cconv[:, :], 0.0),
                                    e.memset(h_f[0][:, :], 0.0),
                                    e.memset(h_f[1][:, :], 0.0),
                                    e.memset(h_bf[0][:, :], 0.0),
                                    e.memset(h_bf[1][:, :], 0.0))[-1],
               writes=("xconv", "Bconv", "Cconv", "h_f0", "h_f1", "h_bf0", "h_bf1"))

        pending_yln = [None]

        def mk_yln(cp, yr, yrn, t0, cs):
            def emit():
                sch.op("vector", lambda e: e.reduce_sum(ycols[:, 0:1], yr[:, :],
                                                        axis=mybir.AxisListType.X),
                       reads=(yrn,), writes=("ycols",))
                sch.op("scalar", lambda e: e.activation(sq_y[:, :], yr[:, :],
                                                        AF.Square,
                                                        accum_out=ycols[:, 1:2]),
                       reads=(yrn,), writes=("sq_y", "ycols"))
                sch.op("vector", lambda e: e.tensor_scalar(ycols[:, 2:4], ycols[:, 0:2],
                                                           1.0 / DI, None, OP.mult),
                       reads=("ycols",), writes=("ycols",))
                sch.op("vector", lambda e: e.tensor_tensor(ycols[:, 4:5], ycols[:, 2:3],
                                                           ycols[:, 2:3], OP.mult),
                       reads=("ycols",), writes=("ycols",))
                sch.op("vector", lambda e: e.tensor_tensor(ycols[:, 5:6], ycols[:, 3:4],
                                                           ycols[:, 4:5], OP.subtract),
                       reads=("ycols",), writes=("ycols",))
                sch.op("scalar", lambda e: e.activation(ycols[:, 6:7], ycols[:, 5:6],
                                                        AF.Sqrt, bias=cx.cvals[:, 0:1]),
                       reads=("ycols", "cvals"), writes=("ycols",))
                sch.op("vector", lambda e: e.reciprocal(ycols[:, 7:8], ycols[:, 6:7]),
                       reads=("ycols",), writes=("ycols",))
                sch.op("vector", lambda e: e.tensor_scalar(yln_bf[:, :], yr[:, :],
                                                           ycols[:, 2:3],
                                                           ycols[:, 7:8],
                                                           OP.subtract, OP.mult),
                       reads=(yrn, "ycols"), writes=("yln_bf",))
                sch.op("vector", lambda e: e.tensor_tensor(scr2[:, :], yln_bf[:, :],
                                                           gam_bc[:, :], OP.mult),
                       reads=("yln_bf", "gam_bc"), writes=("scr2",))
                sch.op("vector", lambda e: e.tensor_tensor(yln_bf[:, :], scr2[:, :],
                                                           bet_bc[:, :], OP.add),
                       reads=("scr2", "bet_bc"), writes=("yln_bf",))
                for wv2 in range(8):
                    def ytp(e, wv2=wv2):
                        e.transpose(px[:, 0:128], yln_bf[:, bass.ts(2 * wv2, 128)],
                                    cx.id128_bf[:, :])
                        return e.transpose(px[:, 128:256],
                                           yln_bf[:, bass.ts(2 * wv2 + 1, 128)],
                                           cx.id128_bf[:, :])
                    sch.op("tensor", ytp, reads=("yln_bf", "id128_bf"), writes=("px",))
                    sch.op("scalar", lambda e, wv2=wv2: e.copy(
                        yln_c[:, :].rearrange("p (k t) -> p k t", k=16)
                        [:, 2 * wv2:2 * wv2 + 2, :], px[:, 0:256].rearrange(
                            "p (a t) -> p a t", a=2)),
                        reads=("px",), writes=("yln_c",))
                sch.op("sync", lambda e: e.dma_start(fm(cx.yln_h, 16, t0 + cs, 128),
                                                     t3(yln_c, 16)),
                       reads=("yln_c",), writes=("yln_h",), dma=True)
            return emit

        def upre(it2):
            t0p = it2 * TT
            sch.op("sync", lambda e, t0=t0p: e.dma_start(t3(u_t, 8),
                                                         fm(hbm["u"], 8, t0, TT)),
                   writes=("u_t",), dma=True)
            sch.op("scalar", lambda e: e.activation(sq_t[:, :], u_t[:, :], AF.Square),
                   reads=("u_t",), writes=("sq_t",))

        for it in range(ntile):
            t0 = it * TT
            if it == 0:
                upre(0)
            sch.mm_group(p8[0:1, 0:TT],
                         [(cx.ones_f[:, 0:1], u_t[:, bass.ts(k, TT)]) for k in range(8)],
                         reads=("u_t", "ones_f"), writes=("p8",))
            sch.mm_group(p8[0:1, TT:2 * TT],
                         [(cx.ones_f[:, 0:1], sq_t[:, bass.ts(k, TT)]) for k in range(8)],
                         reads=("sq_t", "ones_f"), writes=("p8",))
            sch.op("vector", lambda e: e.tensor_scalar(rows[0:1, 0:2 * TT],
                                                       p8[0:1, 0:2 * TT],
                                                       1.0 / D, None, OP.mult),
                   reads=("p8",), writes=("rows",))
            sch.op("vector", lambda e: e.tensor_tensor(rows[0:1, 2 * TT:3 * TT],
                                                       rows[0:1, 0:TT], rows[0:1, 0:TT],
                                                       OP.mult),
                   reads=("rows",), writes=("rows",))
            sch.op("vector", lambda e: e.tensor_tensor(rows[0:1, 2 * TT:3 * TT],
                                                       rows[0:1, TT:2 * TT],
                                                       rows[0:1, 2 * TT:3 * TT],
                                                       OP.subtract),
                   reads=("rows",), writes=("rows",))
            sch.op("scalar", lambda e: e.activation(rows[0:1, 3 * TT:4 * TT],
                                                    rows[0:1, 2 * TT:3 * TT],
                                                    AF.Sqrt, bias=cx.cvals[0:1, 0:1]),
                   reads=("rows", "cvals"), writes=("rows",))
            sch.op("vector", lambda e: e.reciprocal(rows[0:1, 3 * TT:4 * TT],
                                                    rows[0:1, 3 * TT:4 * TT]),
                   reads=("rows",), writes=("rows",))
            sch.op("vector", lambda e: e.tensor_tensor(rows[0:1, 4 * TT:5 * TT],
                                                       rows[0:1, 0:TT],
                                                       rows[0:1, 3 * TT:4 * TT],
                                                       OP.mult),
                   reads=("rows",), writes=("rows",))
            sch.op("tensor", lambda e: e.matmul(p8[:, 0:2 * TT], cx.ones_f[0:1, 0:128],
                                                rows[0:1, 3 * TT:5 * TT],
                                                start=True, stop=True),
                   reads=("rows", "ones_f"), writes=("p8",))
            sch.op("vector", lambda e: e.tensor_tensor(
                sq_t[:, :].rearrange("p (k t) -> p k t", k=8),
                u_t[:, :].rearrange("p (k t) -> p k t", k=8),
                p8[:, 0:TT].unsqueeze(1).broadcast_to([128, 8, TT]), OP.mult),
                reads=("u_t", "p8"), writes=("sq_t",))
            sch.op("vector", lambda e: e.tensor_tensor(
                lnu_t[:, :].rearrange("p (k t) -> p k t", k=8),
                sq_t[:, :].rearrange("p (k t) -> p k t", k=8),
                p8[:, TT:2 * TT].unsqueeze(1).broadcast_to([128, 8, TT]), OP.subtract),
                reads=("sq_t", "p8"), writes=("lnu_t",))
            sch.op("sync", lambda e, t0=t0: e.dma_start(fm(cx.lnu_h, 8, t0, TT),
                                                        t3(lnu_t, 8)),
                   reads=("lnu_t",), writes=("lnu_h",), dma=True)
            for m in range(16):
                pb, sn, bn = ((pxin, "pxin0", "pxin") if m % 2 == 0
                              else (p8, "p8x", "p8"))
                sch.mm_group(pb[:, 0:TT],
                             [(wx[:, k * DI + m * 128:k * DI + (m + 1) * 128],
                               lnu_t[:, bass.ts(k, TT)]) for k in range(8)],
                             reads=("wx", "lnu_t"), writes=(sn, bn))
                sch.op("scalar",
                       lambda e, m=m, pb=pb: e.copy(
                           xconv[:, m * (TT + 3) + 3:(m + 1) * (TT + 3)],
                           pb[:, 0:TT]),
                       reads=(sn,), writes=("xconv",))
            sch.mm_group(pdt[0:16, 0:TT],
                         [(wB[:, bass.ts(k, 16)], lnu_t[:, bass.ts(k, TT)])
                          for k in range(8)],
                         reads=("wB", "lnu_t"), writes=("pdt",))
            sch.op("scalar", lambda e: e.copy(Bconv[:, 3:3 + TT], pdt[0:16, 0:TT]),
                   reads=("pdt",), writes=("Bconv",))
            sch.mm_group(pdt[0:16, 256:256 + TT],
                         [(wC[:, bass.ts(k, 16)], lnu_t[:, bass.ts(k, TT)])
                          for k in range(8)],
                         reads=("wC", "lnu_t"), writes=("pdt",))
            sch.op("scalar", lambda e: e.copy(Cconv[:, 3:3 + TT], pdt[0:16, 256:256 + TT]),
                   reads=("pdt",), writes=("Cconv",))
            sch.mm_group(pdt[0:8, 0:TT],
                         [(wdt[:, bass.ts(k, 8)], lnu_t[:, bass.ts(k, TT)])
                          for k in range(8)],
                         reads=("wdt", "lnu_t"), writes=("pdt",))
            sch.op("vector", lambda e: e.tensor_scalar(dtf[:, 0:TT], pdt[0:8, 0:TT],
                                                       dtb[:, 0:1], None, OP.add),
                   reads=("pdt", "dtb"), writes=("dtf",))
            sch.op("vector", lambda e: e.tensor_scalar(dtf[:, TT:2 * TT],
                                                       dtf[:, 0:TT], -1.0, None,
                                                       OP.mult),
                   reads=("dtf",), writes=("dtf",))
            sch.op("vector", lambda e: e.tensor_tensor(dtf[:, 0:TT], dtf[:, 0:TT],
                                                       dtf[:, TT:2 * TT], OP.min),
                   reads=("dtf",), writes=("dtf",))
            sch.op("scalar", lambda e: e.activation(dtf[:, TT:2 * TT], dtf[:, 0:TT],
                                                    AF.Exp),
                   reads=("dtf",), writes=("dtf",))
            sch.op("scalar", lambda e: e.activation(dtf[:, 2 * TT:3 * TT],
                                                    dtf[:, TT:2 * TT], AF.Ln, bias=1.0),
                   reads=("dtf",), writes=("dtf",))
            sch.op("vector", lambda e: e.tensor_scalar(dtf[:, 3 * TT:4 * TT],
                                                       pdt[0:8, 0:TT], dtb[:, 0:1], 0.0,
                                                       OP.add, OP.max),
                   reads=("pdt", "dtb"), writes=("dtf",))
            sch.op("vector", lambda e: e.tensor_tensor(dtf[:, 4 * TT:5 * TT],
                                                       dtf[:, 3 * TT:4 * TT],
                                                       dtf[:, 2 * TT:3 * TT], OP.add),
                   reads=("dtf",), writes=("dtf",))
            for m in range(16):
                base = m * (TT + 3)
                sch.op("vector", lambda e, base=base, m=m: e.tensor_scalar(
                    cscr[:, 0:TT], xconv[:, base:base + TT], convx[:, 4 * m:4 * m + 1],
                    None, OP.mult),
                    reads=("xconv", "convx_s"), writes=("cscr",))
                for j2, (o, i) in enumerate(((TT, 0), (0, TT), (TT, 0))):
                    sch.op("vector", lambda e, base=base, m=m, j2=j2, o=o, i=i:
                           e.scalar_tensor_tensor(
                               cscr[:, o:o + TT], xconv[:, base + j2 + 1:base + j2 + 1 + TT],
                               convx[:, 4 * m + j2 + 1:4 * m + j2 + 2], cscr[:, i:i + TT],
                               OP.mult, OP.add),
                           reads=("xconv", "convx_s", "cscr"), writes=("cscr",))
                sch.op("scalar", lambda e, m=m: e.activation(
                    xs_t[:, bass.ts(m, TT)], cscr[:, TT:2 * TT], AF.Silu,
                    bias=convxb[:, m:m + 1]),
                    reads=("cscr", "convxb_s"), writes=("xs_t",))
            for (cin, cwt, cbt, sname) in ((Bconv, convB, convBb, "B"),
                                           (Cconv, convC, convCb, "C")):
                sch.op("vector", lambda e, cin=cin, cwt=cwt: e.tensor_scalar(
                    cscr[0:16, 0:TT], cin[:, 0:TT], cwt[:, 0:1], None, OP.mult),
                    reads=(cin.name, cwt.name), writes=("cscr",))
                for j2, (o, i) in enumerate(((TT, 0), (0, TT), (TT, 0))):
                    sch.op("vector", lambda e, cin=cin, cwt=cwt, j2=j2, o=o, i=i:
                           e.scalar_tensor_tensor(
                               cscr[0:16, o:o + TT], cin[:, j2 + 1:j2 + 1 + TT],
                               cwt[:, j2 + 1:j2 + 2], cscr[0:16, i:i + TT],
                               OP.mult, OP.add),
                           reads=(cin.name, cwt.name, "cscr"), writes=("cscr",))
                if sname == "B":
                    sch.op("scalar", lambda e, cbt=cbt: e.activation(
                        Bs_bf[:, :], cscr[0:16, TT:2 * TT], AF.Silu, bias=cbt[:, 0:1]),
                        reads=("cscr", cbt.name), writes=("Bs_bf",))
                else:
                    sch.op("scalar", lambda e, cbt=cbt: e.activation(
                        Cs_f[:, :], cscr[0:16, TT:2 * TT], AF.Silu, bias=cbt[:, 0:1]),
                        reads=("cscr", cbt.name), writes=("Cs_f",))
                    sch.op("scalar", lambda e: e.copy(Cs_bf[:, :], Cs_f[:, :]),
                           reads=("Cs_f",), writes=("Cs_bf",))
            sch.op("gpsimd", lambda e: (
                e.tensor_copy(xconv[:, :].rearrange("p (m t) -> p m t", m=16)[:, :, 0:3],
                              xconv[:, :].rearrange("p (m t) -> p m t", m=16)[:, :, TT:TT + 3]),
                e.tensor_copy(Bconv[:, 0:3], Bconv[:, TT:TT + 3]),
                e.tensor_copy(Cconv[:, 0:3], Cconv[:, TT:TT + 3]))[-1],
                reads=("xconv", "Bconv", "Cconv"),
                writes=("xconv", "Bconv", "Cconv"))

            def dtpart(c):
                cs = c * 128
                cp = c % 2
                dtc, dtcn = dt_tm[cp], "dt_tm%d" % cp
                dAc, dAcn = dtA_tm[cp], "dtA_tm%d" % cp
                nsc, nscn = ns_tm[cp], "ns_tm%d" % cp
                prc, prcn = prod[cp], "prod%d" % cp
                sch.op("tensor", lambda e, cs=cs: e.transpose(
                    pdt[:, 256:264], dtf[:, 4 * TT + cs:4 * TT + cs + 128], cx.id8_f[:, :]),
                    reads=("dtf", "id8_f"), writes=("pdt",))
                sch.op("scalar", lambda e, dtc=dtc: e.copy(dtc[:, :], pdt[:, 256:264]),
                       reads=("pdt",), writes=(dtcn,))
                sch.op("vector", lambda e, dAc=dAc, dtc=dtc: e.tensor_tensor(
                    dAc[:, :], dtc[:, :], A_bc[:, :], OP.mult),
                    reads=(dtcn, "A_bc"), writes=(dAcn,))
                sch.op("tensor", lambda e, dAc=dAc: e.matmul(pdt[:, 264:272], cx.U_f[:, :],
                                                             dAc[:, :], start=True,
                                                             stop=True),
                       reads=("U_f", dAcn), writes=("pdt",))
                sch.op("vector", lambda e, nsc=nsc: e.tensor_scalar(
                    nsc[:, :], pdt[:, 264:272], -1.0, None, OP.mult),
                    reads=("pdt",), writes=(nscn,))
                sch.op("vector", lambda e, prc=prc, dAc=dAc: e.tensor_tensor(
                    prc[:, :].rearrange("p (h t) -> p h t", h=8),
                    dAc[:, :].unsqueeze(2).broadcast_to([128, 8, 128]),
                    cx.U_f[:, :].unsqueeze(1).broadcast_to([128, 8, 128]), OP.mult),
                    reads=(dAcn, "U_f"), writes=(prcn,))

            def phpart(c):
                cs = c * 128
                cp = c % 2
                eac, eacn = e_all[cp], "e_all%d" % cp
                prc, prcn = prod[cp], "prod%d" % cp
                src_, srcn = srow[cp], "srow%d" % cp
                Gc, Gcn = G_sb[cp], "G_sb%d" % cp
                Cac, Cacn = Csc_all[cp], "Csc_all%d" % cp
                sch.mm_group(ph[0:1, 0:512],
                             [(cx.ones_f[:, 0:1], prc[:, 0:512])],
                             reads=(prcn, "ones_f"), writes=("ph",))
                sch.mm_group(ph[0:1, 512:1024],
                             [(cx.ones_f[:, 0:1], prc[:, 512:1024])],
                             reads=(prcn, "ones_f"), writes=("ph",))
                sch.op("scalar", lambda e, src_=src_: e.copy(src_[0:1, 0:1024],
                                                             ph[0:1, 0:1024]),
                       reads=("ph",), writes=(srcn,))
                sch.op("scalar", lambda e, src_=src_: e.activation(
                    src_[0:1, 1024:2048], ph[0:1, 0:1024], AF.Exp),
                    reads=("ph",), writes=(srcn,))
                srow_last = src_[0:1, 0:1024].rearrange(
                    "p (h t) -> p h t", h=8)[:, :, 127:128]
                sch.op("tensor", lambda e, r=srow_last: e.matmul(
                    pdt[0:16, 272:280], cx.ones_f[0:1, 0:16], r,
                    start=True, stop=True),
                    reads=(srcn, "ones_f"), writes=("pdt",))
                sch.op("scalar", lambda e, eac=eac: e.activation(
                    eac[:, :], pdt[0:16, 272:280], AF.Exp),
                    reads=("pdt",), writes=(eacn,))
                sch.mm_group(pgs[:, 0:128],
                             [(Bs_bf[:, cs:cs + 128], Cs_bf[:, cs:cs + 128])],
                             reads=("Bs_bf", "Cs_bf"), writes=("pgs",))
                sch.op("scalar", lambda e, Gc=Gc: e.copy(Gc[:, :], pgs[:, 0:128]),
                       reads=("pgs",), writes=(Gcn,))
                def pebmm(e, src_=src_):
                    e.matmul(ph[0:16, 0:512], cx.ones_f[0:1, 0:16],
                             src_[0:1, 1024:1536], start=True, stop=True)
                    return e.matmul(ph[0:16, 512:1024], cx.ones_f[0:1, 0:16],
                                    src_[0:1, 1536:2048], start=True, stop=True)
                sch.op("tensor", pebmm, reads=(srcn, "ones_f"), writes=("ph",))
                sch.op("vector", lambda e, cs=cs, Cac=Cac: e.tensor_tensor(
                    Cac[:, :].rearrange("p (h t) -> p h t", h=8),
                    Cs_f[:, cs:cs + 128].unsqueeze(1).broadcast_to([16, 8, 128]),
                    ph[0:16, 0:1024].rearrange("p (h t) -> p h t", h=8), OP.mult),
                    reads=("Cs_f", "ph"), writes=(Cacn,))
                sch.op("tensor", lambda e, cs=cs: e.transpose(
                    px[:, 256:272], Bs_bf[:, cs:cs + 128], cx.id16_bf[:, :]),
                    reads=("Bs_bf", "id16_bf"), writes=("px",))
                sch.op("scalar", lambda e: e.copy(B_tm[:, :], px[:, 256:272]),
                       reads=("px",), writes=("B_tm",))

            def headswork(c):
                cs = c * 128
                cp = c % 2
                dtc, dtcn = dt_tm[cp], "dt_tm%d" % cp
                nsc, nscn = ns_tm[cp], "ns_tm%d" % cp
                eac, eacn = e_all[cp], "e_all%d" % cp
                src_, srcn = srow[cp], "srow%d" % cp
                yr, yrn = y_row[cp], "y_row%d" % cp
                Gc, Gcn = G_sb[cp], "G_sb%d" % cp
                Cac, Cacn = Csc_all[cp], "Csc_all%d" % cp
                def spair(pp):
                    gsb, gsn = (pgs, "pgs") if pp % 2 == 0 else (p8, "p8")

                    def smm(e, pp=pp, gsb=gsb, src_=src_):
                        e.matmul(gsb[:, 0:256], cx.ones_f[0:1, 0:128],
                                 src_[0:1, pp * 256:(pp + 1) * 256], start=True,
                                 stop=False)
                        return e.matmul(gsb[:, 0:256], cx.id128_f[:, :],
                                        cx.maskS2_f[:, :], start=False, stop=True)
                    sch.op("tensor", smm, reads=(srcn, "ones_f", "id128_f", "maskS2_f"),
                           writes=(gsn,) if gsn == "pgs" else (gsn, "p8x"))

                def lamop(h):
                    hp = h % 2
                    gsb, gsn = (pgs, "pgs") if (h // 2) % 2 == 0 else (p8, "p8")
                    lm, lmn = lam[hp], "lam%d" % hp
                    sch.op("scalar", lambda e, h=h, gsb=gsb, lm=lm, nsc=nsc: e.activation(
                        lm[:, :], gsb[:, 128 * (h % 2):128 * (h % 2) + 128], AF.Exp,
                        bias=nsc[:, h:h + 1]),
                        reads=(gsn, nscn), writes=(lmn,))

                def front(h):
                    hp = h % 2
                    Xc, Xcn = X_tm[hp], "X_tm%d" % hp

                    def xtp(e, h=h, cs=cs):
                        e.transpose(px[:, 0:128],
                                    xs_t[:, (2 * h) * TT + cs:(2 * h) * TT + cs + 128],
                                    cx.id128_bf[:, :])
                        return e.transpose(px[:, 128:256],
                                           xs_t[:, (2 * h + 1) * TT + cs:(2 * h + 1) * TT + cs + 128],
                                           cx.id128_bf[:, :])
                    sch.op("tensor", xtp, reads=("xs_t", "id128_bf"), writes=("px",))
                    sch.op("scalar", lambda e, Xc=Xc: e.copy(Xc[:, :], px[:, 0:256]),
                           reads=("px",), writes=(Xcn,))

                def back(h):
                    hp = h % 2
                    wv = h // 4
                    hbuf = h_bf[wv]
                    Xc, Xcn = X_tm[hp], "X_tm%d" % hp
                    ymb, ymn = (pym, "pym") if hp == 0 else (pxin, "pxin")
                    lm, lmn = lam[hp], "lam%d" % hp
                    Mc, Mcn = Mm[hp], "Mm%d" % hp
                    wc, wcn = wcol[hp], "wcol%d" % hp
                    Bww, Bwn = Bw[hp], "Bw%d" % hp
                    sch.op("vector", lambda e, h=h, Mc=Mc, lm=lm, dtc=dtc, Gc=Gc:
                           e.scalar_tensor_tensor(
                               Mc[:, :], Gc[:, :], dtc[:, h:h + 1], lm[:, :],
                               OP.mult, OP.mult),
                           reads=(Gcn, dtcn, lmn), writes=(Mcn,))

                    def ymm(e, h=h, hbuf=hbuf, ymb=ymb, Mc=Mc, Xc=Xc, Cac=Cac):
                        e.matmul(ymb[:, 0:256], Mc[:, :], Xc[:, :], start=True, stop=False)
                        e.matmul(ymb[:, 0:256], idD[:, bass.ts(h, 128)], Xc[:, :],
                                 start=False, stop=False)
                        return e.matmul(ymb[:, 0:256], Cac[:, bass.ts(h, 128)],
                                        hbuf[:, bass.ts(h % 4, 256)],
                                        start=False, stop=True)
                    sch.op("tensor", ymm,
                           reads=(Mcn, Xcn, "idD", Cacn, hbuf.name),
                           writes=(ymn,) if hp == 0 else (ymn, "pxin0"))
                    sch.op("scalar", lambda e, h=h, ymb=ymb, yr=yr: e.copy(
                        yr[:, bass.ts(h, 256)], ymb[:, 0:256]),
                        reads=(ymn,), writes=(yrn,))
                    sch.op("vector", lambda e, h=h, wc=wc, lm=lm, dtc=dtc: e.tensor_scalar(
                        wc[:, :], lm[:, 127:128], dtc[:, h:h + 1], None, OP.mult),
                        reads=(lmn, dtcn), writes=(wcn,))
                    sch.op("vector", lambda e, Bww=Bww, wc=wc: e.tensor_scalar(
                        Bww[:, :], B_tm[:, :], wc[:, 0:1], None, OP.mult),
                        reads=("B_tm", wcn), writes=(Bwn,))
                    sch.mm_group(ph[0:16, bass.ts(h % 4, 256)],
                                 [(Bww[:, :], Xc[:, :])],
                                 reads=(Bwn, Xcn), writes=("ph",))
                    if h % 4 == 3:
                        hf = h_f[wv]
                        hb = h_bf[wv]
                        sch.op("vector", lambda e, wv=wv, hf=hf, eac=eac: e.tensor_tensor(
                            h_tmp[:, :].rearrange("p (g t) -> p g t", g=4),
                            hf[:, :].rearrange("p (g t) -> p g t", g=4),
                            eac[:, 4 * wv:4 * wv + 4].unsqueeze(2).broadcast_to([16, 4, 256]),
                            OP.mult),
                            reads=(hf.name, eacn), writes=("h_tmp",))
                        sch.op("vector", lambda e, hf=hf: e.tensor_tensor(
                            hf[:, :], h_tmp[:, :], ph[0:16, 0:1024], OP.add),
                            reads=("h_tmp", "ph"), writes=(hf.name,))
                        sch.op("scalar", lambda e, hf=hf, hb=hb: e.copy(hb[:, :], hf[:, :]),
                               reads=(hf.name,), writes=(hb.name,))

                spair(0)
                lamop(0)
                lamop(1)
                front(0)
                front(1)
                for h in range(8):
                    back(h)
                    if h % 2 == 1 and h < 7:
                        spair((h + 1) // 2)
                        lamop(h + 1)
                        lamop(h + 2)
                        front(h + 1)
                        front(h + 2)


            for c in range(nch):
                if c == 0:
                    dtpart(0)
                phpart(c)
                if c + 1 < nch:
                    dtpart(c + 1)
                cp = c % 2
                yr, yrn = y_row[cp], "y_row%d" % cp
                headswork(c)
                if c == 0 and it + 1 < ntile:
                    upre(it + 1)
                if pending_yln[0] is not None:
                    pending_yln[0]()
                pending_yln[0] = mk_yln(cp, yr, yrn, t0, c * 128)
            if cx.dev and it == ntile - 1:
                for nm, buf in (("d_xs", xs_t), ("d_dtf", dtf), ("d_yrow", y_row[1]),
                                ("d_lam", lam[1]), ("d_Mm", Mm[1]), ("d_Xtm", X_tm[1]),
                                ("d_srow", srow[1]), ("d_dttm", dt_tm[1]),
                                ("d_Bs", Bs_bf), ("d_Cs", Cs_f),
                                ("d_eall", e_all[1])):
                    sch.op("sync", lambda e, nm=nm, buf=buf: e.dma_start(
                        cx.dbg[nm][:, :], buf[:, :]), reads=(buf.name,), dma=True)
                sch.op("sync", lambda e: e.dma_start(cx.dbg["d_hf"][:, 0:1024],
                                                     h_f[0][:, :]),
                       reads=("h_f0",), dma=True)
                sch.op("sync", lambda e: e.dma_start(cx.dbg["d_hf"][:, 1024:2048],
                                                     h_f[1][:, :]),
                       reads=("h_f1",), dma=True)
        if pending_yln[0] is not None:
            pending_yln[0]()
            pending_yln[0] = None


def _phase_MG(cx, nc, sch):
    TT = TS_MA
    ntile = T // TT
    hbm = cx.hbm
    es = ExitStack()
    sb = lambda n, s, d: es.enter_context(nc.sbuf_tensor(n, s, d))
    ps = lambda n, s, d: es.enter_context(nc.psum_tensor(n, s, d))
    wg = sb("wg", [128, 16 * DI], BF16)
    bg = sb("bg_s", [128, 16], F32)
    yln2 = [sb("yln_t0", [128, 16 * TT], BF16), sb("yln_t1", [128, 16 * TT], BF16)]
    gbf2 = [sb("g_bf0", [128, 16 * TT], BF16), sb("g_bf1", [128, 16 * TT], BF16)]
    pga = [ps("pga0", [128, 256], F32), ps("pga1", [128, 256], F32)]
    with es:
        s = hbm["wgT"][:, :].rearrange("(k p) o -> p k o", p=128)
        sch.op("gpsimd", lambda e, d=t3(wg, 16): e.dma_start(d, s),
               writes=("wg",), dma=True)
        sch.op("sync", lambda e: e.dma_start(bg[:, :], hbm["bg"][:, :]),
               writes=("bg_s",), dma=True)
        def yload(it2):
            t0 = it2 * TT
            yb = yln2[it2 % 2]
            sch.op("sync", lambda e, t0=t0, yb=yb: e.dma_start(
                t3(yb, 16), fm(cx.yln_h, 16, t0, TT)),
                reads=("yln_h",), writes=("yln_t%d" % (it2 % 2),), dma=True)

        yload(0)
        for it in range(ntile):
            t0 = it * TT
            yln_t, yn = yln2[it % 2], "yln_t%d" % (it % 2)
            g_bf, gn = gbf2[it % 2], "g_bf%d" % (it % 2)
            if it + 1 < ntile:
                yload(it + 1)
            for m in range(16):
                pg = pga[m % 2]
                pgn = "pga%d" % (m % 2)
                sch.mm_group(pg[:, 0:TT],
                             [(wg[:, k * DI + m * 128:k * DI + (m + 1) * 128],
                               yln_t[:, bass.ts(k, TT)]) for k in range(16)],
                             reads=("wg", yn), writes=(pgn,))
                sch.op("scalar", lambda e, m=m, pg=pg, g_bf=g_bf: e.activation(
                    g_bf[:, bass.ts(m, TT)], pg[:, 0:TT], AF.Sigmoid,
                    bias=bg[:, m:m + 1]),
                    reads=(pgn, "bg_s"), writes=(gn,))
            for j in range(8):
                sch.op("vector", lambda e, j=j, yln_t=yln_t, g_bf=g_bf: e.tensor_tensor(
                    g_bf[:, bass.ts(j, 512)], g_bf[:, bass.ts(j, 512)],
                    yln_t[:, bass.ts(j, 512)], OP.mult),
                    reads=(gn, yn), writes=(gn,))
            sch.op("sync", lambda e, t0=t0, g_bf=g_bf: e.dma_start(
                fm(cx.yg_h, 16, t0, TT), t3(g_bf, 16)),
                reads=(gn,), writes=("yg_h",), dma=True)

def _phase_MB(cx, nc, sch):
    TT = TS_MB
    ntile = T // TT
    hbm = cx.hbm
    es = ExitStack()
    sb = lambda n, s, d: es.enter_context(nc.sbuf_tensor(n, s, d))
    ps = lambda n, s, d: es.enter_context(nc.psum_tensor(n, s, d))

    wz0 = sb("wz0", [128, 8 * DI], BF16)
    wx0 = sb("wx0", [128, 8 * DI], BF16)
    wo = sb("wo", [128, 32 * D], BF16)
    lnu2 = [sb("lnu_tb0", [128, 8 * TT], BF16), sb("lnu_tb1", [128, 8 * TT], BF16)]
    yg2 = [sb("yg_tb0", [128, 16 * TT], BF16), sb("yg_tb1", [128, 16 * TT], BF16)]
    ub2 = [sb("u_tb0", [128, 8 * TT], F32), sb("u_tb1", [128, 8 * TT], F32)]
    zx = sb("zx", [128, 512], BF16)
    skip_t = sb("skip_t", [128, 16 * TT], BF16)
    x12 = [sb("x1_t0", [128, 8 * TT], F32), sb("x1_t1", [128, 8 * TT], F32)]
    pz = ps("pz", [128, 256], F32)
    px0 = ps("px0", [128, 256], F32)
    po = [ps("po0", [128, 256], F32), ps("po1", [128, 256], F32)]

    with es:
        for dst, srcn, kch, width in ((wz0, "wz0T", 8, DI), (wx0, "wx0T", 8, DI),
                                      (wo, "woT", 32, D)):
            s = hbm[srcn][:, :].rearrange("(k p) o -> p k o", p=128)
            sch.op("gpsimd", lambda e, d=t3(dst, kch), s=s: e.dma_start(d, s),
                   writes=(dst.name,), dma=True)
        def mbload(it2):
            t0 = it2 * TT
            p = it2 % 2
            sch.op("sync", lambda e, t0=t0, d=lnu2[p]: e.dma_start(
                t3(d, 8), fm(cx.lnu_h, 8, t0, TT)),
                reads=("lnu_h",), writes=("lnu_tb%d" % p,), dma=True)
            sch.op("sync", lambda e, t0=t0, d=yg2[p]: e.dma_start(
                t3(d, 16), fm(cx.yg_h, 16, t0, TT)),
                reads=("yg_h",), writes=("yg_tb%d" % p,), dma=True)
            sch.op("sync", lambda e, t0=t0, d=ub2[p]: e.dma_start(
                t3(d, 8), fm(hbm["u"], 8, t0, TT)),
                writes=("u_tb%d" % p,), dma=True)

        mbload(0)
        for it in range(ntile):
            t0 = it * TT
            p = it % 2
            lnu_t, lnn = lnu2[p], "lnu_tb%d" % p
            yg_t, ygn = yg2[p], "yg_tb%d" % p
            u_t, un = ub2[p], "u_tb%d" % p
            x1_t, xn = x12[p], "x1_t%d" % p
            if it + 1 < ntile:
                mbload(it + 1)
            for m in range(16):
                sch.mm_group(pz[:, 0:TT],
                             [(wz0[:, k * DI + m * 128:k * DI + (m + 1) * 128],
                               lnu_t[:, bass.ts(k, TT)]) for k in range(8)],
                             reads=("wz0", lnn), writes=("pz",))
                sch.op("scalar", lambda e: e.activation(zx[:, 0:256], pz[:, 0:TT],
                                                        AF.Silu),
                       reads=("pz",), writes=("zx",))
                sch.mm_group(px0[:, 0:TT],
                             [(wx0[:, k * DI + m * 128:k * DI + (m + 1) * 128],
                               lnu_t[:, bass.ts(k, TT)]) for k in range(8)],
                             reads=("wx0", lnn), writes=("px0",))
                sch.op("scalar", lambda e: e.copy(zx[:, 256:512], px0[:, 0:TT]),
                       reads=("px0",), writes=("zx",))
                sch.op("vector", lambda e, m=m: e.tensor_tensor(
                    skip_t[:, bass.ts(m, TT)], zx[:, 0:256], zx[:, 256:512], OP.mult),
                    reads=("zx",), writes=("skip_t",))
            for m2 in range(8):
                p = po[m2 % 2]
                pn = "po%d" % (m2 % 2)
                pairs = [(wo[:, k * D + m2 * 128:k * D + (m2 + 1) * 128],
                          skip_t[:, bass.ts(k, TT)]) for k in range(16)]
                pairs += [(wo[:, (16 + k) * D + m2 * 128:(16 + k) * D + (m2 + 1) * 128],
                           yg_t[:, bass.ts(k, TT)]) for k in range(16)]
                sch.mm_group(p[:, 0:TT], pairs,
                             reads=("wo", "skip_t", ygn), writes=(pn,))
                sch.op("vector", lambda e, m2=m2, p=p, x1_t=x1_t, u_t=u_t:
                       e.tensor_tensor(
                           x1_t[:, bass.ts(m2, TT)], u_t[:, bass.ts(m2, TT)],
                           p[:, 0:TT], OP.add),
                       reads=(un, pn), writes=(xn,))
            sch.op("sync", lambda e, t0=t0, x1_t=x1_t: e.dma_start(
                fm(cx.x1_h, 8, t0, TT), t3(x1_t, 8)),
                reads=(xn,), writes=("x1_h",), dma=True)


def _phase_F(cx, nc, sch):
    hbm = cx.hbm
    es = ExitStack()
    sb = lambda n, s, d: es.enter_context(nc.sbuf_tensor(n, s, d))
    ps = lambda n, s, d: es.enter_context(nc.psum_tensor(n, s, d))
    NCH = T // 128

    K_fm = sb("K_fm", [128, 8 * T], BF16)
    V_tm = sb("V_tm", [128, NCH * (NH * 65)], BF16)
    xf_t = [sb("xf_t0", [128, 8 * 128], F32), sb("xf_t1", [128, 8 * 128], F32)]
    sqf = sb("sqf", [128, 8 * 128], F32)
    y0_t = sb("y0_t", [128, 8 * 128], F32)
    rowsF = sb("rowsF", [1, 5 * 128], F32)
    ptr = ps("ptr", [128, 512], BF16)
    fa = ExitStack()
    fsb = lambda n, s, d: fa.enter_context(nc.sbuf_tensor(n, s, d))
    fps = lambda n, s, d: fa.enter_context(nc.psum_tensor(n, s, d))
    kvd = fsb("kvd", [128, 8 * LAT], BF16)
    qd = fsb("qd", [128, 8 * LAT], BF16)
    ku = fsb("ku", [128, 2 * 512], BF16)
    qu = fsb("qu", [128, 2 * 512], BF16)
    vu = fsb("vu", [128, 2 * D], BF16)
    rq = fsb("rq", [128, 2 * 512], BF16)
    rqp = fsb("rqp", [128, 2 * 512], BF16)
    rk = fsb("rk", [128, 8 * 512], BF16)
    rkp = fsb("rkp", [128, 8 * 512], BF16)
    ln1 = fsb("ln1", [128, 8], F32)
    posr = fsb("posr", [1, 128], F32)
    invf = fsb("invf", [1, 16], F32)
    a_t = [fsb("a_t0", [128, 8 * 128], BF16), fsb("a_t1", [128, 8 * 128], BF16)]
    kvl2 = [fsb("kvl0", [128, 2 * 128], BF16), fsb("kvl1", [128, 2 * 128], BF16)]
    ql2 = [fsb("ql0", [128, 2 * 128], BF16), fsb("ql1", [128, 2 * 128], BF16)]
    fscr = fsb("fscr", [128, 4 * 16], F32)
    fqi = fsb("fqi", [128, 16], I32)
    cosr = fsb("cosr", [128, 512], F32)
    sinr = fsb("sinr", [128, 512], F32)
    ro1 = fsb("ro1", [128, 512], F32)
    ro2 = fsb("ro2", [128, 512], F32)
    kasm = fsb("kasm", [128, 8 * 128], BF16)
    qasm = fsb("qasm", [128, 8 * 128], BF16)
    p8f = [fps("p8f0", [128, 512], F32), fps("p8f1", [128, 512], F32)]
    pno = fps("pno", [128, 512], F32)
    prot = fps("prot", [128, 512], F32)
    protp = fps("protp", [128, 512], F32)
    pvv = fps("pvv", [128, 1024], F32)

    with es:
        blkFA = nc.Block()
        blkFA.__enter__()
        for dst, srcn, kch, width in ((kvd, "kvdT", 8, LAT), (qd, "qdT", 8, LAT),
                                      (ku, "kuT", 2, 512), (qu, "quT", 2, 512),
                                      (vu, "vuT", 2, D), (rq, "rqT", 2, 512),
                                      (rqp, "rqTp", 2, 512), (rk, "rkT", 8, 512),
                                      (rkp, "rkTp", 8, 512)):
            s = hbm[srcn][:, :].rearrange("(k p) o -> p k o", p=128)
            sch.op("gpsimd", lambda e, d=t3(dst, kch), s=s: e.dma_start(d, s),
                   writes=(dst.name,), dma=True)
        sch.op("sync", lambda e: e.dma_start(ln1[:, :], hbm["ln1c"][:, :]),
               writes=("ln1",), dma=True)
        def prep(g):
            g.iota(invf[:, :], [[1, 16]], base=0, channel_multiplier=0,
                   allow_small_or_imprecise_dtypes=True)
            return g.memset(
                V_tm[:, :].rearrange("p (c h e) -> p c h e", c=NCH, h=NH)[:, :, :, 64:65],
                1.0)
        sch.op("gpsimd", prep, writes=("invf", "V_tm"))
        sch.op("scalar", lambda e: e.activation(invf[:, :], invf[:, :], AF.Exp,
                                                scale=-math.log(10000.0) / 16.0),
               reads=("invf",), writes=("invf",))

        def ln_pass(xin, xname, wcolumn, wname, outbuf, outname, pbank=None,
                    pbn=None):
            if pbank is None:
                pbank, pbn = p8f[0], "p8f0"
            sch.op("scalar", lambda e: e.activation(sqf[:, :], xin[:, :], AF.Square),
                   reads=(xname,), writes=("sqf",))
            sch.mm_group(pbank[0:1, 0:128],
                         [(cx.ones_f[:, 0:1], xin[:, bass.ts(k, 128)]) for k in range(8)],
                         reads=(xname, "ones_f"), writes=(pbn,))
            sch.mm_group(pbank[0:1, 128:256],
                         [(cx.ones_f[:, 0:1], sqf[:, bass.ts(k, 128)]) for k in range(8)],
                         reads=("sqf", "ones_f"), writes=(pbn,))
            sch.op("vector", lambda e, pbank=pbank: e.tensor_scalar(rowsF[0:1, 0:256], pbank[0:1, 0:256],
                                                       1.0 / D, None, OP.mult),
                   reads=(pbn,), writes=("rowsF",))
            sch.op("vector", lambda e: e.tensor_tensor(rowsF[0:1, 256:384],
                                                       rowsF[0:1, 0:128],
                                                       rowsF[0:1, 0:128], OP.mult),
                   reads=("rowsF",), writes=("rowsF",))
            sch.op("vector", lambda e: e.tensor_tensor(rowsF[0:1, 256:384],
                                                       rowsF[0:1, 128:256],
                                                       rowsF[0:1, 256:384], OP.subtract),
                   reads=("rowsF",), writes=("rowsF",))
            sch.op("scalar", lambda e: e.activation(rowsF[0:1, 384:512],
                                                    rowsF[0:1, 256:384], AF.Sqrt,
                                                    bias=cx.cvals[0:1, 0:1]),
                   reads=("rowsF", "cvals"), writes=("rowsF",))
            sch.op("vector", lambda e: e.reciprocal(rowsF[0:1, 384:512],
                                                    rowsF[0:1, 384:512]),
                   reads=("rowsF",), writes=("rowsF",))
            sch.op("vector", lambda e: e.tensor_tensor(rowsF[0:1, 512:640],
                                                       rowsF[0:1, 0:128],
                                                       rowsF[0:1, 384:512], OP.mult),
                   reads=("rowsF",), writes=("rowsF",))
            sch.op("tensor", lambda e: e.matmul(pbank[:, 256:512], cx.ones_f[0:1, 0:128],
                                                rowsF[0:1, 384:640], start=True,
                                                stop=True),
                   reads=("rowsF", "ones_f"), writes=(pbn,))
            for k in range(8):
                sch.op("vector", lambda e, k=k, pbank=pbank: e.tensor_tensor(
                    sqf[:, bass.ts(k, 128)], xin[:, bass.ts(k, 128)], pbank[:, 256:384],
                    OP.mult),
                    reads=(xname, pbn), writes=("sqf",))
                if wcolumn is None:
                    sch.op("vector", lambda e, k=k, pbank=pbank: e.tensor_tensor(
                        outbuf[:, bass.ts(k, 128)], sqf[:, bass.ts(k, 128)],
                        pbank[:, 384:512], OP.subtract),
                        reads=("sqf", pbn), writes=(outname,))
                else:
                    sch.op("vector", lambda e, k=k, pbank=pbank: e.tensor_tensor(
                        sqf[:, bass.ts(k, 128)], sqf[:, bass.ts(k, 128)],
                        pbank[:, 384:512], OP.subtract),
                        reads=("sqf", pbn), writes=("sqf",))
                    sch.op("scalar", lambda e, k=k: e.mul(
                        outbuf[:, bass.ts(k, 128)], sqf[:, bass.ts(k, 128)],
                        wcolumn[:, k:k + 1]),
                        reads=("sqf", wname), writes=(outname,))

        def ln_work(ch):
            t0 = ch * 128
            p = ch % 2
            xb, xn = xf_t[p], "xf_t%d" % p
            ab, an = a_t[p], "a_t%d" % p
            pb, pn = p8f[p], "p8f%d" % p
            sch.op("sync", lambda e, t0=t0, xb=xb: e.dma_start(
                t3(xb, 8), fm(cx.x1_h, 8, t0, 128)),
                reads=("x1_h",), writes=(xn,), dma=True)
            ln_pass(xb, xn, None, None, y0_t, "y0_t", pb, pn)
            sch.op("sync", lambda e, t0=t0: e.dma_start(fm(cx.y0_h, 8, t0, 128),
                                                        t3(y0_t, 8)),
                   reads=("y0_t",), writes=("y0_h",), dma=True)
            ln_pass(y0_t, "y0_t", ln1, "ln1", ab, an, pb, pn)

        def lat_work(ch):
            p = ch % 2
            ab, an = a_t[p], "a_t%d" % p
            pb, pn = p8f[p], "p8f%d" % p
            for (lat, latn, wt, wtn) in ((kvl2[p], "kvl%d" % p, kvd, "kvd"),
                                         (ql2[p], "ql%d" % p, qd, "qd")):
                for m in range(2):
                    sch.mm_group(pb[:, bass.ts(m, 128)],
                                 [(wt[:, k * LAT + m * 128:k * LAT + (m + 1) * 128],
                                   ab[:, bass.ts(k, 128)]) for k in range(8)],
                                 reads=(wtn, an), writes=(pn,))
                sch.op("scalar", lambda e, lat=lat, pb=pb: e.copy(lat[:, :],
                                                                  pb[:, 0:256]),
                       reads=(pn,), writes=(latn,))

        ln_work(0)
        lat_work(0)
        for ch in range(NCH):
            if ch + 1 < NCH:
                ln_work(ch + 1)
            t0 = ch * 128
            p = ch % 2
            a_tc, a_tn = a_t[p], "a_t%d" % p
            kvl, kvln = kvl2[p], "kvl%d" % p
            ql, qln = ql2[p], "ql%d" % p
            sch.op("gpsimd", lambda e, t0=t0: e.iota(
                posr[:, :], [[1, 128]], base=t0, channel_multiplier=0,
                allow_small_or_imprecise_dtypes=True),
                writes=("posr",))
            pfb, pfn = p8f[p], "p8f%d" % p
            sch.op("tensor", lambda e, pfb=pfb: e.matmul(
                pfb[:, 0:16], posr[0:1, :], invf[0:1, :],
                start=True, stop=True),
                reads=("posr", "invf"), writes=(pfn,))
            for (half, bias0) in ((0, 0.5), (1, 0.75)):
                sch.op("vector", lambda e, half=half, bias0=bias0, pfb=pfb: e.tensor_scalar(
                    fscr[:, bass.ts(half, 16)], pfb[:, 0:16], 1.0 / TWO_PI, bias0,
                    OP.mult, OP.add),
                    reads=(pfn,), writes=("fscr",))
                sch.op("vector", lambda e, half=half: e.tensor_copy(
                    fqi[:, :], fscr[:, bass.ts(half, 16)]),
                    reads=("fscr",), writes=("fqi",))
                sch.op("vector", lambda e, half=half: e.tensor_copy(
                    fscr[:, bass.ts(half, 16)], fqi[:, :]),
                    reads=("fqi",), writes=("fscr",))
                sch.op("vector", lambda e, half=half, pfb=pfb: e.scalar_tensor_tensor(
                    fscr[:, bass.ts(2 + half, 16)], fscr[:, bass.ts(half, 16)],
                    -TWO_PI, pfb[:, 0:16], OP.mult, OP.add),
                    reads=("fscr", pfn), writes=("fscr",))
            sch.op("scalar", lambda e: e.activation(fscr[:, 0:16], fscr[:, 32:48],
                                                    AF.Sin),
                   reads=("fscr",), writes=("fscr",))
            sch.op("scalar", lambda e: e.activation(fscr[:, 16:32], fscr[:, 48:64],
                                                    AF.Sin, bias=cx.cvals[:, 1:2]),
                   reads=("fscr", "cvals"), writes=("fscr",))
            cosv = fscr[:, 16:32]
            sinv = fscr[:, 0:16]
            sch.op("scalar", lambda e: e.copy(
                cosr[:, :].rearrange("p (h v d) -> p h v d", h=16, v=2),
                cosv.unsqueeze(1).unsqueeze(2).broadcast_to([128, 16, 2, 16])),
                reads=("fscr",), writes=("cosr",))
            sch.op("vector", lambda e: e.tensor_scalar(
                sinr[:, :].rearrange("p (h v d) -> p h v d", h=16, v=2)[:, :, 0:1, :],
                sinv.unsqueeze(1).unsqueeze(2).broadcast_to([128, 16, 1, 16]),
                -1.0, None, OP.mult),
                reads=("fscr",), writes=("sinr",))
            sch.op("scalar", lambda e: e.copy(
                sinr[:, :].rearrange("p (h v d) -> p h v d", h=16, v=2)[:, :, 1:2, :],
                sinv.unsqueeze(1).unsqueeze(2).broadcast_to([128, 16, 1, 16])),
                reads=("fscr",), writes=("sinr",))

            def side(lat_b, latn, nope_w, nope_wn, rot_w, rot_wn, rot_wp, rot_wpn,
                     rot_k, asm, asmn):
                sch.mm_group(pno[:, 0:512],
                             [(lat_b[:, bass.ts(lc, 128)],
                               nope_w[:, bass.ts(lc, 512)]) for lc in range(2)],
                             reads=(latn, nope_wn), writes=("pno",))
                if rot_k == 2:
                    prs = [(lat_b[:, bass.ts(lc, 128)], rot_w[:, bass.ts(lc, 512)])
                           for lc in range(2)]
                    prsp = [(lat_b[:, bass.ts(lc, 128)], rot_wp[:, bass.ts(lc, 512)])
                            for lc in range(2)]
                    rdn = (latn,)
                else:
                    prs = [(a_tc[:, bass.ts(k, 128)], rot_w[:, bass.ts(k, 512)])
                           for k in range(8)]
                    prsp = [(a_tc[:, bass.ts(k, 128)], rot_wp[:, bass.ts(k, 512)])
                            for k in range(8)]
                    rdn = (a_tn,)
                sch.mm_group(prot[:, :], prs, reads=rdn + (rot_wn,), writes=("prot",))
                sch.mm_group(protp[:, :], prsp, reads=rdn + (rot_wpn,), writes=("protp",))
                sch.op("vector", lambda e: e.tensor_tensor(ro1[:, :], prot[:, :],
                                                           cosr[:, :], OP.mult),
                       reads=("prot", "cosr"), writes=("ro1",))
                sch.op("vector", lambda e: e.tensor_tensor(ro2[:, :], protp[:, :],
                                                           sinr[:, :], OP.mult),
                       reads=("protp", "sinr"), writes=("ro2",))
                sch.op("vector", lambda e, asm=asm: e.tensor_tensor(
                    asm[:, :].rearrange("p (h d) -> p h d", h=16)[:, :, 0:32],
                    ro1[:, :].rearrange("p (h d) -> p h d", h=16),
                    ro2[:, :].rearrange("p (h d) -> p h d", h=16), OP.add),
                    reads=("ro1", "ro2"), writes=(asmn,))
                sch.op("scalar", lambda e, asm=asm: e.copy(
                    asm[:, :].rearrange("p (h d) -> p h d", h=16)[:, :, 32:64],
                    pno[:, 0:512].rearrange("p (h d) -> p h d", h=16)),
                    reads=("pno",), writes=(asmn,))

            side(kvl, kvln, ku, "ku", rk, "rk", rkp, "rkp", 8, kasm, "kasm")
            for wv in range(2):
                def ktp(e, wv=wv):
                    for j in range(4):
                        inst = e.transpose(ptr[:, bass.ts(j, 128)],
                                           kasm[:, bass.ts(4 * wv + j, 128)],
                                           cx.id128_bf[:, :])
                    return inst
                sch.op("tensor", ktp, reads=("kasm", "id128_bf"), writes=("ptr",))
                sch.op("scalar", lambda e, wv=wv, t0=t0: e.copy(
                    K_fm[:, :].rearrange("p (pr t) -> p pr t", pr=8)
                    [:, 4 * wv:4 * wv + 4, t0:t0 + 128],
                    ptr[:, :].rearrange("p (a t) -> p a t", a=4)),
                    reads=("ptr",), writes=("K_fm",))
            for half in range(2):
                sch.mm_group(pvv[:, bass.ts(half, 512)],
                             [(kvl[:, bass.ts(lc, 128)],
                               vu[:, lc * D + half * 512:lc * D + (half + 1) * 512])
                              for lc in range(2)],
                             reads=(kvln, "vu"), writes=("pvv",))
                sch.op("scalar", lambda e, half=half, ch=ch: e.copy(
                    V_tm[:, :].rearrange("p (c h e) -> p c h e", c=NCH, h=NH)
                    [:, ch, 8 * half:8 * half + 8, 0:64],
                    pvv[:, bass.ts(half, 512)].rearrange("p (h d) -> p h d", h=8)),
                    reads=("pvv",), writes=("V_tm",))
            side(ql, qln, qu, "qu", rq, "rq", rqp, "rqp", 2, qasm, "qasm")
            for wv in range(2):
                def qtp(e, wv=wv):
                    for j in range(4):
                        inst = e.transpose(ptr[:, bass.ts(j, 128)],
                                           qasm[:, bass.ts(4 * wv + j, 128)],
                                           cx.id128_bf[:, :])
                    return inst
                sch.op("tensor", qtp, reads=("qasm", "id128_bf"), writes=("ptr",))
                sch.op("scalar", lambda e, wv=wv: e.copy(
                    qasm[:, 4 * wv * 128:(4 * wv + 4) * 128], ptr[:, :]),
                    reads=("ptr",), writes=("qasm",))
            sch.op("sync", lambda e, ch=ch: e.dma_start(
                cx.q_h[:, ch * 1024:(ch + 1) * 1024], qasm[:, :]),
                reads=("qasm",), writes=("q_h",), dma=True)
            if ch + 1 < NCH:
                lat_work(ch + 1)

        sch.dma_join()
        sch.flush(blkFA)
        blkFA.__exit__(None, None, None)
        fa.close()

        blkFB = nc.Block()
        blkFB.__enter__()
        ow = sb("ow", [128, 8 * D], BF16)
        Qp2 = [sb("Qp0", [128, 2048], BF16), sb("Qp1", [128, 2048], BF16)]
        PTs = [sb("PT%d" % h, [128, 256], BF16) for h in range(NH)]
        rc = sb("rc", [128, 32], F32)
        Oasm = [sb("Oasm0", [128, 1024], BF16), sb("Oasm1", [128, 1024], BF16)]
        Ofm = sb("Ofm", [128, 16 * 128], BF16)
        y1_t = sb("y1_t", [128, 8 * 128], F32)
        y0_2 = [sb("y0_q0", [128, 8 * 128], F32), sb("y0_q1", [128, 8 * 128], F32)]
        psS = [ps("psS%d" % i, [128, 256], F32) for i in range(2)]
        pPV = ps("pPV", [128, NH * 130], F32)

        s = hbm["owT"][:, :].rearrange("(k p) o -> p k o", p=128)
        sch.op("gpsimd", lambda e, d=t3(ow, 8): e.dma_start(d, s),
               writes=("ow",), dma=True)

        def segs(c0, c1):
            out = []
            while c0 < c1:
                nxt = min(c1, (c0 // 512 + 1) * 512)
                out.append((c0, nxt))
                c0 = nxt
            return out

        def mk_T1(j, leftovers):
            def emit():
                for fn, rdn in leftovers:
                    sch.op("tensor", fn, reads=(rdn, "V_tm"), writes=("pPV",))
                for h in range(NH):
                    sch.op("vector", lambda e, h=h: e.reciprocal(
                        rc[:, 2 * h:2 * h + 1], pPV[:, 130 * h + 64:130 * h + 65]),
                        reads=("pPV",), writes=("rc",))
                    sch.op("vector", lambda e, h=h: e.reciprocal(
                        rc[:, 2 * h + 1:2 * h + 2], pPV[:, 130 * h + 129:130 * h + 130]),
                        reads=("pPV",), writes=("rc",))
                    sch.op("vector", lambda e, h=h: e.tensor_scalar(
                        Oasm[0][:, bass.ts(h, 64)], pPV[:, 130 * h:130 * h + 64],
                        rc[:, 2 * h:2 * h + 1], None, OP.mult),
                        reads=("pPV", "rc"), writes=("Oasm0",))
                    sch.op("vector", lambda e, h=h: e.tensor_scalar(
                        Oasm[1][:, bass.ts(h, 64)], pPV[:, 130 * h + 65:130 * h + 129],
                        rc[:, 2 * h + 1:2 * h + 2], None, OP.mult),
                        reads=("pPV", "rc"), writes=("Oasm1",))
                for qq in range(2):
                    y0b, y0n = y0_2[qq], "y0_q%d" % qq
                    t0 = j * 256 + qq * 128
                    sch.op("sync", lambda e, t0=t0, y0b=y0b: e.dma_start(
                        t3(y0b, 8), fm(cx.y0_h, 8, t0, 128)),
                        reads=("y0_h",), writes=(y0n,), dma=True)
                    for wv in range(2):
                        def otp(e, wv=wv, qq=qq):
                            for j3 in range(4):
                                inst = e.transpose(ptr[:, bass.ts(j3, 128)],
                                                   Oasm[qq][:, bass.ts(4 * wv + j3, 128)],
                                                   cx.id128_bf[:, :])
                            return inst
                        sch.op("tensor", otp, reads=("Oasm%d" % qq, "id128_bf"),
                               writes=("ptr",))
                        sch.op("scalar", lambda e, wv=wv, qq=qq: e.copy(
                            Ofm[:, qq * 1024 + wv * 512:qq * 1024 + (wv + 1) * 512],
                            ptr[:, :]),
                            reads=("ptr",), writes=("Ofm",))
            return emit

        def mk_T2(j):
            def emit():
                for qq in range(2):
                    y0b, y0n = y0_2[qq], "y0_q%d" % qq
                    t0 = j * 256 + qq * 128
                    for m2 in range(8):
                        sp = psS[m2 % 2]
                        sn = "psS%d" % (m2 % 2)
                        sch.mm_group(sp[:, 0:128],
                                     [(ow[:, k * D + m2 * 128:k * D + (m2 + 1) * 128],
                                       Ofm[:, qq * 1024 + k * 128:qq * 1024 + (k + 1) * 128])
                                      for k in range(8)],
                                     reads=("ow", "Ofm"), writes=(sn,))
                        sch.op("vector", lambda e, m2=m2, sp=sp, y0b=y0b:
                               e.tensor_tensor(
                                   y1_t[:, bass.ts(m2, 128)],
                                   y0b[:, bass.ts(m2, 128)],
                                   sp[:, 0:128], OP.add),
                               reads=(y0n, sn), writes=("y1_t",))
                    sch.op("sync", lambda e, t0=t0: e.dma_start(
                        fm(cx.y1_h, 8, t0, 128), t3(y1_t, 8)),
                        reads=("y1_t",), writes=("y1_h",), dma=True)
            return emit

        PVQ = 8
        pending_T1 = [None]
        pending_T2 = [None]
        for j in range(T // 256):
            jp = j % 2
            Qp, Qpn = Qp2[jp], "Qp%d" % jp
            sch.op("sync", lambda e, j=j, Qp=Qp: e.dma_start(
                Qp[:, :], cx.q_h[:, 2 * j * 1024:(2 * j + 2) * 1024]),
                reads=("q_h",), writes=(Qpn,), dma=True)
            Q3 = Qp[:, :].rearrange("p (c pr t) -> p c pr t", c=2, pr=8)

            def clr(e):
                for b in range(5):
                    n = min(512, NH * 130 - b * 512)
                    inst = e.matmul(pPV[:, b * 512:b * 512 + n],
                                    cx.zrow_bf[:, :], K_fm[0:1, 0:n],
                                    start=True, stop=True, skip_group_check=True)
                return inst
            pvq = []
            ntk = 2 * j + 2
            cnt = 0
            for tkc in range(ntk):
                for h in range(NH):
                    if cnt == 4:
                        if pending_T1[0] is not None:
                            pending_T1[0]()
                            pending_T1[0] = None
                        sch.op("tensor", clr, reads=("zrow_bf", "K_fm"),
                               writes=("pPV",))
                    cnt += 1
                    i = tkc * NH + h
                    sp = psS[i % 2]
                    spn = "psS%d" % (i % 2)
                    base = 64 * (h % 2)
                    pr = h // 2
                    lhsK = K_fm[base:base + 64,
                                pr * T + tkc * 128:pr * T + (tkc + 1) * 128]
                    PTh = PTs[h]
                    PThn = "PT%d" % h
                    if tkc < 2 * j + 1:
                        def smm(e, sp=sp, lhsK=lhsK, h=h, diag=(tkc == 2 * j), Q3=Q3):
                            b2 = 64 * (h % 2)
                            p2 = h // 2
                            inst = e.matmul(sp[:, 0:256], lhsK,
                                            Q3[b2:b2 + 64, :, p2, :], start=True,
                                            stop=not diag)
                            if diag:
                                inst = e.matmul(sp[:, 0:128], cx.id128_bf[:, :],
                                                cx.maskS_bf[:, :], start=False,
                                                stop=True)
                            return inst
                        sch.op("tensor", smm,
                               reads=("K_fm", Qpn, "id128_bf", "maskS_bf"),
                               writes=(spn,))
                        sch.op("scalar", lambda e, sp=sp, PTh=PTh: e.activation(
                            PTh[:, :], sp[:, 0:256], AF.Exp, scale=0.125),
                            reads=(spn,), writes=(PThn,))
                        q0v = True
                    else:
                        def smm(e, sp=sp, lhsK=lhsK, h=h, Q3=Q3):
                            b2 = 64 * (h % 2)
                            p2 = h // 2
                            e.matmul(sp[:, 128:256], lhsK, Q3[b2:b2 + 64, 1, p2, :],
                                     start=True, stop=False)
                            return e.matmul(sp[:, 128:256], cx.id128_bf[:, :],
                                            cx.maskS_bf[:, :], start=False, stop=True)
                        sch.op("tensor", smm,
                               reads=("K_fm", Qpn, "id128_bf", "maskS_bf"),
                               writes=(spn,))
                        sch.op("scalar", lambda e, sp=sp, PTh=PTh: e.activation(
                            PTh[:, 128:256], sp[:, 128:256], AF.Exp, scale=0.125),
                            reads=(spn,), writes=(PThn,))
                        q0v = False

                    vsl = V_tm[:, tkc * (NH * 65) + h * 65:tkc * (NH * 65) + (h + 1) * 65]

                    def pvmm(e, h=h, q0v=q0v, vsl=vsl, PTh=PTh):
                        inst = None
                        for qi, cbase in ((0, 130 * h), (1, 130 * h + 65)):
                            if qi == 0 and not q0v:
                                continue
                            for (c0, c1) in segs(cbase, cbase + 65):
                                inst = e.matmul(pPV[:, c0:c1],
                                                PTh[:, 128 * qi:128 * qi + 128],
                                                vsl[:, c0 - cbase:c1 - cbase],
                                                start=False, stop=False,
                                                skip_group_check=True)
                        return inst
                    pvq.append((pvmm, PThn))
                    if len(pvq) > PVQ:
                        fn, rdn = pvq.pop(0)
                        sch.op("tensor", fn, reads=(rdn, "V_tm"), writes=("pPV",))
            if pending_T2[0] is not None:
                pending_T2[0]()
                pending_T2[0] = None
            pending_T1[0] = mk_T1(j, list(pvq))
            pending_T2[0] = mk_T2(j)
            pvq = []
        if pending_T1[0] is not None:
            pending_T1[0]()
            pending_T1[0] = None
        if pending_T2[0] is not None:
            pending_T2[0]()
            pending_T2[0] = None
        sch.dma_join()
        sch.flush(blkFB)
        blkFB.__exit__(None, None, None)


def _phase_FM(cx, nc, sch):
    hbm = cx.hbm
    es = ExitStack()
    sb = lambda n, s, d: es.enter_context(nc.sbuf_tensor(n, s, d))
    ps = lambda n, s, d: es.enter_context(nc.psum_tensor(n, s, d))
    TH = T // 2
    NCH = TH // 128

    m_in = sb("m_in", [128, 8 * TH], BF16)
    acc = sb("acc", [128, 8 * TH], F32)
    rw = sb("rw", [128, 8 * 7], BF16)
    rb_bc = sb("rb_bc", [128, 7], F32)
    rbrow = sb("rbrow", [1, 7], F32)
    ln2 = sb("ln2", [128, 8], F32)
    xf2 = [sb("xf_m0", [128, 8 * 128], F32), sb("xf_m1", [128, 8 * 128], F32)]
    y1c2 = [sb("y1c0", [128, 8 * 128], F32), sb("y1c1", [128, 8 * 128], F32)]
    sqf2 = [sb("sqf_m0", [128, 8 * 128], F32), sb("sqf_m1", [128, 8 * 128], F32)]
    rowsF2 = [sb("rowsF_m0", [1, 5 * 128], F32), sb("rowsF_m1", [1, 5 * 128], F32)]
    probs2 = [sb("probs0", [128, 7], F32), sb("probs1", [128, 7], F32)]
    e12 = [sb("e1_0", [128, 7], F32), sb("e1_1", [128, 7], F32)]
    pmk2 = [sb("pmk0", [128, 7], F32), sb("pmk1", [128, 7], F32)]
    e22 = [sb("e2_0", [128, 7], F32), sb("e2_1", [128, 7], F32)]
    rt2 = [sb("rt0", [128, 8], F32), sb("rt1", [128, 8], F32)]
    wgt_tm = sb("wgt_tm", [128, NCH * 7], F32)
    wgt_fm = sb("wgt_fm", [7, TH], F32)
    ehsel = sb("ehsel", [7, 7 * 128], F32)
    gbuf = [sb("gbuf0", [128, 8 * 512], BF16), sb("gbuf1", [128, 8 * 512], BF16)]
    ubuf = [sb("ubuf0", [128, 8 * 512], BF16), sb("ubuf1", [128, 8 * 512], BF16)]
    dbuf = [sb("dbuf0", [128, 4 * D], BF16), sb("dbuf1", [128, 4 * D], BF16)]
    gs2 = [sb("gs0", [128, 512], BF16), sb("gs1", [128, 512], BF16)]
    wbc = sb("wbc", [128, 512], BF16)
    hq2 = [sb("hq0", [128, 4 * 512], BF16), sb("hq1", [128, 4 * 512], BF16)]

    pg = [ps("pgm0", [128, 512], F32), ps("pgm1", [128, 512], F32)]
    pu = [ps("pum0", [128, 512], F32), ps("pum1", [128, 512], F32)]
    pd = [ps("pdm0", [128, 512], F32), ps("pdm1", [128, 512], F32)]
    pmisc = ps("pmisc", [128, 512], F32)
    pwb = ps("pwb", [128, 512], F32)

    with es:
        s = hbm["rwT"][:, :].rearrange("(k p) o -> p k o", p=128)
        sch.op("gpsimd", lambda e, d=t3(rw, 8): e.dma_start(d, s),
               writes=("rw",), dma=True)
        sch.op("sync", lambda e: e.dma_start(rbrow[:, :], hbm["rbias"][:, :]),
               writes=("rbrow",), dma=True)
        sch.op("sync", lambda e: e.dma_start(ln2[:, :], hbm["ln2c"][:, :]),
               writes=("ln2",), dma=True)
        sch.op("tensor", lambda e: e.matmul(pmisc[:, 0:7], cx.ones_f[0:1, 0:128],
                                            rbrow[0:1, :], start=True, stop=True),
               reads=("rbrow", "ones_f"), writes=("pmisc",))
        sch.op("scalar", lambda e: e.copy(rb_bc[:, :], pmisc[:, 0:7]),
               reads=("pmisc",), writes=("rb_bc",))

        sch.op("gpsimd", lambda e: e.memset(ehsel[:, :], 1.0), writes=("ehsel",))
        sch.op("gpsimd", lambda e: e.affine_select(
            out=ehsel[:, :], in_=ehsel[:, :], compare_op=OP.is_ge,
            fill=0.0, base=0, pattern=[[1, 7 * 128]], channel_multiplier=-128),
            reads=("ehsel",), writes=("ehsel",))
        sch.op("gpsimd", lambda e: e.affine_select(
            out=ehsel[:, :], in_=ehsel[:, :], compare_op=OP.is_ge,
            fill=0.0, base=127, pattern=[[-1, 7 * 128]], channel_multiplier=128),
            reads=("ehsel",), writes=("ehsel",))

        def bufs(ch):
            par = ch % 2
            return dict(
                yc=y1c2[par], ycn="y1c%d" % par, sq=sqf2[par], sqn="sqf_m%d" % par,
                xf=xf2[par], xfn="xf_m%d" % par, rf=rowsF2[par],
                rfn="rowsF_m%d" % par,
                pm=pmisc if par == 0 else pwb, pmn="pmisc" if par == 0 else "pwb",
                probs=probs2[par], prn="probs%d" % par, e1=e12[par],
                e1n="e1_%d" % par, pmk=pmk2[par], pkn="pmk%d" % par,
                e2=e22[par], e2n="e2_%d" % par, rt=rt2[par], rtn="rt%d" % par)

        def stageA(half, ch):
            b = bufs(ch)
            yc, ycn, sq, sqn, xf, xfn = (b[k] for k in
                                         ("yc", "ycn", "sq", "sqn", "xf", "xfn"))
            pm, pmn = b["pm"], b["pmn"]
            t0 = half * TH + ch * 128
            sch.op("sync", lambda e, t0=t0, yc=yc: e.dma_start(
                t3(yc, 8), fm(cx.y1_h, 8, t0, 128)),
                reads=("y1_h",), writes=(ycn,), dma=True)
            sch.op("sync", lambda e, t0=t0, xf=xf: e.dma_start(
                t3(xf, 8), fm(cx.x1_h, 8, t0, 128)),
                reads=("x1_h",), writes=(xfn,), dma=True)
            sch.op("scalar", lambda e, sq=sq, yc=yc: e.activation(
                sq[:, :], yc[:, :], AF.Square),
                reads=(ycn,), writes=(sqn,))
            sch.mm_group(pm[0:1, 0:128],
                         [(cx.ones_f[:, 0:1], yc[:, bass.ts(k, 128)]) for k in range(8)],
                         reads=(ycn, "ones_f"), writes=(pmn,))
            sch.mm_group(pm[0:1, 128:256],
                         [(cx.ones_f[:, 0:1], sq[:, bass.ts(k, 128)]) for k in range(8)],
                         reads=(sqn, "ones_f"), writes=(pmn,))

        def stageB(half, ch):
            b = bufs(ch)
            yc, ycn, sq, sqn, xf, xfn, rf, rfn, pm, pmn = (
                b[k] for k in ("yc", "ycn", "sq", "sqn", "xf", "xfn", "rf", "rfn",
                               "pm", "pmn"))
            t0h = ch * 128
            sch.op("vector", lambda e, rf=rf, pm=pm: e.tensor_scalar(
                rf[0:1, 0:256], pm[0:1, 0:256], 1.0 / D, None, OP.mult),
                reads=(pmn,), writes=(rfn,))
            sch.op("vector", lambda e, rf=rf: e.tensor_tensor(
                rf[0:1, 256:384], rf[0:1, 0:128], rf[0:1, 0:128], OP.mult),
                reads=(rfn,), writes=(rfn,))
            sch.op("vector", lambda e, rf=rf: e.tensor_tensor(
                rf[0:1, 256:384], rf[0:1, 128:256], rf[0:1, 256:384], OP.subtract),
                reads=(rfn,), writes=(rfn,))
            sch.op("scalar", lambda e, rf=rf: e.activation(
                rf[0:1, 384:512], rf[0:1, 256:384], AF.Sqrt, bias=cx.cvals[0:1, 0:1]),
                reads=(rfn, "cvals"), writes=(rfn,))
            sch.op("vector", lambda e, rf=rf: e.reciprocal(
                rf[0:1, 384:512], rf[0:1, 384:512]),
                reads=(rfn,), writes=(rfn,))
            sch.op("vector", lambda e, rf=rf: e.tensor_tensor(
                rf[0:1, 512:640], rf[0:1, 0:128], rf[0:1, 384:512], OP.mult),
                reads=(rfn,), writes=(rfn,))
            sch.op("tensor", lambda e, pm=pm, rf=rf: e.matmul(
                pm[:, 256:512], cx.ones_f[0:1, 0:128], rf[0:1, 384:640],
                start=True, stop=True),
                reads=(rfn, "ones_f"), writes=(pmn,))
            sch.op("vector", lambda e, sq=sq, yc=yc, pm=pm: e.tensor_tensor(
                sq[:, :].rearrange("p (k t) -> p k t", k=8),
                yc[:, :].rearrange("p (k t) -> p k t", k=8),
                pm[:, 256:384].unsqueeze(1).broadcast_to([128, 8, 128]), OP.mult),
                reads=(ycn, pmn), writes=(sqn,))
            sch.op("vector", lambda e, sq=sq, pm=pm: e.tensor_tensor(
                sq[:, :].rearrange("p (k t) -> p k t", k=8),
                sq[:, :].rearrange("p (k t) -> p k t", k=8),
                pm[:, 384:512].unsqueeze(1).broadcast_to([128, 8, 128]), OP.subtract),
                reads=(sqn, pmn), writes=(sqn,))
            for k in range(8):
                sch.op("scalar", lambda e, k=k, t0h=t0h, sq=sq: e.mul(
                    m_in[:, k * TH + t0h:k * TH + t0h + 128], sq[:, bass.ts(k, 128)],
                    ln2[:, k:k + 1]),
                    reads=(sqn, "ln2"), writes=("m_in",))
            sch.op("vector", lambda e, t0h=t0h, xf=xf, yc=yc: e.tensor_tensor(
                acc[:, :].rearrange("p (k t) -> p k t", k=8)[:, :, t0h:t0h + 128],
                t3(xf, 8), t3(yc, 8), OP.add),
                reads=(xfn, ycn), writes=("acc",))

        def stageC(half, ch):
            b = bufs(ch)
            pm, pmn, probs, prn, e1, e1n, pmk, pkn, e2, e2n, rt, rtn = (
                b[k] for k in ("pm", "pmn", "probs", "prn", "e1", "e1n", "pmk",
                               "pkn", "e2", "e2n", "rt", "rtn"))
            sch.mm_group(pm[:, 0:7],
                         [(m_in[:, k * TH + ch * 128:k * TH + (ch + 1) * 128],
                           rw[:, bass.ts(k, 7)]) for k in range(8)],
                         reads=("m_in", "rw"), writes=(pmn,))
            sch.op("vector", lambda e, probs=probs, pm=pm: e.tensor_tensor(
                probs[:, :], pm[:, 0:7], rb_bc[:, :], OP.add),
                reads=(pmn, "rb_bc"), writes=(prn,))
            sch.op("scalar", lambda e, probs=probs: e.activation(
                probs[:, :], probs[:, :], AF.Sigmoid),
                reads=(prn,), writes=(prn,))
            sch.op("vector", lambda e, rt=rt, probs=probs: e.reduce_max(
                rt[:, 0:1], probs[:, :], axis=mybir.AxisListType.X),
                reads=(prn,), writes=(rtn,))
            sch.op("vector", lambda e, e1=e1, probs=probs, rt=rt: e.tensor_scalar(
                e1[:, :], probs[:, :], rt[:, 0:1], None, OP.is_equal),
                reads=(prn, rtn), writes=(e1n,))
            sch.op("vector", lambda e, pmk=pmk, e1=e1, probs=probs:
                   e.scalar_tensor_tensor(
                       pmk[:, :], e1[:, :], -1e30, probs[:, :], OP.mult, OP.add),
                   reads=(e1n, prn), writes=(pkn,))
            sch.op("vector", lambda e, rt=rt, pmk=pmk: e.reduce_max(
                rt[:, 1:2], pmk[:, :], axis=mybir.AxisListType.X),
                reads=(pkn,), writes=(rtn,))
            sch.op("vector", lambda e, e2=e2, pmk=pmk, rt=rt: e.tensor_scalar(
                e2[:, :], pmk[:, :], rt[:, 1:2], None, OP.is_equal),
                reads=(pkn, rtn), writes=(e2n,))
            sch.op("vector", lambda e, rt=rt: e.tensor_tensor(
                rt[:, 2:3], rt[:, 0:1], rt[:, 1:2], OP.add),
                reads=(rtn,), writes=(rtn,))
            sch.op("vector", lambda e, rt=rt: e.reciprocal(rt[:, 3:4], rt[:, 2:3]),
                   reads=(rtn,), writes=(rtn,))
            sch.op("vector", lambda e, rt=rt: e.tensor_tensor(
                rt[:, 4:5], rt[:, 0:1], rt[:, 3:4], OP.mult),
                reads=(rtn,), writes=(rtn,))
            sch.op("vector", lambda e, rt=rt: e.tensor_tensor(
                rt[:, 5:6], rt[:, 1:2], rt[:, 3:4], OP.mult),
                reads=(rtn,), writes=(rtn,))
            sch.op("vector", lambda e, e1=e1, rt=rt: e.tensor_scalar(
                e1[:, :], e1[:, :], rt[:, 4:5], None, OP.mult),
                reads=(e1n, rtn), writes=(e1n,))
            sch.op("vector", lambda e, ch=ch, e2=e2, rt=rt, e1=e1:
                   e.scalar_tensor_tensor(
                       wgt_tm[:, bass.ts(ch, 7)], e2[:, :], rt[:, 5:6], e1[:, :],
                       OP.mult, OP.add),
                   reads=(e2n, rtn, e1n), writes=("wgt_tm",))
            sch.op("tensor", lambda e, ch=ch, pm=pm: e.transpose(
                pm[0:7, 0:128], wgt_tm[:, bass.ts(ch, 7)], cx.id128_f[:, :]),
                reads=("wgt_tm", "id128_f"), writes=(pmn,))
            sch.op("scalar", lambda e, ch=ch, pm=pm: e.copy(
                wgt_fm[:, ch * 128:(ch + 1) * 128], pm[0:7, 0:128]),
                reads=(pmn,), writes=("wgt_fm",))

        for half in range(2):
            th0 = half * TH
            for i in range(NCH + 2):
                if i >= 2:
                    stageC(half, i - 2)
                if 1 <= i <= NCH:
                    stageB(half, i - 1)
                if i < NCH:
                    stageA(half, i)

            pending_down = [None]
            step = [0]
            for ex in range(NE):
                for q in range(4):
                    par = (ex * 4 + q) % 2
                    gb, ub, db = gbuf[par], ubuf[par], dbuf[par]
                    sg = hbm["moe_g"][ex * D:(ex + 1) * D,
                                      q * 512:(q + 1) * 512].rearrange(
                        "(k p) o -> p k o", p=128)
                    su = hbm["moe_u"][ex * D:(ex + 1) * D,
                                      q * 512:(q + 1) * 512].rearrange(
                        "(k p) o -> p k o", p=128)
                    sd = hbm["moe_d"][ex * IM + q * 512:ex * IM + (q + 1) * 512,
                                      :].rearrange("(k p) o -> p k o", p=128)
                    sch.op("gpsimd", lambda e, d=t3(gb, 8), s=sg: e.dma_start(d, s),
                           writes=(gb.name,), dma=True)
                    sch.op("gpsimd", lambda e, d=t3(ub, 8), s=su: e.dma_start(d, s),
                           writes=(ub.name,), dma=True)
                    sch.op("gpsimd", lambda e, d=t3(db, 4), s=sd: e.dma_start(d, s),
                           writes=(db.name,), dma=True)
                    for t in range(4):
                        sp_ = step[0] % 2
                        hqc, hqn = hq2[sp_], "hq%d" % sp_
                        gsc, gsn2 = gs2[sp_], "gs%d" % sp_
                        step[0] += 1
                        if ex > 0:
                            sch.op("tensor", lambda e, ex=ex, t=t: e.matmul(
                                pwb[:, :], ehsel[:, bass.ts(ex - 1, 128)],
                                wgt_fm[:, t * 512:(t + 1) * 512],
                                start=True, stop=True),
                                reads=("ehsel", "wgt_fm"), writes=("pwb",))
                            sch.op("scalar", lambda e: e.copy(wbc[:, :], pwb[:, :]),
                                   reads=("pwb",), writes=("wbc",))
                        for mi in range(4):
                            pgi = pg[mi % 2]
                            pgn = "pgm%d" % (mi % 2)
                            pui = pu[mi % 2]
                            pun = "pum%d" % (mi % 2)
                            sch.mm_group(pgi[:, :],
                                         [(gb[:, k * 512 + mi * 128:k * 512 + (mi + 1) * 128],
                                           m_in[:, k * TH + t * 512:k * TH + (t + 1) * 512])
                                          for k in range(8)],
                                         reads=(gb.name, "m_in"), writes=(pgn,))
                            sch.op("scalar", lambda e, pgi=pgi, gsc=gsc: e.activation(
                                gsc[:, :], pgi[:, :], AF.Silu),
                                reads=(pgn,), writes=(gsn2,))
                            sch.mm_group(pui[:, :],
                                         [(ub[:, k * 512 + mi * 128:k * 512 + (mi + 1) * 128],
                                           m_in[:, k * TH + t * 512:k * TH + (t + 1) * 512])
                                          for k in range(8)],
                                         reads=(ub.name, "m_in"), writes=(pun,))
                            if ex > 0:
                                sch.op("vector", lambda e, pui=pui, gsc=gsc: e.tensor_tensor(
                                    gsc[:, :], gsc[:, :], pui[:, :], OP.mult),
                                    reads=(gsn2, pun), writes=(gsn2,))
                                sch.op("vector", lambda e, mi=mi, hqc=hqc, gsc=gsc:
                                       e.tensor_tensor(
                                           hqc[:, bass.ts(mi, 512)], gsc[:, :], wbc[:, :],
                                           OP.mult),
                                       reads=(gsn2, "wbc"), writes=(hqn,))
                            else:
                                sch.op("vector", lambda e, mi=mi, pui=pui, hqc=hqc, gsc=gsc:
                                       e.tensor_tensor(
                                           hqc[:, bass.ts(mi, 512)], gsc[:, :], pui[:, :],
                                           OP.mult),
                                       reads=(gsn2, pun), writes=(hqn,))

                        def mk_down(db=db, hqc=hqc, hqn=hqn, t=t):
                            def emit():
                                for m2 in range(8):
                                    pdi = pd[m2 % 2]
                                    pdn = "pdm%d" % (m2 % 2)
                                    sch.mm_group(pdi[:, :],
                                                 [(db[:, k * D + m2 * 128:k * D + (m2 + 1) * 128],
                                                   hqc[:, bass.ts(k, 512)]) for k in range(4)],
                                                 reads=(db.name, hqn), writes=(pdn,))
                                    sch.op("vector", lambda e, m2=m2, t=t, pdi=pdi:
                                           e.tensor_tensor(
                                               acc[:, m2 * TH + t * 512:m2 * TH + (t + 1) * 512],
                                               acc[:, m2 * TH + t * 512:m2 * TH + (t + 1) * 512],
                                               pdi[:, :], OP.add),
                                           reads=("acc", pdn), writes=("acc",))
                            return emit
                        if pending_down[0] is not None:
                            pending_down[0]()
                        pending_down[0] = mk_down()
            if pending_down[0] is not None:
                pending_down[0]()
                pending_down[0] = None
            sch.op("sync", lambda e, th0=th0: e.dma_start(
                fm(cx.out_fm, 8, th0, TH), t3(acc, 8)),
                reads=("acc",), dma=True)


def kernel(**inputs):
    in_maps = _stage(inputs)
    nc = build_program(dev=False, upto="FM")
    res = run_bass_kernel_spmd(nc, in_maps, core_ids=list(range(B)))
    out = np.empty((B, T, D), np.float32)
    for b in range(B):
        out[b] = np.asarray(res.results[b]["out_fm"]).T
    return out


# version 25 (final)
# speedup vs baseline: 1.0261x; 1.0261x over previous
import bisect
import math
from contextlib import ExitStack

import numpy as np
import concourse.bass as bass
import concourse.mybir as mybir
from concourse.bass_utils import run_bass_kernel_spmd

F32 = mybir.dt.float32
BF16 = mybir.dt.bfloat16
I32 = mybir.dt.int32
AF = mybir.ActivationFunctionType
OP = mybir.AluOpType

B, T, D = 8, 4096, 1024
DI = 2048
NH = 16
RD = 32
LAT = 256
IM = 2048
NE = 8
EPS = 1e-5
TS_MA = 256
TS_MB = 256
TWO_PI = 2.0 * math.pi

ENGINES = ("tensor", "vector", "scalar", "gpsimd", "sync")
STRICT_SAME_ENGINE = True
DMA_QUEUE = {"sync": "dq_sync", "gpsimd": "dq_gp", "scalar": "dq_sca"}


class _Op:
    __slots__ = ("engine", "fn", "eidx", "marked", "val", "dma", "dmaval",
                 "dmasem", "waits")

    def __init__(self, engine, fn, dma):
        self.engine, self.fn, self.dma = engine, fn, dma
        self.eidx, self.marked, self.val, self.dmaval = -1, False, None, None
        self.waits = []


DMA_POOL = 8


class Sched:

    def __init__(self, nc, es):
        self.nc = nc
        self.ops = {e: [] for e in ENGINES}
        self.sem = {}
        for e in ENGINES:
            self.sem[e] = es.enter_context(nc.semaphore("s_" + e))
        self.dma_pool = {}
        self.dma_uses = {}
        self.dma_prev = {}
        for q in set(DMA_QUEUE.values()):
            self.dma_pool[q] = [es.enter_context(nc.semaphore("%s_%d" % (q, i)))
                                for i in range(DMA_POOL)]
            self.dma_uses[q] = [0] * DMA_POOL
            self.dma_prev[q] = [None] * DMA_POOL
        self.sem_count = {e: 0 for e in ENGINES}
        self.dma_count = {q: 0 for q in set(DMA_QUEUE.values())}
        self.marked_idx = {e: [] for e in ENGINES}
        self.bufs = {}

    def _buf(self, name):
        return self.bufs.setdefault(name, {"w": None, "r": []})

    def _dep(self, op, producer):
        if producer is None or producer is op:
            return
        if producer.dma:
            op.waits.append(producer)
            return
        if producer.engine == op.engine and (producer.engine == "tensor"
                                             or not STRICT_SAME_ENGINE):
            return
        ml = self.marked_idx[producer.engine]
        lst = self.ops[producer.engine]
        i = bisect.bisect_left(ml, producer.eidx)
        if i < len(ml):
            op.waits.append(lst[ml[i]])
        else:
            producer.marked = True
            ml.append(producer.eidx)
            op.waits.append(producer)

    def op(self, engine, fn, reads=(), writes=(), dma=False):
        o = _Op(engine, fn, dma)
        o.eidx = len(self.ops[engine])
        if dma:
            q = DMA_QUEUE[engine]
            slot = self.dma_count[q] % DMA_POOL
            self.dma_count[q] += 1
            if self.dma_uses[q][slot] > 0:
                o.waits.append(("sem", self.dma_pool[q][slot],
                                16 * self.dma_uses[q][slot]))
            self.dma_uses[q][slot] += 1
            o.dmasem = self.dma_pool[q][slot]
            o.dmaval = 16 * self.dma_uses[q][slot]
        for r in reads:
            self._dep(o, self._buf(r)["w"])
        for w in writes:
            b = self._buf(w)
            self._dep(o, b["w"])
            for rd in b["r"]:
                self._dep(o, rd)
        self.ops[engine].append(o)
        for r in reads:
            self._buf(r)["r"].append(o)
        for w in writes:
            b = self._buf(w)
            b["w"] = o
            b["r"] = []
        return o

    def dma_join(self):
        tot = []
        for q in self.dma_pool:
            for i in range(DMA_POOL):
                if self.dma_uses[q][i]:
                    tot.append((self.dma_pool[q][i], 16 * self.dma_uses[q][i]))

        def f(e):
            inst = e.nop()
            for s, v in tot:
                e.wait_ge(s, v)
            return e.nop()
        o = _Op("sync", f, False)
        o.eidx = len(self.ops["sync"])
        self.ops["sync"].append(o)

    def flush(self, block, reset_bufs=True):
        for e in ENGINES:
            for o in self.ops[e]:
                if o.marked and not o.dma:
                    self.sem_count[e] += 1
                    o.val = self.sem_count[e]

        def resolve(w):
            if isinstance(w, tuple):
                return w[1], w[2]
            if w.dma:
                return w.dmasem, w.dmaval
            return self.sem[w.engine], w.val

        def emitter(ename):
            ops = self.ops[ename]

            def emit(eng):
                waited = {}
                for o in ops:
                    for w in o.waits:
                        semh, val = resolve(w)
                        assert val is not None, "unresolved wait"
                        if waited.get(semh.num, -1) < val:
                            eng.wait_ge(semh, val)
                            waited[semh.num] = val
                    inst = o.fn(eng)
                    if o.dma:
                        inst.then_inc(o.dmasem, 16)
                    elif o.marked:
                        inst.then_inc(self.sem[o.engine], 1)
            return emit

        for e in ENGINES:
            if self.ops[e]:
                getattr(block, e)(emitter(e))
        self.ops = {e: [] for e in ENGINES}
        self.marked_idx = {e: [] for e in ENGINES}
        if reset_bufs:
            nb = {}
            for k, v in self.bufs.items():
                if v["w"] is not None and v["w"].dma:
                    nb[k] = {"w": v["w"], "r": []}
            self.bufs = nb

    def mm_group(self, out, pairs, reads, writes, accum=False):
        n = len(pairs)

        def f(e):
            for i, (l, r) in enumerate(pairs):
                inst = e.matmul(out, l, r, start=(i == 0 and not accum),
                                stop=(i == n - 1))
            return inst
        return self.op("tensor", f, reads=reads, writes=writes)


def _stage(inp):
    c = lambda x: np.ascontiguousarray(np.asarray(x, dtype=np.float32))
    w = {}
    W_in = np.asarray(inp["L0_attn_W_in"], np.float32)
    w["wxT"] = c(W_in[6144:8192].T)
    w["wBT"] = c(W_in[8192:8208].T)
    w["wCT"] = c(W_in[8208:8224].T)
    w["wdtT"] = c(W_in[8224:8232].T)
    w["wz0T"] = c(W_in[0:2048].T)
    w["wx0T"] = c(W_in[2048:4096].T)
    cw = np.asarray(inp["L0_attn_conv_w"], np.float32)[:, 0, :]
    cb = np.asarray(inp["L0_attn_conv_b"], np.float32)
    w["convx"] = c(cw[:2048].reshape(16, 128, 4).transpose(1, 0, 2).reshape(128, 64))
    w["convxb"] = c(cb[:2048].reshape(16, 128).T)
    w["convB"] = c(cw[2048:2064])
    w["convBb"] = c(cb[2048:2064][:, None])
    w["convC"] = c(cw[2064:2080])
    w["convCb"] = c(cb[2064:2080][:, None])
    w["dtbias"] = c(np.asarray(inp["L0_attn_dt_bias"], np.float32)[:, None])
    w["arow"] = c(np.asarray(inp["L0_attn_A_log"], np.float32)[None, :])
    w["drow"] = c(np.asarray(inp["L0_attn_D_param"], np.float32)[None, :])
    w["gbrow"] = c(np.concatenate([np.asarray(inp["L0_attn_ln_gamma"], np.float32),
                                   np.asarray(inp["L0_attn_ln_beta"], np.float32)])[None, :])
    w["wgT"] = c(np.asarray(inp["L0_attn_W_gate"], np.float32).T)
    w["bg"] = c(np.asarray(inp["L0_attn_b_gate"], np.float32).reshape(16, 128).T)
    w["woT"] = c(np.asarray(inp["L0_attn_W_out"], np.float32).T)
    w["kvdT"] = c(np.asarray(inp["L0_ffn_kv_d"], np.float32).T)
    w["qdT"] = c(np.asarray(inp["L0_ffn_q_d"], np.float32).T)
    k_u = np.asarray(inp["L0_ffn_k_u"], np.float32).reshape(NH, 64, LAT)
    q_u = np.asarray(inp["L0_ffn_q_u"], np.float32).reshape(NH, 64, LAT)
    w["kuT"] = c(k_u[:, RD:, :].reshape(NH * RD, LAT).T)
    w["quT"] = c(q_u[:, RD:, :].reshape(NH * RD, LAT).T)
    w["vuT"] = c(np.asarray(inp["L0_ffn_v_u"], np.float32).T)
    rq = np.asarray(inp["L0_ffn_rope_q_w"], np.float32)
    rk = np.asarray(inp["L0_ffn_rope_k_w"], np.float32)
    perm = (np.arange(NH)[:, None] * RD + (np.arange(RD)[None, :] + RD // 2) % RD).ravel()
    w["rqT"] = c(rq.T)
    w["rqTp"] = c(rq.T[:, perm])
    w["rkT"] = c(rk.T)
    w["rkTp"] = c(rk.T[:, perm])
    w["owT"] = c(np.asarray(inp["L0_ffn_o_w"], np.float32).T)
    w["ln1c"] = c(np.asarray(inp["L0_ffn_ln1_w"], np.float32).reshape(8, 128).T)
    w["ln2c"] = c(np.asarray(inp["L0_ffn_ln2_w"], np.float32).reshape(8, 128).T)
    w["rwT"] = c(np.asarray(inp["L0_ffn_router_w"], np.float32).T)
    w["rbias"] = c(np.asarray(inp["L0_ffn_routing_bias"], np.float32)[None, :])
    w["moe_g"] = c(np.concatenate([np.asarray(inp["L0_ffn_shared_gate"], np.float32),
                                   np.asarray(inp["L0_ffn_routed_gate"], np.float32)], 0
                                  ).transpose(0, 2, 1).reshape(NE * D, IM))
    w["moe_u"] = c(np.concatenate([np.asarray(inp["L0_ffn_shared_up"], np.float32),
                                   np.asarray(inp["L0_ffn_routed_up"], np.float32)], 0
                                  ).transpose(0, 2, 1).reshape(NE * D, IM))
    w["moe_d"] = c(np.concatenate([np.asarray(inp["L0_ffn_shared_down"], np.float32),
                                   np.asarray(inp["L0_ffn_routed_down"], np.float32)], 0
                                  ).transpose(0, 2, 1).reshape(NE * IM, D))
    u = np.asarray(inp["L0_attn_u"], np.float32)
    in_maps = []
    for b in range(B):
        m = dict(w)
        m["u"] = c(u[b].T)
        in_maps.append(m)
    return in_maps


def fm(hbm, kch, t0, tt):
    return hbm[:, :].rearrange("(k p) t -> p k t", p=128)[:, :, t0:t0 + tt]


def t3(tile, kch):
    return tile[:, :].rearrange("p (k t) -> p k t", k=kch)


HBM_SHAPES = [
    ("u", [D, T]), ("wxT", [D, DI]), ("wBT", [D, 16]), ("wCT", [D, 16]),
    ("wdtT", [D, 8]), ("wz0T", [D, DI]), ("wx0T", [D, DI]),
    ("convx", [128, 64]), ("convxb", [128, 16]), ("convB", [16, 4]),
    ("convBb", [16, 1]), ("convC", [16, 4]), ("convCb", [16, 1]),
    ("dtbias", [8, 1]), ("arow", [1, 8]), ("drow", [1, 8]),
    ("gbrow", [1, 2 * DI]), ("wgT", [DI, DI]), ("bg", [128, 16]),
    ("woT", [2 * DI, D]),
    ("kvdT", [D, LAT]), ("qdT", [D, LAT]), ("kuT", [LAT, 512]),
    ("quT", [LAT, 512]), ("vuT", [LAT, D]), ("rqT", [LAT, 512]),
    ("rqTp", [LAT, 512]), ("rkT", [D, 512]), ("rkTp", [D, 512]),
    ("owT", [D, D]), ("ln1c", [128, 8]), ("ln2c", [128, 8]),
    ("rwT", [D, 7]), ("rbias", [1, 7]),
    ("moe_g", [NE * D, IM]), ("moe_u", [NE * D, IM]), ("moe_d", [NE * IM, D]),
]


class Ctx:
    pass


def build_program(dev=False, upto="FM"):
    nc = bass.Bass("TRN2", target_bir_lowering=False, detect_race_conditions=False)
    skind = "ExternalOutput" if dev else "Internal"
    cx = Ctx()
    cx.dev = dev
    cx.hbm = {}
    for name, shape in HBM_SHAPES:
        cx.hbm[name] = nc.dram_tensor(name, shape, F32, kind="ExternalInput")
    cx.lnu_h = nc.dram_tensor("lnu_h", [D, T], BF16, kind=skind)
    cx.yln_h = nc.dram_tensor("yln_h", [DI, T], BF16, kind=skind)
    cx.yg_h = nc.dram_tensor("yg_h", [DI, T], BF16, kind=skind)
    cx.x1_h = nc.dram_tensor("x1_h", [D, T], F32, kind=skind)
    cx.q_h = nc.dram_tensor("q_h", [128, (T // 128) * 1024], BF16, kind="Internal")
    cx.y1_h = nc.dram_tensor("y1_h", [D, T], F32, kind=skind)
    cx.y0_h = nc.dram_tensor("y0_h", [D, T], F32, kind="Internal")
    cx.out_fm = nc.dram_tensor("out_fm", [D, T], F32, kind="ExternalOutput")
    if dev:
        cx.dbg = {}
        for nm, shape, dt_ in (("d_xs", [128, 16 * TS_MA], BF16),
                               ("d_dtf", [8, 5 * TS_MA], F32),
                               ("d_yrow", [128, DI], F32),
                               ("d_lam", [128, 128], F32),
                               ("d_Mm", [128, 128], BF16),
                               ("d_Xtm", [128, 256], BF16),
                               ("d_srow", [1, 2048], F32),
                               ("d_dttm", [128, 8], F32),
                               ("d_hf", [16, 2048], F32),
                               ("d_Bs", [16, TS_MA], BF16),
                               ("d_Cs", [16, TS_MA], F32),
                               ("d_Csc", [16, 128], BF16),
                               ("d_eall", [16, 8], F32)):
            cx.dbg[nm] = nc.dram_tensor(nm, shape, dt_, kind="ExternalOutput")

    es = ExitStack()
    sch = Sched(nc, es)
    cx.nc, cx.sch = nc, sch

    def sb(name, shape, dt):
        return es.enter_context(nc.sbuf_tensor(name, shape, dt))
    cx.id128_bf = sb("id128_bf", [128, 128], BF16)
    cx.id128_f = sb("id128_f", [128, 128], F32)
    cx.id16_bf = sb("id16_bf", [16, 16], BF16)
    cx.id8_f = sb("id8_f", [8, 8], F32)
    cx.U_f = sb("U_f", [128, 128], F32)
    cx.maskS_f = sb("maskS_f", [128, 128], F32)
    cx.maskS_bf = sb("maskS_bf", [128, 128], BF16)
    cx.maskS2_f = sb("maskS2_f", [128, 256], F32)
    cx.ones_f = sb("ones_f", [128, 128], F32)
    cx.zrow_bf = sb("zrow_bf", [1, 128], BF16)
    cx.cvals = sb("cvals", [128, 2], F32)

    with es:
        with nc.Block() as blk:
            for t_, sq in ((cx.id128_bf, 128), (cx.id128_f, 128),
                           (cx.id16_bf, 16), (cx.id8_f, 8)):
                nm = t_.name
                sch.op("gpsimd", lambda e, t_=t_: e.memset(t_[:, :], 0.0), writes=(nm,))
                sch.op("gpsimd", lambda e, t_=t_, sq=sq: e.affine_select(
                    out=t_[:, :], in_=t_[:, :], compare_op=OP.not_equal,
                    fill=1.0, base=0, pattern=[[-1, sq]], channel_multiplier=1),
                    reads=(nm,), writes=(nm,))
            sch.op("gpsimd", lambda e: e.memset(cx.U_f[:, :], 0.0), writes=("U_f",))
            sch.op("gpsimd", lambda e: e.affine_select(
                out=cx.U_f[:, :], in_=cx.U_f[:, :], compare_op=OP.is_gt,
                fill=1.0, base=0, pattern=[[-1, 128]], channel_multiplier=1),
                reads=("U_f",), writes=("U_f",))
            for mt in (cx.maskS_f, cx.maskS_bf):
                nm = mt.name
                sch.op("gpsimd", lambda e, mt=mt: e.memset(mt[:, :], 0.0), writes=(nm,))
                sch.op("gpsimd", lambda e, mt=mt: e.affine_select(
                    out=mt[:, :], in_=mt[:, :], compare_op=OP.is_ge,
                    fill=-1e30, base=0, pattern=[[1, 128]], channel_multiplier=-1),
                    reads=(nm,), writes=(nm,))
            sch.op("gpsimd", lambda e: e.memset(cx.maskS2_f[:, :], 0.0),
                   writes=("maskS2_f",))
            for off in (0, 128):
                sch.op("gpsimd", lambda e, off=off: e.affine_select(
                    out=cx.maskS2_f[:, off:off + 128],
                    in_=cx.maskS2_f[:, off:off + 128],
                    compare_op=OP.is_ge, fill=-1e30, base=0,
                    pattern=[[1, 128]], channel_multiplier=-1),
                    reads=("maskS2_f",), writes=("maskS2_f",))
            sch.op("gpsimd", lambda e: e.memset(cx.ones_f[:, :], 1.0), writes=("ones_f",))
            sch.op("gpsimd", lambda e: e.memset(cx.zrow_bf[:, :], 0.0), writes=("zrow_bf",))
            sch.op("gpsimd", lambda e: (e.memset(cx.cvals[:, 0:1], EPS),
                                        e.memset(cx.cvals[:, 1:2], math.pi / 2))[-1],
                   writes=("cvals",))
            _phase_MA(cx, nc, sch)
            sch.dma_join()
            sch.flush(blk)
        if upto != "MA":
            with nc.Block() as blk:
                _phase_MG(cx, nc, sch)
                sch.dma_join()
                sch.flush(blk)
        if upto not in ("MA", "MG"):
            with nc.Block() as blk:
                _phase_MB(cx, nc, sch)
                sch.dma_join()
                sch.flush(blk)
        if upto in ("F", "FM"):
            _phase_F(cx, nc, sch)
        if upto == "FM":
            with nc.Block() as blk:
                _phase_FM(cx, nc, sch)
                sch.dma_join()
                sch.flush(blk)
        with nc.Block() as blk:
            def fin(e):
                for q in sch.dma_pool:
                    for i in range(DMA_POOL):
                        if sch.dma_uses[q][i]:
                            e.wait_ge(sch.dma_pool[q][i], 16 * sch.dma_uses[q][i])
                return e.nop()
            blk.sync(fin)
    return nc


def _phase_MA(cx, nc, sch):
    TT = TS_MA
    ntile = T // TT
    nch = TT // 128
    hbm = cx.hbm
    es = ExitStack()
    sb = lambda n, s, d: es.enter_context(nc.sbuf_tensor(n, s, d))
    ps = lambda n, s, d: es.enter_context(nc.psum_tensor(n, s, d))

    wx = sb("wx", [128, 8 * DI], BF16)
    wB = sb("wB", [128, 8 * 16], BF16)
    wC = sb("wC", [128, 8 * 16], BF16)
    wdt = sb("wdt", [128, 8 * 8], BF16)
    convx = sb("convx_s", [128, 64], F32)
    convxb = sb("convxb_s", [128, 16], F32)
    convB = sb("convB_s", [16, 4], F32)
    convBb = sb("convBb_s", [16, 1], F32)
    convC = sb("convC_s", [16, 4], F32)
    convCb = sb("convCb_s", [16, 1], F32)
    dtb = sb("dtb", [8, 1], F32)
    rows8 = sb("rows8", [1, 32], F32)
    gbrow = sb("gbrow_s", [1, 2 * DI], F32)
    A_bc = sb("A_bc", [128, 8], F32)
    idD = sb("idD", [128, 8 * 128], BF16)
    gam_bc = sb("gam_bc", [128, DI], BF16)
    bet_bc = sb("bet_bc", [128, DI], BF16)

    u_t = sb("u_t", [128, 8 * TT], F32)
    sq_t = sb("sq_t", [128, 8 * TT], F32)
    lnu_t = sb("lnu_t", [128, 8 * TT], BF16)
    rows = sb("rows", [1, 5 * TT], F32)
    xconv = sb("xconv", [128, 16 * (TT + 3)], F32)
    Bconv = sb("Bconv", [16, TT + 3], F32)
    Cconv = sb("Cconv", [16, TT + 3], F32)
    cscr = sb("cscr", [128, 2 * TT], F32)
    xs_t = sb("xs_t", [128, 16 * TT], BF16)
    Bs_bf = sb("Bs_bf", [16, TT], BF16)
    Cs_f = sb("Cs_f", [16, TT], F32)
    Cs_bf = sb("Cs_bf", [16, TT], BF16)
    dtf = sb("dtf", [8, 5 * TT], F32)
    dt_tm = [sb("dt_tm0", [128, 8], F32), sb("dt_tm1", [128, 8], F32)]
    dtA_tm = [sb("dtA_tm0", [128, 8], F32), sb("dtA_tm1", [128, 8], F32)]
    ns_tm = [sb("ns_tm0", [128, 8], F32), sb("ns_tm1", [128, 8], F32)]
    e_all = [sb("e_all0", [16, 8], F32), sb("e_all1", [16, 8], F32)]
    prod = [sb("prod0", [128, 8 * 128], F32), sb("prod1", [128, 8 * 128], F32)]
    srow = [sb("srow0", [1, 2 * 1024], F32), sb("srow1", [1, 2 * 1024], F32)]
    lam = [sb("lam0", [128, 128], F32), sb("lam1", [128, 128], F32)]
    Mm = [sb("Mm0", [128, 128], BF16), sb("Mm1", [128, 128], BF16)]
    X_tm = [sb("X_tm0", [128, 256], BF16), sb("X_tm1", [128, 256], BF16)]
    B_tm = sb("B_tm", [128, 16], BF16)
    Bw = [sb("Bw0", [128, 16], BF16), sb("Bw1", [128, 16], BF16)]
    wcol = [sb("wcol0", [128, 1], F32), sb("wcol1", [128, 1], F32)]
    G_sb = [sb("G_sb0", [128, 128], F32), sb("G_sb1", [128, 128], F32)]
    Csc_all = [sb("Csc_all0", [16, 1024], BF16), sb("Csc_all1", [16, 1024], BF16)]
    h_f = [sb("h_f0", [16, 1024], F32), sb("h_f1", [16, 1024], F32)]
    h_bf = [sb("h_bf0", [16, 1024], BF16), sb("h_bf1", [16, 1024], BF16)]
    h_tmp = sb("h_tmp", [16, 1024], F32)
    y_row = [sb("y_row0", [128, DI], F32), sb("y_row1", [128, DI], F32)]
    ycols = sb("ycols", [128, 8], F32)
    yln_bf = sb("yln_bf", [128, DI], BF16)
    scr2 = sb("scr2", [128, DI], BF16)
    yln_c = sb("yln_c", [128, 16 * 128], BF16)
    sq_y = sb("sq_y", [128, DI], F32)

    pxin = ps("pxin", [128, 512], F32)
    pdt = ps("pdt", [128, 512], F32)
    pgs = ps("pgs", [128, 256], F32)
    pym = ps("pym", [128, 384], F32)
    px = ps("px", [128, 272], BF16)
    ph = ps("ph", [16, 1024], F32)
    p8 = ps("p8", [128, 512], F32)

    with es:
        for dst, srcn, kch, width in ((wx, "wxT", 8, DI), (wB, "wBT", 8, 16),
                                      (wC, "wCT", 8, 16), (wdt, "wdtT", 8, 8)):
            s = hbm[srcn][:, :].rearrange("(k p) o -> p k o", p=128)
            sch.op("gpsimd", lambda e, d=t3(dst, kch), s=s: e.dma_start(d, s),
                   writes=(dst.name,), dma=True)
        for dst, srcn in ((convx, "convx"), (convxb, "convxb"), (convB, "convB"),
                          (convBb, "convBb"), (convC, "convC"), (convCb, "convCb"),
                          (dtb, "dtbias"), (gbrow, "gbrow")):
            sch.op("sync", lambda e, d=dst, s=hbm[srcn]: e.dma_start(d[:, :], s[:, :]),
                   writes=(dst.name,), dma=True)
        sch.op("sync", lambda e: e.dma_start(rows8[0:1, 0:8], hbm["arow"][:, :]),
               writes=("rows8",), dma=True)
        sch.op("sync", lambda e: e.dma_start(rows8[0:1, 8:16], hbm["drow"][:, :]),
               writes=("rows8",), dma=True)

        sch.op("scalar", lambda e: e.activation(rows8[0:1, 16:24], rows8[0:1, 0:8],
                                                AF.Exp, scale=1.0),
               reads=("rows8",), writes=("rows8",))
        sch.op("vector", lambda e: e.tensor_scalar(rows8[0:1, 16:24], rows8[0:1, 16:24],
                                                   -1.0, None, OP.mult),
               reads=("rows8",), writes=("rows8",))
        sch.op("tensor", lambda e: e.matmul(p8[:, 0:16], cx.ones_f[0:1, 0:128],
                                            rows8[0:1, 8:24], start=True, stop=True),
               reads=("rows8", "ones_f"), writes=("p8",))
        sch.op("scalar", lambda e: e.copy(A_bc[:, :], p8[:, 8:16]),
               reads=("p8",), writes=("A_bc",))
        for h in range(8):
            sch.op("vector",
                   lambda e, h=h: e.tensor_scalar(idD[:, bass.ts(h, 128)],
                                                  cx.id128_f[:, :],
                                                  p8[:, h:h + 1], None, OP.mult),
                   reads=("p8", "id128_f"), writes=("idD",))
        for j in range(8):
            dst = gam_bc if j < 4 else bet_bc
            jj = j % 4
            sch.op("tensor",
                   lambda e, j=j: e.matmul(p8[:, :], cx.ones_f[0:1, 0:128],
                                           gbrow[0:1, bass.ts(j, 512)],
                                           start=True, stop=True),
                   reads=("gbrow_s", "ones_f"), writes=("p8",))
            sch.op("scalar",
                   lambda e, d=dst, jj=jj: e.copy(d[:, bass.ts(jj, 512)], p8[:, :]),
                   reads=("p8",), writes=(dst.name,))
        sch.op("gpsimd", lambda e: (e.memset(xconv[:, :], 0.0),
                                    e.memset(Bconv[:, :], 0.0),
                                    e.memset(Cconv[:, :], 0.0),
                                    e.memset(h_f[0][:, :], 0.0),
                                    e.memset(h_f[1][:, :], 0.0),
                                    e.memset(h_bf[0][:, :], 0.0),
                                    e.memset(h_bf[1][:, :], 0.0))[-1],
               writes=("xconv", "Bconv", "Cconv", "h_f0", "h_f1", "h_bf0", "h_bf1"))

        pending_yln = [None]

        def mk_yln(cp, yr, yrn, t0, cs):
            def emit():
                sch.op("vector", lambda e: e.reduce_sum(ycols[:, 0:1], yr[:, :],
                                                        axis=mybir.AxisListType.X),
                       reads=(yrn,), writes=("ycols",))
                sch.op("scalar", lambda e: e.activation(sq_y[:, :], yr[:, :],
                                                        AF.Square,
                                                        accum_out=ycols[:, 1:2]),
                       reads=(yrn,), writes=("sq_y", "ycols"))
                sch.op("vector", lambda e: e.tensor_scalar(ycols[:, 2:4], ycols[:, 0:2],
                                                           1.0 / DI, None, OP.mult),
                       reads=("ycols",), writes=("ycols",))
                sch.op("vector", lambda e: e.tensor_tensor(ycols[:, 4:5], ycols[:, 2:3],
                                                           ycols[:, 2:3], OP.mult),
                       reads=("ycols",), writes=("ycols",))
                sch.op("vector", lambda e: e.tensor_tensor(ycols[:, 5:6], ycols[:, 3:4],
                                                           ycols[:, 4:5], OP.subtract),
                       reads=("ycols",), writes=("ycols",))
                sch.op("scalar", lambda e: e.activation(ycols[:, 6:7], ycols[:, 5:6],
                                                        AF.Sqrt, bias=cx.cvals[:, 0:1]),
                       reads=("ycols", "cvals"), writes=("ycols",))
                sch.op("vector", lambda e: e.reciprocal(ycols[:, 7:8], ycols[:, 6:7]),
                       reads=("ycols",), writes=("ycols",))
                sch.op("vector", lambda e: e.tensor_scalar(yln_bf[:, :], yr[:, :],
                                                           ycols[:, 2:3],
                                                           ycols[:, 7:8],
                                                           OP.subtract, OP.mult),
                       reads=(yrn, "ycols"), writes=("yln_bf",))
                sch.op("vector", lambda e: e.tensor_tensor(scr2[:, :], yln_bf[:, :],
                                                           gam_bc[:, :], OP.mult),
                       reads=("yln_bf", "gam_bc"), writes=("scr2",))
                sch.op("vector", lambda e: e.tensor_tensor(yln_bf[:, :], scr2[:, :],
                                                           bet_bc[:, :], OP.add),
                       reads=("scr2", "bet_bc"), writes=("yln_bf",))
                for wv2 in range(8):
                    def ytp(e, wv2=wv2):
                        e.transpose(px[:, 0:128], yln_bf[:, bass.ts(2 * wv2, 128)],
                                    cx.id128_bf[:, :])
                        return e.transpose(px[:, 128:256],
                                           yln_bf[:, bass.ts(2 * wv2 + 1, 128)],
                                           cx.id128_bf[:, :])
                    sch.op("tensor", ytp, reads=("yln_bf", "id128_bf"), writes=("px",))
                    sch.op("scalar", lambda e, wv2=wv2: e.copy(
                        yln_c[:, :].rearrange("p (k t) -> p k t", k=16)
                        [:, 2 * wv2:2 * wv2 + 2, :], px[:, 0:256].rearrange(
                            "p (a t) -> p a t", a=2)),
                        reads=("px",), writes=("yln_c",))
                sch.op("sync", lambda e: e.dma_start(fm(cx.yln_h, 16, t0 + cs, 128),
                                                     t3(yln_c, 16)),
                       reads=("yln_c",), writes=("yln_h",), dma=True)
            return emit

        def upre(it2):
            t0p = it2 * TT
            sch.op("sync", lambda e, t0=t0p: e.dma_start(t3(u_t, 8),
                                                         fm(hbm["u"], 8, t0, TT)),
                   writes=("u_t",), dma=True)
            sch.op("scalar", lambda e: e.activation(sq_t[:, :], u_t[:, :], AF.Square),
                   reads=("u_t",), writes=("sq_t",))

        for it in range(ntile):
            t0 = it * TT
            if it == 0:
                upre(0)
            sch.mm_group(p8[0:1, 0:TT],
                         [(cx.ones_f[:, 0:1], u_t[:, bass.ts(k, TT)]) for k in range(8)],
                         reads=("u_t", "ones_f"), writes=("p8",))
            sch.mm_group(p8[0:1, TT:2 * TT],
                         [(cx.ones_f[:, 0:1], sq_t[:, bass.ts(k, TT)]) for k in range(8)],
                         reads=("sq_t", "ones_f"), writes=("p8",))
            sch.op("vector", lambda e: e.tensor_scalar(rows[0:1, 0:2 * TT],
                                                       p8[0:1, 0:2 * TT],
                                                       1.0 / D, None, OP.mult),
                   reads=("p8",), writes=("rows",))
            sch.op("vector", lambda e: e.tensor_tensor(rows[0:1, 2 * TT:3 * TT],
                                                       rows[0:1, 0:TT], rows[0:1, 0:TT],
                                                       OP.mult),
                   reads=("rows",), writes=("rows",))
            sch.op("vector", lambda e: e.tensor_tensor(rows[0:1, 2 * TT:3 * TT],
                                                       rows[0:1, TT:2 * TT],
                                                       rows[0:1, 2 * TT:3 * TT],
                                                       OP.subtract),
                   reads=("rows",), writes=("rows",))
            sch.op("scalar", lambda e: e.activation(rows[0:1, 3 * TT:4 * TT],
                                                    rows[0:1, 2 * TT:3 * TT],
                                                    AF.Sqrt, bias=cx.cvals[0:1, 0:1]),
                   reads=("rows", "cvals"), writes=("rows",))
            sch.op("vector", lambda e: e.reciprocal(rows[0:1, 3 * TT:4 * TT],
                                                    rows[0:1, 3 * TT:4 * TT]),
                   reads=("rows",), writes=("rows",))
            sch.op("vector", lambda e: e.tensor_tensor(rows[0:1, 4 * TT:5 * TT],
                                                       rows[0:1, 0:TT],
                                                       rows[0:1, 3 * TT:4 * TT],
                                                       OP.mult),
                   reads=("rows",), writes=("rows",))
            sch.op("tensor", lambda e: e.matmul(p8[:, 0:2 * TT], cx.ones_f[0:1, 0:128],
                                                rows[0:1, 3 * TT:5 * TT],
                                                start=True, stop=True),
                   reads=("rows", "ones_f"), writes=("p8",))
            sch.op("vector", lambda e: e.tensor_tensor(
                sq_t[:, :].rearrange("p (k t) -> p k t", k=8),
                u_t[:, :].rearrange("p (k t) -> p k t", k=8),
                p8[:, 0:TT].unsqueeze(1).broadcast_to([128, 8, TT]), OP.mult),
                reads=("u_t", "p8"), writes=("sq_t",))
            sch.op("vector", lambda e: e.tensor_tensor(
                lnu_t[:, :].rearrange("p (k t) -> p k t", k=8),
                sq_t[:, :].rearrange("p (k t) -> p k t", k=8),
                p8[:, TT:2 * TT].unsqueeze(1).broadcast_to([128, 8, TT]), OP.subtract),
                reads=("sq_t", "p8"), writes=("lnu_t",))
            sch.op("sync", lambda e, t0=t0: e.dma_start(fm(cx.lnu_h, 8, t0, TT),
                                                        t3(lnu_t, 8)),
                   reads=("lnu_t",), writes=("lnu_h",), dma=True)
            for m in range(16):
                pb, sn, bn = ((pxin, "pxin0", "pxin") if m % 2 == 0
                              else (p8, "p8x", "p8"))
                sch.mm_group(pb[:, 0:TT],
                             [(wx[:, k * DI + m * 128:k * DI + (m + 1) * 128],
                               lnu_t[:, bass.ts(k, TT)]) for k in range(8)],
                             reads=("wx", "lnu_t"), writes=(sn, bn))
                sch.op("scalar",
                       lambda e, m=m, pb=pb: e.copy(
                           xconv[:, m * (TT + 3) + 3:(m + 1) * (TT + 3)],
                           pb[:, 0:TT]),
                       reads=(sn,), writes=("xconv",))
            sch.mm_group(pdt[0:16, 0:TT],
                         [(wB[:, bass.ts(k, 16)], lnu_t[:, bass.ts(k, TT)])
                          for k in range(8)],
                         reads=("wB", "lnu_t"), writes=("pdt",))
            sch.op("scalar", lambda e: e.copy(Bconv[:, 3:3 + TT], pdt[0:16, 0:TT]),
                   reads=("pdt",), writes=("Bconv",))
            sch.mm_group(pdt[0:16, 256:256 + TT],
                         [(wC[:, bass.ts(k, 16)], lnu_t[:, bass.ts(k, TT)])
                          for k in range(8)],
                         reads=("wC", "lnu_t"), writes=("pdt",))
            sch.op("scalar", lambda e: e.copy(Cconv[:, 3:3 + TT], pdt[0:16, 256:256 + TT]),
                   reads=("pdt",), writes=("Cconv",))
            sch.mm_group(pdt[0:8, 0:TT],
                         [(wdt[:, bass.ts(k, 8)], lnu_t[:, bass.ts(k, TT)])
                          for k in range(8)],
                         reads=("wdt", "lnu_t"), writes=("pdt",))
            sch.op("vector", lambda e: e.tensor_scalar(dtf[:, 0:TT], pdt[0:8, 0:TT],
                                                       dtb[:, 0:1], None, OP.add),
                   reads=("pdt", "dtb"), writes=("dtf",))
            sch.op("vector", lambda e: e.tensor_scalar(dtf[:, TT:2 * TT],
                                                       dtf[:, 0:TT], -1.0, None,
                                                       OP.mult),
                   reads=("dtf",), writes=("dtf",))
            sch.op("vector", lambda e: e.tensor_tensor(dtf[:, 0:TT], dtf[:, 0:TT],
                                                       dtf[:, TT:2 * TT], OP.min),
                   reads=("dtf",), writes=("dtf",))
            sch.op("scalar", lambda e: e.activation(dtf[:, TT:2 * TT], dtf[:, 0:TT],
                                                    AF.Exp),
                   reads=("dtf",), writes=("dtf",))
            sch.op("scalar", lambda e: e.activation(dtf[:, 2 * TT:3 * TT],
                                                    dtf[:, TT:2 * TT], AF.Ln, bias=1.0),
                   reads=("dtf",), writes=("dtf",))
            sch.op("vector", lambda e: e.tensor_scalar(dtf[:, 3 * TT:4 * TT],
                                                       pdt[0:8, 0:TT], dtb[:, 0:1], 0.0,
                                                       OP.add, OP.max),
                   reads=("pdt", "dtb"), writes=("dtf",))
            sch.op("vector", lambda e: e.tensor_tensor(dtf[:, 4 * TT:5 * TT],
                                                       dtf[:, 3 * TT:4 * TT],
                                                       dtf[:, 2 * TT:3 * TT], OP.add),
                   reads=("dtf",), writes=("dtf",))
            for m in range(16):
                base = m * (TT + 3)
                sch.op("vector", lambda e, base=base, m=m: e.tensor_scalar(
                    cscr[:, 0:TT], xconv[:, base:base + TT], convx[:, 4 * m:4 * m + 1],
                    None, OP.mult),
                    reads=("xconv", "convx_s"), writes=("cscr",))
                for j2, (o, i) in enumerate(((TT, 0), (0, TT), (TT, 0))):
                    sch.op("vector", lambda e, base=base, m=m, j2=j2, o=o, i=i:
                           e.scalar_tensor_tensor(
                               cscr[:, o:o + TT], xconv[:, base + j2 + 1:base + j2 + 1 + TT],
                               convx[:, 4 * m + j2 + 1:4 * m + j2 + 2], cscr[:, i:i + TT],
                               OP.mult, OP.add),
                           reads=("xconv", "convx_s", "cscr"), writes=("cscr",))
                sch.op("scalar", lambda e, m=m: e.activation(
                    xs_t[:, bass.ts(m, TT)], cscr[:, TT:2 * TT], AF.Silu,
                    bias=convxb[:, m:m + 1]),
                    reads=("cscr", "convxb_s"), writes=("xs_t",))
            for (cin, cwt, cbt, sname) in ((Bconv, convB, convBb, "B"),
                                           (Cconv, convC, convCb, "C")):
                sch.op("vector", lambda e, cin=cin, cwt=cwt: e.tensor_scalar(
                    cscr[0:16, 0:TT], cin[:, 0:TT], cwt[:, 0:1], None, OP.mult),
                    reads=(cin.name, cwt.name), writes=("cscr",))
                for j2, (o, i) in enumerate(((TT, 0), (0, TT), (TT, 0))):
                    sch.op("vector", lambda e, cin=cin, cwt=cwt, j2=j2, o=o, i=i:
                           e.scalar_tensor_tensor(
                               cscr[0:16, o:o + TT], cin[:, j2 + 1:j2 + 1 + TT],
                               cwt[:, j2 + 1:j2 + 2], cscr[0:16, i:i + TT],
                               OP.mult, OP.add),
                           reads=(cin.name, cwt.name, "cscr"), writes=("cscr",))
                if sname == "B":
                    sch.op("scalar", lambda e, cbt=cbt: e.activation(
                        Bs_bf[:, :], cscr[0:16, TT:2 * TT], AF.Silu, bias=cbt[:, 0:1]),
                        reads=("cscr", cbt.name), writes=("Bs_bf",))
                else:
                    sch.op("scalar", lambda e, cbt=cbt: e.activation(
                        Cs_f[:, :], cscr[0:16, TT:2 * TT], AF.Silu, bias=cbt[:, 0:1]),
                        reads=("cscr", cbt.name), writes=("Cs_f",))
                    sch.op("scalar", lambda e: e.copy(Cs_bf[:, :], Cs_f[:, :]),
                           reads=("Cs_f",), writes=("Cs_bf",))
            sch.op("gpsimd", lambda e: (
                e.tensor_copy(xconv[:, :].rearrange("p (m t) -> p m t", m=16)[:, :, 0:3],
                              xconv[:, :].rearrange("p (m t) -> p m t", m=16)[:, :, TT:TT + 3]),
                e.tensor_copy(Bconv[:, 0:3], Bconv[:, TT:TT + 3]),
                e.tensor_copy(Cconv[:, 0:3], Cconv[:, TT:TT + 3]))[-1],
                reads=("xconv", "Bconv", "Cconv"),
                writes=("xconv", "Bconv", "Cconv"))

            def dtpart(c):
                cs = c * 128
                cp = c % 2
                dtc, dtcn = dt_tm[cp], "dt_tm%d" % cp
                dAc, dAcn = dtA_tm[cp], "dtA_tm%d" % cp
                nsc, nscn = ns_tm[cp], "ns_tm%d" % cp
                prc, prcn = prod[cp], "prod%d" % cp
                sch.op("tensor", lambda e, cs=cs: e.transpose(
                    pdt[:, 256:264], dtf[:, 4 * TT + cs:4 * TT + cs + 128], cx.id8_f[:, :]),
                    reads=("dtf", "id8_f"), writes=("pdt",))
                sch.op("scalar", lambda e, dtc=dtc: e.copy(dtc[:, :], pdt[:, 256:264]),
                       reads=("pdt",), writes=(dtcn,))
                sch.op("vector", lambda e, dAc=dAc, dtc=dtc: e.tensor_tensor(
                    dAc[:, :], dtc[:, :], A_bc[:, :], OP.mult),
                    reads=(dtcn, "A_bc"), writes=(dAcn,))
                sch.op("tensor", lambda e, dAc=dAc: e.matmul(pdt[:, 264:272], cx.U_f[:, :],
                                                             dAc[:, :], start=True,
                                                             stop=True),
                       reads=("U_f", dAcn), writes=("pdt",))
                sch.op("vector", lambda e, nsc=nsc: e.tensor_scalar(
                    nsc[:, :], pdt[:, 264:272], -1.0, None, OP.mult),
                    reads=("pdt",), writes=(nscn,))
                sch.op("vector", lambda e, prc=prc, dAc=dAc: e.tensor_tensor(
                    prc[:, :].rearrange("p (h t) -> p h t", h=8),
                    dAc[:, :].unsqueeze(2).broadcast_to([128, 8, 128]),
                    cx.U_f[:, :].unsqueeze(1).broadcast_to([128, 8, 128]), OP.mult),
                    reads=(dAcn, "U_f"), writes=(prcn,))

            def phpart(c):
                cs = c * 128
                cp = c % 2
                eac, eacn = e_all[cp], "e_all%d" % cp
                prc, prcn = prod[cp], "prod%d" % cp
                src_, srcn = srow[cp], "srow%d" % cp
                Gc, Gcn = G_sb[cp], "G_sb%d" % cp
                Cac, Cacn = Csc_all[cp], "Csc_all%d" % cp
                sch.mm_group(ph[0:1, 0:512],
                             [(cx.ones_f[:, 0:1], prc[:, 0:512])],
                             reads=(prcn, "ones_f"), writes=("ph",))
                sch.mm_group(ph[0:1, 512:1024],
                             [(cx.ones_f[:, 0:1], prc[:, 512:1024])],
                             reads=(prcn, "ones_f"), writes=("ph",))
                sch.op("scalar", lambda e, src_=src_: e.copy(src_[0:1, 0:1024],
                                                             ph[0:1, 0:1024]),
                       reads=("ph",), writes=(srcn,))
                sch.op("scalar", lambda e, src_=src_: e.activation(
                    src_[0:1, 1024:2048], ph[0:1, 0:1024], AF.Exp),
                    reads=("ph",), writes=(srcn,))
                srow_last = src_[0:1, 0:1024].rearrange(
                    "p (h t) -> p h t", h=8)[:, :, 127:128]
                sch.op("tensor", lambda e, r=srow_last: e.matmul(
                    pdt[0:16, 272:280], cx.ones_f[0:1, 0:16], r,
                    start=True, stop=True),
                    reads=(srcn, "ones_f"), writes=("pdt",))
                sch.op("scalar", lambda e, eac=eac: e.activation(
                    eac[:, :], pdt[0:16, 272:280], AF.Exp),
                    reads=("pdt",), writes=(eacn,))
                sch.mm_group(pgs[:, 0:128],
                             [(Bs_bf[:, cs:cs + 128], Cs_bf[:, cs:cs + 128])],
                             reads=("Bs_bf", "Cs_bf"), writes=("pgs",))
                sch.op("scalar", lambda e, Gc=Gc: e.copy(Gc[:, :], pgs[:, 0:128]),
                       reads=("pgs",), writes=(Gcn,))
                def pebmm(e, src_=src_):
                    e.matmul(ph[0:16, 0:512], cx.ones_f[0:1, 0:16],
                             src_[0:1, 1024:1536], start=True, stop=True)
                    return e.matmul(ph[0:16, 512:1024], cx.ones_f[0:1, 0:16],
                                    src_[0:1, 1536:2048], start=True, stop=True)
                sch.op("tensor", pebmm, reads=(srcn, "ones_f"), writes=("ph",))
                sch.op("vector", lambda e, cs=cs, Cac=Cac: e.tensor_tensor(
                    Cac[:, :].rearrange("p (h t) -> p h t", h=8),
                    Cs_f[:, cs:cs + 128].unsqueeze(1).broadcast_to([16, 8, 128]),
                    ph[0:16, 0:1024].rearrange("p (h t) -> p h t", h=8), OP.mult),
                    reads=("Cs_f", "ph"), writes=(Cacn,))
                sch.op("tensor", lambda e, cs=cs: e.transpose(
                    px[:, 256:272], Bs_bf[:, cs:cs + 128], cx.id16_bf[:, :]),
                    reads=("Bs_bf", "id16_bf"), writes=("px",))
                sch.op("scalar", lambda e: e.copy(B_tm[:, :], px[:, 256:272]),
                       reads=("px",), writes=("B_tm",))

            def headswork(c):
                cs = c * 128
                cp = c % 2
                dtc, dtcn = dt_tm[cp], "dt_tm%d" % cp
                nsc, nscn = ns_tm[cp], "ns_tm%d" % cp
                eac, eacn = e_all[cp], "e_all%d" % cp
                src_, srcn = srow[cp], "srow%d" % cp
                yr, yrn = y_row[cp], "y_row%d" % cp
                Gc, Gcn = G_sb[cp], "G_sb%d" % cp
                Cac, Cacn = Csc_all[cp], "Csc_all%d" % cp
                def spair(pp):
                    gsb, gsn = (pgs, "pgs") if pp % 2 == 0 else (p8, "p8")

                    def smm(e, pp=pp, gsb=gsb, src_=src_):
                        e.matmul(gsb[:, 0:256], cx.ones_f[0:1, 0:128],
                                 src_[0:1, pp * 256:(pp + 1) * 256], start=True,
                                 stop=False)
                        return e.matmul(gsb[:, 0:256], cx.id128_f[:, :],
                                        cx.maskS2_f[:, :], start=False, stop=True)
                    sch.op("tensor", smm, reads=(srcn, "ones_f", "id128_f", "maskS2_f"),
                           writes=(gsn,) if gsn == "pgs" else (gsn, "p8x"))

                def lamop(h):
                    hp = h % 2
                    gsb, gsn = (pgs, "pgs") if (h // 2) % 2 == 0 else (p8, "p8")
                    lm, lmn = lam[hp], "lam%d" % hp
                    sch.op("scalar", lambda e, h=h, gsb=gsb, lm=lm, nsc=nsc: e.activation(
                        lm[:, :], gsb[:, 128 * (h % 2):128 * (h % 2) + 128], AF.Exp,
                        bias=nsc[:, h:h + 1]),
                        reads=(gsn, nscn), writes=(lmn,))

                def front(h):
                    hp = h % 2
                    Xc, Xcn = X_tm[hp], "X_tm%d" % hp

                    def xtp(e, h=h, cs=cs):
                        e.transpose(px[:, 0:128],
                                    xs_t[:, (2 * h) * TT + cs:(2 * h) * TT + cs + 128],
                                    cx.id128_bf[:, :])
                        return e.transpose(px[:, 128:256],
                                           xs_t[:, (2 * h + 1) * TT + cs:(2 * h + 1) * TT + cs + 128],
                                           cx.id128_bf[:, :])
                    sch.op("tensor", xtp, reads=("xs_t", "id128_bf"), writes=("px",))
                    sch.op("scalar", lambda e, Xc=Xc: e.copy(Xc[:, :], px[:, 0:256]),
                           reads=("px",), writes=(Xcn,))

                def back(h):
                    hp = h % 2
                    wv = h // 4
                    hbuf = h_bf[wv]
                    Xc, Xcn = X_tm[hp], "X_tm%d" % hp
                    ymb, ymn = (pym, "pym") if hp == 0 else (pxin, "pxin")
                    lm, lmn = lam[hp], "lam%d" % hp
                    Mc, Mcn = Mm[hp], "Mm%d" % hp
                    wc, wcn = wcol[hp], "wcol%d" % hp
                    Bww, Bwn = Bw[hp], "Bw%d" % hp
                    sch.op("vector", lambda e, h=h, Mc=Mc, lm=lm, dtc=dtc, Gc=Gc:
                           e.scalar_tensor_tensor(
                               Mc[:, :], Gc[:, :], dtc[:, h:h + 1], lm[:, :],
                               OP.mult, OP.mult),
                           reads=(Gcn, dtcn, lmn), writes=(Mcn,))

                    def ymm(e, h=h, hbuf=hbuf, ymb=ymb, Mc=Mc, Xc=Xc, Cac=Cac):
                        e.matmul(ymb[:, 0:256], Mc[:, :], Xc[:, :], start=True, stop=False)
                        e.matmul(ymb[:, 0:256], idD[:, bass.ts(h, 128)], Xc[:, :],
                                 start=False, stop=False)
                        return e.matmul(ymb[:, 0:256], Cac[:, bass.ts(h, 128)],
                                        hbuf[:, bass.ts(h % 4, 256)],
                                        start=False, stop=True)
                    sch.op("tensor", ymm,
                           reads=(Mcn, Xcn, "idD", Cacn, hbuf.name),
                           writes=(ymn,) if hp == 0 else (ymn, "pxin0"))
                    sch.op("scalar", lambda e, h=h, ymb=ymb, yr=yr: e.copy(
                        yr[:, bass.ts(h, 256)], ymb[:, 0:256]),
                        reads=(ymn,), writes=(yrn,))
                    sch.op("vector", lambda e, h=h, wc=wc, lm=lm, dtc=dtc: e.tensor_scalar(
                        wc[:, :], lm[:, 127:128], dtc[:, h:h + 1], None, OP.mult),
                        reads=(lmn, dtcn), writes=(wcn,))
                    sch.op("vector", lambda e, Bww=Bww, wc=wc: e.tensor_scalar(
                        Bww[:, :], B_tm[:, :], wc[:, 0:1], None, OP.mult),
                        reads=("B_tm", wcn), writes=(Bwn,))
                    sch.mm_group(ph[0:16, bass.ts(h % 4, 256)],
                                 [(Bww[:, :], Xc[:, :])],
                                 reads=(Bwn, Xcn), writes=("ph",))
                    if h % 4 == 3:
                        hf = h_f[wv]
                        hb = h_bf[wv]
                        sch.op("vector", lambda e, wv=wv, hf=hf, eac=eac: e.tensor_tensor(
                            h_tmp[:, :].rearrange("p (g t) -> p g t", g=4),
                            hf[:, :].rearrange("p (g t) -> p g t", g=4),
                            eac[:, 4 * wv:4 * wv + 4].unsqueeze(2).broadcast_to([16, 4, 256]),
                            OP.mult),
                            reads=(hf.name, eacn), writes=("h_tmp",))
                        sch.op("vector", lambda e, hf=hf: e.tensor_tensor(
                            hf[:, :], h_tmp[:, :], ph[0:16, 0:1024], OP.add),
                            reads=("h_tmp", "ph"), writes=(hf.name,))
                        sch.op("scalar", lambda e, hf=hf, hb=hb: e.copy(hb[:, :], hf[:, :]),
                               reads=(hf.name,), writes=(hb.name,))

                spair(0)
                lamop(0)
                lamop(1)
                front(0)
                front(1)
                for h in range(8):
                    back(h)
                    if h % 2 == 1 and h < 7:
                        spair((h + 1) // 2)
                        lamop(h + 1)
                        lamop(h + 2)
                        front(h + 1)
                        front(h + 2)


            for c in range(nch):
                if c == 0:
                    dtpart(0)
                phpart(c)
                if c + 1 < nch:
                    dtpart(c + 1)
                cp = c % 2
                yr, yrn = y_row[cp], "y_row%d" % cp
                headswork(c)
                if c == 0 and it + 1 < ntile:
                    upre(it + 1)
                if pending_yln[0] is not None:
                    pending_yln[0]()
                pending_yln[0] = mk_yln(cp, yr, yrn, t0, c * 128)
            if cx.dev and it == ntile - 1:
                for nm, buf in (("d_xs", xs_t), ("d_dtf", dtf), ("d_yrow", y_row[1]),
                                ("d_lam", lam[1]), ("d_Mm", Mm[1]), ("d_Xtm", X_tm[1]),
                                ("d_srow", srow[1]), ("d_dttm", dt_tm[1]),
                                ("d_Bs", Bs_bf), ("d_Cs", Cs_f),
                                ("d_eall", e_all[1])):
                    sch.op("sync", lambda e, nm=nm, buf=buf: e.dma_start(
                        cx.dbg[nm][:, :], buf[:, :]), reads=(buf.name,), dma=True)
                sch.op("sync", lambda e: e.dma_start(cx.dbg["d_hf"][:, 0:1024],
                                                     h_f[0][:, :]),
                       reads=("h_f0",), dma=True)
                sch.op("sync", lambda e: e.dma_start(cx.dbg["d_hf"][:, 1024:2048],
                                                     h_f[1][:, :]),
                       reads=("h_f1",), dma=True)
        if pending_yln[0] is not None:
            pending_yln[0]()
            pending_yln[0] = None


def _phase_MG(cx, nc, sch):
    TT = TS_MA
    ntile = T // TT
    hbm = cx.hbm
    es = ExitStack()
    sb = lambda n, s, d: es.enter_context(nc.sbuf_tensor(n, s, d))
    ps = lambda n, s, d: es.enter_context(nc.psum_tensor(n, s, d))
    wg = sb("wg", [128, 16 * DI], BF16)
    bg = sb("bg_s", [128, 16], F32)
    yln2 = [sb("yln_t0", [128, 16 * TT], BF16), sb("yln_t1", [128, 16 * TT], BF16)]
    gbf2 = [sb("g_bf0", [128, 16 * TT], BF16), sb("g_bf1", [128, 16 * TT], BF16)]
    pga = [ps("pga0", [128, 256], F32), ps("pga1", [128, 256], F32)]
    with es:
        s = hbm["wgT"][:, :].rearrange("(k p) o -> p k o", p=128)
        sch.op("gpsimd", lambda e, d=t3(wg, 16): e.dma_start(d, s),
               writes=("wg",), dma=True)
        sch.op("sync", lambda e: e.dma_start(bg[:, :], hbm["bg"][:, :]),
               writes=("bg_s",), dma=True)
        def yload(it2):
            t0 = it2 * TT
            yb = yln2[it2 % 2]
            sch.op("sync", lambda e, t0=t0, yb=yb: e.dma_start(
                t3(yb, 16), fm(cx.yln_h, 16, t0, TT)),
                reads=("yln_h",), writes=("yln_t%d" % (it2 % 2),), dma=True)

        yload(0)
        for it in range(ntile):
            t0 = it * TT
            yln_t, yn = yln2[it % 2], "yln_t%d" % (it % 2)
            g_bf, gn = gbf2[it % 2], "g_bf%d" % (it % 2)
            if it + 1 < ntile:
                yload(it + 1)
            for m in range(16):
                pg = pga[m % 2]
                pgn = "pga%d" % (m % 2)
                sch.mm_group(pg[:, 0:TT],
                             [(wg[:, k * DI + m * 128:k * DI + (m + 1) * 128],
                               yln_t[:, bass.ts(k, TT)]) for k in range(16)],
                             reads=("wg", yn), writes=(pgn,))
                sch.op("scalar", lambda e, m=m, pg=pg, g_bf=g_bf: e.activation(
                    g_bf[:, bass.ts(m, TT)], pg[:, 0:TT], AF.Sigmoid,
                    bias=bg[:, m:m + 1]),
                    reads=(pgn, "bg_s"), writes=(gn,))
            for j in range(8):
                sch.op("vector", lambda e, j=j, yln_t=yln_t, g_bf=g_bf: e.tensor_tensor(
                    g_bf[:, bass.ts(j, 512)], g_bf[:, bass.ts(j, 512)],
                    yln_t[:, bass.ts(j, 512)], OP.mult),
                    reads=(gn, yn), writes=(gn,))
            sch.op("sync", lambda e, t0=t0, g_bf=g_bf: e.dma_start(
                fm(cx.yg_h, 16, t0, TT), t3(g_bf, 16)),
                reads=(gn,), writes=("yg_h",), dma=True)

def _phase_MB(cx, nc, sch):
    TT = TS_MB
    ntile = T // TT
    hbm = cx.hbm
    es = ExitStack()
    sb = lambda n, s, d: es.enter_context(nc.sbuf_tensor(n, s, d))
    ps = lambda n, s, d: es.enter_context(nc.psum_tensor(n, s, d))

    wz0 = sb("wz0", [128, 8 * DI], BF16)
    wx0 = sb("wx0", [128, 8 * DI], BF16)
    wo = sb("wo", [128, 32 * D], BF16)
    lnu2 = [sb("lnu_tb0", [128, 8 * TT], BF16), sb("lnu_tb1", [128, 8 * TT], BF16)]
    yg2 = [sb("yg_tb0", [128, 16 * TT], BF16), sb("yg_tb1", [128, 16 * TT], BF16)]
    ub2 = [sb("u_tb0", [128, 8 * TT], F32), sb("u_tb1", [128, 8 * TT], F32)]
    zx = sb("zx", [128, 512], BF16)
    skip_t = sb("skip_t", [128, 16 * TT], BF16)
    x12 = [sb("x1_t0", [128, 8 * TT], F32), sb("x1_t1", [128, 8 * TT], F32)]
    pz = ps("pz", [128, 256], F32)
    px0 = ps("px0", [128, 256], F32)
    po = [ps("po0", [128, 256], F32), ps("po1", [128, 256], F32)]

    with es:
        for dst, srcn, kch, width in ((wz0, "wz0T", 8, DI), (wx0, "wx0T", 8, DI),
                                      (wo, "woT", 32, D)):
            s = hbm[srcn][:, :].rearrange("(k p) o -> p k o", p=128)
            sch.op("gpsimd", lambda e, d=t3(dst, kch), s=s: e.dma_start(d, s),
                   writes=(dst.name,), dma=True)
        def mbload(it2):
            t0 = it2 * TT
            p = it2 % 2
            sch.op("sync", lambda e, t0=t0, d=lnu2[p]: e.dma_start(
                t3(d, 8), fm(cx.lnu_h, 8, t0, TT)),
                reads=("lnu_h",), writes=("lnu_tb%d" % p,), dma=True)
            sch.op("sync", lambda e, t0=t0, d=yg2[p]: e.dma_start(
                t3(d, 16), fm(cx.yg_h, 16, t0, TT)),
                reads=("yg_h",), writes=("yg_tb%d" % p,), dma=True)
            sch.op("sync", lambda e, t0=t0, d=ub2[p]: e.dma_start(
                t3(d, 8), fm(hbm["u"], 8, t0, TT)),
                writes=("u_tb%d" % p,), dma=True)

        mbload(0)
        for it in range(ntile):
            t0 = it * TT
            p = it % 2
            lnu_t, lnn = lnu2[p], "lnu_tb%d" % p
            yg_t, ygn = yg2[p], "yg_tb%d" % p
            u_t, un = ub2[p], "u_tb%d" % p
            x1_t, xn = x12[p], "x1_t%d" % p
            if it + 1 < ntile:
                mbload(it + 1)
            for m in range(16):
                sch.mm_group(pz[:, 0:TT],
                             [(wz0[:, k * DI + m * 128:k * DI + (m + 1) * 128],
                               lnu_t[:, bass.ts(k, TT)]) for k in range(8)],
                             reads=("wz0", lnn), writes=("pz",))
                sch.op("scalar", lambda e: e.activation(zx[:, 0:256], pz[:, 0:TT],
                                                        AF.Silu),
                       reads=("pz",), writes=("zx",))
                sch.mm_group(px0[:, 0:TT],
                             [(wx0[:, k * DI + m * 128:k * DI + (m + 1) * 128],
                               lnu_t[:, bass.ts(k, TT)]) for k in range(8)],
                             reads=("wx0", lnn), writes=("px0",))
                sch.op("scalar", lambda e: e.copy(zx[:, 256:512], px0[:, 0:TT]),
                       reads=("px0",), writes=("zx",))
                sch.op("vector", lambda e, m=m: e.tensor_tensor(
                    skip_t[:, bass.ts(m, TT)], zx[:, 0:256], zx[:, 256:512], OP.mult),
                    reads=("zx",), writes=("skip_t",))
            for m2 in range(8):
                p = po[m2 % 2]
                pn = "po%d" % (m2 % 2)
                pairs = [(wo[:, k * D + m2 * 128:k * D + (m2 + 1) * 128],
                          skip_t[:, bass.ts(k, TT)]) for k in range(16)]
                pairs += [(wo[:, (16 + k) * D + m2 * 128:(16 + k) * D + (m2 + 1) * 128],
                           yg_t[:, bass.ts(k, TT)]) for k in range(16)]
                sch.mm_group(p[:, 0:TT], pairs,
                             reads=("wo", "skip_t", ygn), writes=(pn,))
                sch.op("vector", lambda e, m2=m2, p=p, x1_t=x1_t, u_t=u_t:
                       e.tensor_tensor(
                           x1_t[:, bass.ts(m2, TT)], u_t[:, bass.ts(m2, TT)],
                           p[:, 0:TT], OP.add),
                       reads=(un, pn), writes=(xn,))
            sch.op("sync", lambda e, t0=t0, x1_t=x1_t: e.dma_start(
                fm(cx.x1_h, 8, t0, TT), t3(x1_t, 8)),
                reads=(xn,), writes=("x1_h",), dma=True)


def _phase_F(cx, nc, sch):
    hbm = cx.hbm
    es = ExitStack()
    sb = lambda n, s, d: es.enter_context(nc.sbuf_tensor(n, s, d))
    ps = lambda n, s, d: es.enter_context(nc.psum_tensor(n, s, d))
    NCH = T // 128

    K_fm = sb("K_fm", [128, 8 * T], BF16)
    V_tm = sb("V_tm", [128, NCH * (NH * 65)], BF16)
    xf_t = [sb("xf_t0", [128, 8 * 128], F32), sb("xf_t1", [128, 8 * 128], F32)]
    sqf = sb("sqf", [128, 8 * 128], F32)
    y0_t = sb("y0_t", [128, 8 * 128], F32)
    rowsF = sb("rowsF", [1, 5 * 128], F32)
    ptr = ps("ptr", [128, 512], BF16)
    fa = ExitStack()
    fsb = lambda n, s, d: fa.enter_context(nc.sbuf_tensor(n, s, d))
    fps = lambda n, s, d: fa.enter_context(nc.psum_tensor(n, s, d))
    kvd = fsb("kvd", [128, 8 * LAT], BF16)
    qd = fsb("qd", [128, 8 * LAT], BF16)
    ku = fsb("ku", [128, 2 * 512], BF16)
    qu = fsb("qu", [128, 2 * 512], BF16)
    vu = fsb("vu", [128, 2 * D], BF16)
    rq = fsb("rq", [128, 2 * 512], BF16)
    rqp = fsb("rqp", [128, 2 * 512], BF16)
    rk = fsb("rk", [128, 8 * 512], BF16)
    rkp = fsb("rkp", [128, 8 * 512], BF16)
    ln1 = fsb("ln1", [128, 8], F32)
    posr = fsb("posr", [1, 128], F32)
    invf = fsb("invf", [1, 16], F32)
    a_t = [fsb("a_t0", [128, 8 * 128], BF16), fsb("a_t1", [128, 8 * 128], BF16)]
    kvl2 = [fsb("kvl0", [128, 2 * 128], BF16), fsb("kvl1", [128, 2 * 128], BF16)]
    ql2 = [fsb("ql0", [128, 2 * 128], BF16), fsb("ql1", [128, 2 * 128], BF16)]
    fscr = fsb("fscr", [128, 4 * 16], F32)
    fqi = fsb("fqi", [128, 16], I32)
    cosr = fsb("cosr", [128, 512], F32)
    sinr = fsb("sinr", [128, 512], F32)
    ro1 = fsb("ro1", [128, 512], F32)
    ro2 = fsb("ro2", [128, 512], F32)
    kasm = fsb("kasm", [128, 8 * 128], BF16)
    qasm = fsb("qasm", [128, 8 * 128], BF16)
    p8f = [fps("p8f0", [128, 512], F32), fps("p8f1", [128, 512], F32)]
    pno = fps("pno", [128, 512], F32)
    prot = fps("prot", [128, 512], F32)
    protp = fps("protp", [128, 512], F32)
    pvv = fps("pvv", [128, 1024], F32)

    with es:
        blkFA = nc.Block()
        blkFA.__enter__()
        for dst, srcn, kch, width in ((kvd, "kvdT", 8, LAT), (qd, "qdT", 8, LAT),
                                      (ku, "kuT", 2, 512), (qu, "quT", 2, 512),
                                      (vu, "vuT", 2, D), (rq, "rqT", 2, 512),
                                      (rqp, "rqTp", 2, 512), (rk, "rkT", 8, 512),
                                      (rkp, "rkTp", 8, 512)):
            s = hbm[srcn][:, :].rearrange("(k p) o -> p k o", p=128)
            sch.op("gpsimd", lambda e, d=t3(dst, kch), s=s: e.dma_start(d, s),
                   writes=(dst.name,), dma=True)
        sch.op("sync", lambda e: e.dma_start(ln1[:, :], hbm["ln1c"][:, :]),
               writes=("ln1",), dma=True)
        def prep(g):
            g.iota(invf[:, :], [[1, 16]], base=0, channel_multiplier=0,
                   allow_small_or_imprecise_dtypes=True)
            return g.memset(
                V_tm[:, :].rearrange("p (c h e) -> p c h e", c=NCH, h=NH)[:, :, :, 64:65],
                1.0)
        sch.op("gpsimd", prep, writes=("invf", "V_tm"))
        sch.op("scalar", lambda e: e.activation(invf[:, :], invf[:, :], AF.Exp,
                                                scale=-math.log(10000.0) / 16.0),
               reads=("invf",), writes=("invf",))

        def ln_pass(xin, xname, wcolumn, wname, outbuf, outname, pbank=None,
                    pbn=None):
            if pbank is None:
                pbank, pbn = p8f[0], "p8f0"
            sch.op("scalar", lambda e: e.activation(sqf[:, :], xin[:, :], AF.Square),
                   reads=(xname,), writes=("sqf",))
            sch.mm_group(pbank[0:1, 0:128],
                         [(cx.ones_f[:, 0:1], xin[:, bass.ts(k, 128)]) for k in range(8)],
                         reads=(xname, "ones_f"), writes=(pbn,))
            sch.mm_group(pbank[0:1, 128:256],
                         [(cx.ones_f[:, 0:1], sqf[:, bass.ts(k, 128)]) for k in range(8)],
                         reads=("sqf", "ones_f"), writes=(pbn,))
            sch.op("vector", lambda e, pbank=pbank: e.tensor_scalar(rowsF[0:1, 0:256], pbank[0:1, 0:256],
                                                       1.0 / D, None, OP.mult),
                   reads=(pbn,), writes=("rowsF",))
            sch.op("vector", lambda e: e.tensor_tensor(rowsF[0:1, 256:384],
                                                       rowsF[0:1, 0:128],
                                                       rowsF[0:1, 0:128], OP.mult),
                   reads=("rowsF",), writes=("rowsF",))
            sch.op("vector", lambda e: e.tensor_tensor(rowsF[0:1, 256:384],
                                                       rowsF[0:1, 128:256],
                                                       rowsF[0:1, 256:384], OP.subtract),
                   reads=("rowsF",), writes=("rowsF",))
            sch.op("scalar", lambda e: e.activation(rowsF[0:1, 384:512],
                                                    rowsF[0:1, 256:384], AF.Sqrt,
                                                    bias=cx.cvals[0:1, 0:1]),
                   reads=("rowsF", "cvals"), writes=("rowsF",))
            sch.op("vector", lambda e: e.reciprocal(rowsF[0:1, 384:512],
                                                    rowsF[0:1, 384:512]),
                   reads=("rowsF",), writes=("rowsF",))
            sch.op("vector", lambda e: e.tensor_tensor(rowsF[0:1, 512:640],
                                                       rowsF[0:1, 0:128],
                                                       rowsF[0:1, 384:512], OP.mult),
                   reads=("rowsF",), writes=("rowsF",))
            sch.op("tensor", lambda e: e.matmul(pbank[:, 256:512], cx.ones_f[0:1, 0:128],
                                                rowsF[0:1, 384:640], start=True,
                                                stop=True),
                   reads=("rowsF", "ones_f"), writes=(pbn,))
            for k in range(8):
                sch.op("vector", lambda e, k=k, pbank=pbank: e.tensor_tensor(
                    sqf[:, bass.ts(k, 128)], xin[:, bass.ts(k, 128)], pbank[:, 256:384],
                    OP.mult),
                    reads=(xname, pbn), writes=("sqf",))
                if wcolumn is None:
                    sch.op("vector", lambda e, k=k, pbank=pbank: e.tensor_tensor(
                        outbuf[:, bass.ts(k, 128)], sqf[:, bass.ts(k, 128)],
                        pbank[:, 384:512], OP.subtract),
                        reads=("sqf", pbn), writes=(outname,))
                else:
                    sch.op("vector", lambda e, k=k, pbank=pbank: e.tensor_tensor(
                        sqf[:, bass.ts(k, 128)], sqf[:, bass.ts(k, 128)],
                        pbank[:, 384:512], OP.subtract),
                        reads=("sqf", pbn), writes=("sqf",))
                    sch.op("scalar", lambda e, k=k: e.mul(
                        outbuf[:, bass.ts(k, 128)], sqf[:, bass.ts(k, 128)],
                        wcolumn[:, k:k + 1]),
                        reads=("sqf", wname), writes=(outname,))

        def ln_work(ch):
            t0 = ch * 128
            p = ch % 2
            xb, xn = xf_t[p], "xf_t%d" % p
            ab, an = a_t[p], "a_t%d" % p
            pb, pn = p8f[p], "p8f%d" % p
            sch.op("sync", lambda e, t0=t0, xb=xb: e.dma_start(
                t3(xb, 8), fm(cx.x1_h, 8, t0, 128)),
                reads=("x1_h",), writes=(xn,), dma=True)
            ln_pass(xb, xn, None, None, y0_t, "y0_t", pb, pn)
            sch.op("sync", lambda e, t0=t0: e.dma_start(fm(cx.y0_h, 8, t0, 128),
                                                        t3(y0_t, 8)),
                   reads=("y0_t",), writes=("y0_h",), dma=True)
            ln_pass(y0_t, "y0_t", ln1, "ln1", ab, an, pb, pn)

        def lat_work(ch):
            p = ch % 2
            ab, an = a_t[p], "a_t%d" % p
            pb, pn = p8f[p], "p8f%d" % p
            for (lat, latn, wt, wtn) in ((kvl2[p], "kvl%d" % p, kvd, "kvd"),
                                         (ql2[p], "ql%d" % p, qd, "qd")):
                for m in range(2):
                    sch.mm_group(pb[:, bass.ts(m, 128)],
                                 [(wt[:, k * LAT + m * 128:k * LAT + (m + 1) * 128],
                                   ab[:, bass.ts(k, 128)]) for k in range(8)],
                                 reads=(wtn, an), writes=(pn,))
                sch.op("scalar", lambda e, lat=lat, pb=pb: e.copy(lat[:, :],
                                                                  pb[:, 0:256]),
                       reads=(pn,), writes=(latn,))

        ln_work(0)
        lat_work(0)
        for ch in range(NCH):
            if ch + 1 < NCH:
                ln_work(ch + 1)
            t0 = ch * 128
            p = ch % 2
            a_tc, a_tn = a_t[p], "a_t%d" % p
            kvl, kvln = kvl2[p], "kvl%d" % p
            ql, qln = ql2[p], "ql%d" % p
            sch.op("gpsimd", lambda e, t0=t0: e.iota(
                posr[:, :], [[1, 128]], base=t0, channel_multiplier=0,
                allow_small_or_imprecise_dtypes=True),
                writes=("posr",))
            pfb, pfn = p8f[p], "p8f%d" % p
            sch.op("tensor", lambda e, pfb=pfb: e.matmul(
                pfb[:, 0:16], posr[0:1, :], invf[0:1, :],
                start=True, stop=True),
                reads=("posr", "invf"), writes=(pfn,))
            for (half, bias0) in ((0, 0.5), (1, 0.75)):
                sch.op("vector", lambda e, half=half, bias0=bias0, pfb=pfb: e.tensor_scalar(
                    fscr[:, bass.ts(half, 16)], pfb[:, 0:16], 1.0 / TWO_PI, bias0,
                    OP.mult, OP.add),
                    reads=(pfn,), writes=("fscr",))
                sch.op("vector", lambda e, half=half: e.tensor_copy(
                    fqi[:, :], fscr[:, bass.ts(half, 16)]),
                    reads=("fscr",), writes=("fqi",))
                sch.op("vector", lambda e, half=half: e.tensor_copy(
                    fscr[:, bass.ts(half, 16)], fqi[:, :]),
                    reads=("fqi",), writes=("fscr",))
                sch.op("vector", lambda e, half=half, pfb=pfb: e.scalar_tensor_tensor(
                    fscr[:, bass.ts(2 + half, 16)], fscr[:, bass.ts(half, 16)],
                    -TWO_PI, pfb[:, 0:16], OP.mult, OP.add),
                    reads=("fscr", pfn), writes=("fscr",))
            sch.op("scalar", lambda e: e.activation(fscr[:, 0:16], fscr[:, 32:48],
                                                    AF.Sin),
                   reads=("fscr",), writes=("fscr",))
            sch.op("scalar", lambda e: e.activation(fscr[:, 16:32], fscr[:, 48:64],
                                                    AF.Sin, bias=cx.cvals[:, 1:2]),
                   reads=("fscr", "cvals"), writes=("fscr",))
            cosv = fscr[:, 16:32]
            sinv = fscr[:, 0:16]
            sch.op("scalar", lambda e: e.copy(
                cosr[:, :].rearrange("p (h v d) -> p h v d", h=16, v=2),
                cosv.unsqueeze(1).unsqueeze(2).broadcast_to([128, 16, 2, 16])),
                reads=("fscr",), writes=("cosr",))
            sch.op("vector", lambda e: e.tensor_scalar(
                sinr[:, :].rearrange("p (h v d) -> p h v d", h=16, v=2)[:, :, 0:1, :],
                sinv.unsqueeze(1).unsqueeze(2).broadcast_to([128, 16, 1, 16]),
                -1.0, None, OP.mult),
                reads=("fscr",), writes=("sinr",))
            sch.op("scalar", lambda e: e.copy(
                sinr[:, :].rearrange("p (h v d) -> p h v d", h=16, v=2)[:, :, 1:2, :],
                sinv.unsqueeze(1).unsqueeze(2).broadcast_to([128, 16, 1, 16])),
                reads=("fscr",), writes=("sinr",))

            def side(lat_b, latn, nope_w, nope_wn, rot_w, rot_wn, rot_wp, rot_wpn,
                     rot_k, asm, asmn):
                sch.mm_group(pno[:, 0:512],
                             [(lat_b[:, bass.ts(lc, 128)],
                               nope_w[:, bass.ts(lc, 512)]) for lc in range(2)],
                             reads=(latn, nope_wn), writes=("pno",))
                if rot_k == 2:
                    prs = [(lat_b[:, bass.ts(lc, 128)], rot_w[:, bass.ts(lc, 512)])
                           for lc in range(2)]
                    prsp = [(lat_b[:, bass.ts(lc, 128)], rot_wp[:, bass.ts(lc, 512)])
                            for lc in range(2)]
                    rdn = (latn,)
                else:
                    prs = [(a_tc[:, bass.ts(k, 128)], rot_w[:, bass.ts(k, 512)])
                           for k in range(8)]
                    prsp = [(a_tc[:, bass.ts(k, 128)], rot_wp[:, bass.ts(k, 512)])
                            for k in range(8)]
                    rdn = (a_tn,)
                sch.mm_group(prot[:, :], prs, reads=rdn + (rot_wn,), writes=("prot",))
                sch.mm_group(protp[:, :], prsp, reads=rdn + (rot_wpn,), writes=("protp",))
                sch.op("vector", lambda e: e.tensor_tensor(ro1[:, :], prot[:, :],
                                                           cosr[:, :], OP.mult),
                       reads=("prot", "cosr"), writes=("ro1",))
                sch.op("vector", lambda e: e.tensor_tensor(ro2[:, :], protp[:, :],
                                                           sinr[:, :], OP.mult),
                       reads=("protp", "sinr"), writes=("ro2",))
                sch.op("vector", lambda e, asm=asm: e.tensor_tensor(
                    asm[:, :].rearrange("p (h d) -> p h d", h=16)[:, :, 0:32],
                    ro1[:, :].rearrange("p (h d) -> p h d", h=16),
                    ro2[:, :].rearrange("p (h d) -> p h d", h=16), OP.add),
                    reads=("ro1", "ro2"), writes=(asmn,))
                sch.op("scalar", lambda e, asm=asm: e.copy(
                    asm[:, :].rearrange("p (h d) -> p h d", h=16)[:, :, 32:64],
                    pno[:, 0:512].rearrange("p (h d) -> p h d", h=16)),
                    reads=("pno",), writes=(asmn,))

            side(kvl, kvln, ku, "ku", rk, "rk", rkp, "rkp", 8, kasm, "kasm")
            for wv in range(2):
                def ktp(e, wv=wv):
                    for j in range(4):
                        inst = e.transpose(ptr[:, bass.ts(j, 128)],
                                           kasm[:, bass.ts(4 * wv + j, 128)],
                                           cx.id128_bf[:, :])
                    return inst
                sch.op("tensor", ktp, reads=("kasm", "id128_bf"), writes=("ptr",))
                sch.op("scalar", lambda e, wv=wv, t0=t0: e.copy(
                    K_fm[:, :].rearrange("p (pr t) -> p pr t", pr=8)
                    [:, 4 * wv:4 * wv + 4, t0:t0 + 128],
                    ptr[:, :].rearrange("p (a t) -> p a t", a=4)),
                    reads=("ptr",), writes=("K_fm",))
            for half in range(2):
                sch.mm_group(pvv[:, bass.ts(half, 512)],
                             [(kvl[:, bass.ts(lc, 128)],
                               vu[:, lc * D + half * 512:lc * D + (half + 1) * 512])
                              for lc in range(2)],
                             reads=(kvln, "vu"), writes=("pvv",))
                sch.op("scalar", lambda e, half=half, ch=ch: e.copy(
                    V_tm[:, :].rearrange("p (c h e) -> p c h e", c=NCH, h=NH)
                    [:, ch, 8 * half:8 * half + 8, 0:64],
                    pvv[:, bass.ts(half, 512)].rearrange("p (h d) -> p h d", h=8)),
                    reads=("pvv",), writes=("V_tm",))
            side(ql, qln, qu, "qu", rq, "rq", rqp, "rqp", 2, qasm, "qasm")
            for wv in range(2):
                def qtp(e, wv=wv):
                    for j in range(4):
                        inst = e.transpose(ptr[:, bass.ts(j, 128)],
                                           qasm[:, bass.ts(4 * wv + j, 128)],
                                           cx.id128_bf[:, :])
                    return inst
                sch.op("tensor", qtp, reads=("qasm", "id128_bf"), writes=("ptr",))
                sch.op("scalar", lambda e, wv=wv: e.copy(
                    qasm[:, 4 * wv * 128:(4 * wv + 4) * 128], ptr[:, :]),
                    reads=("ptr",), writes=("qasm",))
            sch.op("sync", lambda e, ch=ch: e.dma_start(
                cx.q_h[:, ch * 1024:(ch + 1) * 1024], qasm[:, :]),
                reads=("qasm",), writes=("q_h",), dma=True)
            if ch + 1 < NCH:
                lat_work(ch + 1)

        sch.dma_join()
        sch.flush(blkFA)
        blkFA.__exit__(None, None, None)
        fa.close()

        blkFB = nc.Block()
        blkFB.__enter__()
        ow = sb("ow", [128, 8 * D], BF16)
        Qp2 = [sb("Qp0", [128, 2048], BF16), sb("Qp1", [128, 2048], BF16)]
        PTs = [sb("PT%d" % h, [128, 256], BF16) for h in range(NH)]
        rc = sb("rc", [128, 32], F32)
        Oasm = [sb("Oasm0", [128, 1024], BF16), sb("Oasm1", [128, 1024], BF16)]
        Ofm = sb("Ofm", [128, 16 * 128], BF16)
        y1_t = sb("y1_t", [128, 8 * 128], F32)
        y0_2 = [sb("y0_q0", [128, 8 * 128], F32), sb("y0_q1", [128, 8 * 128], F32)]
        psS = [ps("psS%d" % i, [128, 256], F32) for i in range(2)]
        pPV = ps("pPV", [128, NH * 130], F32)

        s = hbm["owT"][:, :].rearrange("(k p) o -> p k o", p=128)
        sch.op("gpsimd", lambda e, d=t3(ow, 8): e.dma_start(d, s),
               writes=("ow",), dma=True)

        def segs(c0, c1):
            out = []
            while c0 < c1:
                nxt = min(c1, (c0 // 512 + 1) * 512)
                out.append((c0, nxt))
                c0 = nxt
            return out

        def mk_T1(j, leftovers):
            def emit():
                for fn, rdn in leftovers:
                    sch.op("tensor", fn, reads=(rdn, "V_tm"), writes=("pPV",))
                pv3 = pPV[:, :].rearrange("p (h c) -> p h c", c=130)
                rc3 = rc[:, :].rearrange("p (h two) -> p h two", two=2)
                for (h0, h1) in ((0, 4), (4, 8), (8, 12), (12, 16)):
                    sch.op("vector", lambda e, h0=h0, h1=h1: e.reciprocal(
                        rc3[:, h0:h1, 0:1], pv3[:, h0:h1, 64:65]),
                        reads=("pPV",), writes=("rc",))
                for (h0, h1) in ((0, 3), (3, 7), (7, 11), (11, 15), (15, 16)):
                    sch.op("vector", lambda e, h0=h0, h1=h1: e.reciprocal(
                        rc3[:, h0:h1, 1:2], pv3[:, h0:h1, 129:130]),
                        reads=("pPV",), writes=("rc",))
                for h in range(NH):
                    sch.op("vector", lambda e, h=h: e.tensor_scalar(
                        Oasm[0][:, bass.ts(h, 64)], pPV[:, 130 * h:130 * h + 64],
                        rc[:, 2 * h:2 * h + 1], None, OP.mult),
                        reads=("pPV", "rc"), writes=("Oasm0",))
                    sch.op("vector", lambda e, h=h: e.tensor_scalar(
                        Oasm[1][:, bass.ts(h, 64)], pPV[:, 130 * h + 65:130 * h + 129],
                        rc[:, 2 * h + 1:2 * h + 2], None, OP.mult),
                        reads=("pPV", "rc"), writes=("Oasm1",))
                for qq in range(2):
                    y0b, y0n = y0_2[qq], "y0_q%d" % qq
                    t0 = j * 256 + qq * 128
                    sch.op("sync", lambda e, t0=t0, y0b=y0b: e.dma_start(
                        t3(y0b, 8), fm(cx.y0_h, 8, t0, 128)),
                        reads=("y0_h",), writes=(y0n,), dma=True)
                    for wv in range(2):
                        def otp(e, wv=wv, qq=qq):
                            for j3 in range(4):
                                inst = e.transpose(ptr[:, bass.ts(j3, 128)],
                                                   Oasm[qq][:, bass.ts(4 * wv + j3, 128)],
                                                   cx.id128_bf[:, :])
                            return inst
                        sch.op("tensor", otp, reads=("Oasm%d" % qq, "id128_bf"),
                               writes=("ptr",))
                        sch.op("scalar", lambda e, wv=wv, qq=qq: e.copy(
                            Ofm[:, qq * 1024 + wv * 512:qq * 1024 + (wv + 1) * 512],
                            ptr[:, :]),
                            reads=("ptr",), writes=("Ofm",))
            return emit

        def mk_T2(j):
            def emit():
                for qq in range(2):
                    y0b, y0n = y0_2[qq], "y0_q%d" % qq
                    t0 = j * 256 + qq * 128
                    for m2 in range(8):
                        sp = psS[m2 % 2]
                        sn = "psS%d" % (m2 % 2)
                        sch.mm_group(sp[:, 0:128],
                                     [(ow[:, k * D + m2 * 128:k * D + (m2 + 1) * 128],
                                       Ofm[:, qq * 1024 + k * 128:qq * 1024 + (k + 1) * 128])
                                      for k in range(8)],
                                     reads=("ow", "Ofm"), writes=(sn,))
                        sch.op("vector", lambda e, m2=m2, sp=sp, y0b=y0b:
                               e.tensor_tensor(
                                   y1_t[:, bass.ts(m2, 128)],
                                   y0b[:, bass.ts(m2, 128)],
                                   sp[:, 0:128], OP.add),
                               reads=(y0n, sn), writes=("y1_t",))
                    sch.op("sync", lambda e, t0=t0: e.dma_start(
                        fm(cx.y1_h, 8, t0, 128), t3(y1_t, 8)),
                        reads=("y1_t",), writes=("y1_h",), dma=True)
            return emit

        PVQ = 8
        pending_T1 = [None]
        pending_T2 = [None]
        for j in range(T // 256):
            jp = j % 2
            Qp, Qpn = Qp2[jp], "Qp%d" % jp
            sch.op("sync", lambda e, j=j, Qp=Qp: e.dma_start(
                Qp[:, :], cx.q_h[:, 2 * j * 1024:(2 * j + 2) * 1024]),
                reads=("q_h",), writes=(Qpn,), dma=True)
            Q3 = Qp[:, :].rearrange("p (c pr t) -> p c pr t", c=2, pr=8)

            def clr(e):
                for b in range(5):
                    n = min(512, NH * 130 - b * 512)
                    inst = e.matmul(pPV[:, b * 512:b * 512 + n],
                                    cx.zrow_bf[:, :], K_fm[0:1, 0:n],
                                    start=True, stop=True, skip_group_check=True)
                return inst
            pvq = []
            ntk = 2 * j + 2
            cnt = 0
            for tkc in range(ntk):
                for h in range(NH):
                    if cnt == 4:
                        if pending_T1[0] is not None:
                            pending_T1[0]()
                            pending_T1[0] = None
                        sch.op("tensor", clr, reads=("zrow_bf", "K_fm"),
                               writes=("pPV",))
                    cnt += 1
                    i = tkc * NH + h
                    sp = psS[i % 2]
                    spn = "psS%d" % (i % 2)
                    base = 64 * (h % 2)
                    pr = h // 2
                    lhsK = K_fm[base:base + 64,
                                pr * T + tkc * 128:pr * T + (tkc + 1) * 128]
                    PTh = PTs[h]
                    PThn = "PT%d" % h
                    if tkc < 2 * j + 1:
                        def smm(e, sp=sp, lhsK=lhsK, h=h, diag=(tkc == 2 * j), Q3=Q3):
                            b2 = 64 * (h % 2)
                            p2 = h // 2
                            inst = e.matmul(sp[:, 0:256], lhsK,
                                            Q3[b2:b2 + 64, :, p2, :], start=True,
                                            stop=not diag)
                            if diag:
                                inst = e.matmul(sp[:, 0:128], cx.id128_bf[:, :],
                                                cx.maskS_bf[:, :], start=False,
                                                stop=True)
                            return inst
                        sch.op("tensor", smm,
                               reads=("K_fm", Qpn, "id128_bf", "maskS_bf"),
                               writes=(spn,))
                        sch.op("scalar", lambda e, sp=sp, PTh=PTh: e.activation(
                            PTh[:, :], sp[:, 0:256], AF.Exp, scale=0.125),
                            reads=(spn,), writes=(PThn,))
                        q0v = True
                    else:
                        def smm(e, sp=sp, lhsK=lhsK, h=h, Q3=Q3):
                            b2 = 64 * (h % 2)
                            p2 = h // 2
                            e.matmul(sp[:, 128:256], lhsK, Q3[b2:b2 + 64, 1, p2, :],
                                     start=True, stop=False)
                            return e.matmul(sp[:, 128:256], cx.id128_bf[:, :],
                                            cx.maskS_bf[:, :], start=False, stop=True)
                        sch.op("tensor", smm,
                               reads=("K_fm", Qpn, "id128_bf", "maskS_bf"),
                               writes=(spn,))
                        sch.op("scalar", lambda e, sp=sp, PTh=PTh: e.activation(
                            PTh[:, 128:256], sp[:, 128:256], AF.Exp, scale=0.125),
                            reads=(spn,), writes=(PThn,))
                        q0v = False

                    vsl = V_tm[:, tkc * (NH * 65) + h * 65:tkc * (NH * 65) + (h + 1) * 65]

                    def pvmm(e, h=h, q0v=q0v, vsl=vsl, PTh=PTh):
                        inst = None
                        for qi, cbase in ((0, 130 * h), (1, 130 * h + 65)):
                            if qi == 0 and not q0v:
                                continue
                            for (c0, c1) in segs(cbase, cbase + 65):
                                inst = e.matmul(pPV[:, c0:c1],
                                                PTh[:, 128 * qi:128 * qi + 128],
                                                vsl[:, c0 - cbase:c1 - cbase],
                                                start=False, stop=False,
                                                skip_group_check=True)
                        return inst
                    pvq.append((pvmm, PThn))
                    if len(pvq) > PVQ:
                        fn, rdn = pvq.pop(0)
                        sch.op("tensor", fn, reads=(rdn, "V_tm"), writes=("pPV",))
            if pending_T2[0] is not None:
                pending_T2[0]()
                pending_T2[0] = None
            pending_T1[0] = mk_T1(j, list(pvq))
            pending_T2[0] = mk_T2(j)
            pvq = []
        if pending_T1[0] is not None:
            pending_T1[0]()
            pending_T1[0] = None
        if pending_T2[0] is not None:
            pending_T2[0]()
            pending_T2[0] = None
        sch.dma_join()
        sch.flush(blkFB)
        blkFB.__exit__(None, None, None)


def _phase_FM(cx, nc, sch):
    hbm = cx.hbm
    es = ExitStack()
    sb = lambda n, s, d: es.enter_context(nc.sbuf_tensor(n, s, d))
    ps = lambda n, s, d: es.enter_context(nc.psum_tensor(n, s, d))
    TH = T // 2
    NCH = TH // 128

    m_in = sb("m_in", [128, 8 * TH], BF16)
    acc = sb("acc", [128, 8 * TH], F32)
    rw = sb("rw", [128, 8 * 7], BF16)
    rb_bc = sb("rb_bc", [128, 7], F32)
    rbrow = sb("rbrow", [1, 7], F32)
    ln2 = sb("ln2", [128, 8], F32)
    xf2 = [sb("xf_m0", [128, 8 * 128], F32), sb("xf_m1", [128, 8 * 128], F32)]
    y1c2 = [sb("y1c0", [128, 8 * 128], F32), sb("y1c1", [128, 8 * 128], F32)]
    sqf2 = [sb("sqf_m0", [128, 8 * 128], F32), sb("sqf_m1", [128, 8 * 128], F32)]
    rowsF2 = [sb("rowsF_m0", [1, 5 * 128], F32), sb("rowsF_m1", [1, 5 * 128], F32)]
    probs2 = [sb("probs0", [128, 7], F32), sb("probs1", [128, 7], F32)]
    e12 = [sb("e1_0", [128, 7], F32), sb("e1_1", [128, 7], F32)]
    pmk2 = [sb("pmk0", [128, 7], F32), sb("pmk1", [128, 7], F32)]
    e22 = [sb("e2_0", [128, 7], F32), sb("e2_1", [128, 7], F32)]
    rt2 = [sb("rt0", [128, 8], F32), sb("rt1", [128, 8], F32)]
    wgt_tm = sb("wgt_tm", [128, NCH * 7], F32)
    wgt_fm = sb("wgt_fm", [7, TH], F32)
    ehsel = sb("ehsel", [7, 7 * 128], F32)
    gbuf = [sb("gbuf0", [128, 8 * 512], BF16), sb("gbuf1", [128, 8 * 512], BF16)]
    ubuf = [sb("ubuf0", [128, 8 * 512], BF16), sb("ubuf1", [128, 8 * 512], BF16)]
    dbuf = [sb("dbuf0", [128, 4 * D], BF16), sb("dbuf1", [128, 4 * D], BF16)]
    gs2 = [sb("gs0", [128, 512], BF16), sb("gs1", [128, 512], BF16)]
    wbc = sb("wbc", [128, 512], BF16)
    hq2 = [sb("hq0", [128, 4 * 512], BF16), sb("hq1", [128, 4 * 512], BF16)]

    pg = [ps("pgm0", [128, 512], F32), ps("pgm1", [128, 512], F32)]
    pu = [ps("pum0", [128, 512], F32), ps("pum1", [128, 512], F32)]
    pd = [ps("pdm0", [128, 512], F32), ps("pdm1", [128, 512], F32)]
    pmisc = ps("pmisc", [128, 512], F32)
    pwb = ps("pwb", [128, 512], F32)

    with es:
        s = hbm["rwT"][:, :].rearrange("(k p) o -> p k o", p=128)
        sch.op("gpsimd", lambda e, d=t3(rw, 8): e.dma_start(d, s),
               writes=("rw",), dma=True)
        sch.op("sync", lambda e: e.dma_start(rbrow[:, :], hbm["rbias"][:, :]),
               writes=("rbrow",), dma=True)
        sch.op("sync", lambda e: e.dma_start(ln2[:, :], hbm["ln2c"][:, :]),
               writes=("ln2",), dma=True)
        sch.op("tensor", lambda e: e.matmul(pmisc[:, 0:7], cx.ones_f[0:1, 0:128],
                                            rbrow[0:1, :], start=True, stop=True),
               reads=("rbrow", "ones_f"), writes=("pmisc",))
        sch.op("scalar", lambda e: e.copy(rb_bc[:, :], pmisc[:, 0:7]),
               reads=("pmisc",), writes=("rb_bc",))

        sch.op("gpsimd", lambda e: e.memset(ehsel[:, :], 1.0), writes=("ehsel",))
        sch.op("gpsimd", lambda e: e.affine_select(
            out=ehsel[:, :], in_=ehsel[:, :], compare_op=OP.is_ge,
            fill=0.0, base=0, pattern=[[1, 7 * 128]], channel_multiplier=-128),
            reads=("ehsel",), writes=("ehsel",))
        sch.op("gpsimd", lambda e: e.affine_select(
            out=ehsel[:, :], in_=ehsel[:, :], compare_op=OP.is_ge,
            fill=0.0, base=127, pattern=[[-1, 7 * 128]], channel_multiplier=128),
            reads=("ehsel",), writes=("ehsel",))

        def bufs(ch):
            par = ch % 2
            return dict(
                yc=y1c2[par], ycn="y1c%d" % par, sq=sqf2[par], sqn="sqf_m%d" % par,
                xf=xf2[par], xfn="xf_m%d" % par, rf=rowsF2[par],
                rfn="rowsF_m%d" % par,
                pm=pmisc if par == 0 else pwb, pmn="pmisc" if par == 0 else "pwb",
                probs=probs2[par], prn="probs%d" % par, e1=e12[par],
                e1n="e1_%d" % par, pmk=pmk2[par], pkn="pmk%d" % par,
                e2=e22[par], e2n="e2_%d" % par, rt=rt2[par], rtn="rt%d" % par)

        def stageA(half, ch):
            b = bufs(ch)
            yc, ycn, sq, sqn, xf, xfn = (b[k] for k in
                                         ("yc", "ycn", "sq", "sqn", "xf", "xfn"))
            pm, pmn = b["pm"], b["pmn"]
            t0 = half * TH + ch * 128
            sch.op("sync", lambda e, t0=t0, yc=yc: e.dma_start(
                t3(yc, 8), fm(cx.y1_h, 8, t0, 128)),
                reads=("y1_h",), writes=(ycn,), dma=True)
            sch.op("sync", lambda e, t0=t0, xf=xf: e.dma_start(
                t3(xf, 8), fm(cx.x1_h, 8, t0, 128)),
                reads=("x1_h",), writes=(xfn,), dma=True)
            sch.op("scalar", lambda e, sq=sq, yc=yc: e.activation(
                sq[:, :], yc[:, :], AF.Square),
                reads=(ycn,), writes=(sqn,))
            sch.mm_group(pm[0:1, 0:128],
                         [(cx.ones_f[:, 0:1], yc[:, bass.ts(k, 128)]) for k in range(8)],
                         reads=(ycn, "ones_f"), writes=(pmn,))
            sch.mm_group(pm[0:1, 128:256],
                         [(cx.ones_f[:, 0:1], sq[:, bass.ts(k, 128)]) for k in range(8)],
                         reads=(sqn, "ones_f"), writes=(pmn,))

        def stageB(half, ch):
            b = bufs(ch)
            yc, ycn, sq, sqn, xf, xfn, rf, rfn, pm, pmn = (
                b[k] for k in ("yc", "ycn", "sq", "sqn", "xf", "xfn", "rf", "rfn",
                               "pm", "pmn"))
            t0h = ch * 128
            sch.op("vector", lambda e, rf=rf, pm=pm: e.tensor_scalar(
                rf[0:1, 0:256], pm[0:1, 0:256], 1.0 / D, None, OP.mult),
                reads=(pmn,), writes=(rfn,))
            sch.op("vector", lambda e, rf=rf: e.tensor_tensor(
                rf[0:1, 256:384], rf[0:1, 0:128], rf[0:1, 0:128], OP.mult),
                reads=(rfn,), writes=(rfn,))
            sch.op("vector", lambda e, rf=rf: e.tensor_tensor(
                rf[0:1, 256:384], rf[0:1, 128:256], rf[0:1, 256:384], OP.subtract),
                reads=(rfn,), writes=(rfn,))
            sch.op("scalar", lambda e, rf=rf: e.activation(
                rf[0:1, 384:512], rf[0:1, 256:384], AF.Sqrt, bias=cx.cvals[0:1, 0:1]),
                reads=(rfn, "cvals"), writes=(rfn,))
            sch.op("vector", lambda e, rf=rf: e.reciprocal(
                rf[0:1, 384:512], rf[0:1, 384:512]),
                reads=(rfn,), writes=(rfn,))
            sch.op("vector", lambda e, rf=rf: e.tensor_tensor(
                rf[0:1, 512:640], rf[0:1, 0:128], rf[0:1, 384:512], OP.mult),
                reads=(rfn,), writes=(rfn,))
            sch.op("tensor", lambda e, pm=pm, rf=rf: e.matmul(
                pm[:, 256:512], cx.ones_f[0:1, 0:128], rf[0:1, 384:640],
                start=True, stop=True),
                reads=(rfn, "ones_f"), writes=(pmn,))
            sch.op("vector", lambda e, sq=sq, yc=yc, pm=pm: e.tensor_tensor(
                sq[:, :].rearrange("p (k t) -> p k t", k=8),
                yc[:, :].rearrange("p (k t) -> p k t", k=8),
                pm[:, 256:384].unsqueeze(1).broadcast_to([128, 8, 128]), OP.mult),
                reads=(ycn, pmn), writes=(sqn,))
            sch.op("vector", lambda e, sq=sq, pm=pm: e.tensor_tensor(
                sq[:, :].rearrange("p (k t) -> p k t", k=8),
                sq[:, :].rearrange("p (k t) -> p k t", k=8),
                pm[:, 384:512].unsqueeze(1).broadcast_to([128, 8, 128]), OP.subtract),
                reads=(sqn, pmn), writes=(sqn,))
            for k in range(8):
                sch.op("scalar", lambda e, k=k, t0h=t0h, sq=sq: e.mul(
                    m_in[:, k * TH + t0h:k * TH + t0h + 128], sq[:, bass.ts(k, 128)],
                    ln2[:, k:k + 1]),
                    reads=(sqn, "ln2"), writes=("m_in",))
            sch.op("vector", lambda e, t0h=t0h, xf=xf, yc=yc: e.tensor_tensor(
                acc[:, :].rearrange("p (k t) -> p k t", k=8)[:, :, t0h:t0h + 128],
                t3(xf, 8), t3(yc, 8), OP.add),
                reads=(xfn, ycn), writes=("acc",))

        def stageC(half, ch):
            b = bufs(ch)
            pm, pmn, probs, prn, e1, e1n, pmk, pkn, e2, e2n, rt, rtn = (
                b[k] for k in ("pm", "pmn", "probs", "prn", "e1", "e1n", "pmk",
                               "pkn", "e2", "e2n", "rt", "rtn"))
            sch.mm_group(pm[:, 0:7],
                         [(m_in[:, k * TH + ch * 128:k * TH + (ch + 1) * 128],
                           rw[:, bass.ts(k, 7)]) for k in range(8)],
                         reads=("m_in", "rw"), writes=(pmn,))
            sch.op("vector", lambda e, probs=probs, pm=pm: e.tensor_tensor(
                probs[:, :], pm[:, 0:7], rb_bc[:, :], OP.add),
                reads=(pmn, "rb_bc"), writes=(prn,))
            sch.op("scalar", lambda e, probs=probs: e.activation(
                probs[:, :], probs[:, :], AF.Sigmoid),
                reads=(prn,), writes=(prn,))
            sch.op("vector", lambda e, rt=rt, probs=probs: e.reduce_max(
                rt[:, 0:1], probs[:, :], axis=mybir.AxisListType.X),
                reads=(prn,), writes=(rtn,))
            sch.op("vector", lambda e, e1=e1, probs=probs, rt=rt: e.tensor_scalar(
                e1[:, :], probs[:, :], rt[:, 0:1], None, OP.is_equal),
                reads=(prn, rtn), writes=(e1n,))
            sch.op("vector", lambda e, pmk=pmk, e1=e1, probs=probs:
                   e.scalar_tensor_tensor(
                       pmk[:, :], e1[:, :], -1e30, probs[:, :], OP.mult, OP.add),
                   reads=(e1n, prn), writes=(pkn,))
            sch.op("vector", lambda e, rt=rt, pmk=pmk: e.reduce_max(
                rt[:, 1:2], pmk[:, :], axis=mybir.AxisListType.X),
                reads=(pkn,), writes=(rtn,))
            sch.op("vector", lambda e, e2=e2, pmk=pmk, rt=rt: e.tensor_scalar(
                e2[:, :], pmk[:, :], rt[:, 1:2], None, OP.is_equal),
                reads=(pkn, rtn), writes=(e2n,))
            sch.op("vector", lambda e, rt=rt: e.tensor_tensor(
                rt[:, 2:3], rt[:, 0:1], rt[:, 1:2], OP.add),
                reads=(rtn,), writes=(rtn,))
            sch.op("vector", lambda e, rt=rt: e.reciprocal(rt[:, 3:4], rt[:, 2:3]),
                   reads=(rtn,), writes=(rtn,))
            sch.op("vector", lambda e, rt=rt: e.tensor_tensor(
                rt[:, 4:5], rt[:, 0:1], rt[:, 3:4], OP.mult),
                reads=(rtn,), writes=(rtn,))
            sch.op("vector", lambda e, rt=rt: e.tensor_tensor(
                rt[:, 5:6], rt[:, 1:2], rt[:, 3:4], OP.mult),
                reads=(rtn,), writes=(rtn,))
            sch.op("vector", lambda e, e1=e1, rt=rt: e.tensor_scalar(
                e1[:, :], e1[:, :], rt[:, 4:5], None, OP.mult),
                reads=(e1n, rtn), writes=(e1n,))
            sch.op("vector", lambda e, ch=ch, e2=e2, rt=rt, e1=e1:
                   e.scalar_tensor_tensor(
                       wgt_tm[:, bass.ts(ch, 7)], e2[:, :], rt[:, 5:6], e1[:, :],
                       OP.mult, OP.add),
                   reads=(e2n, rtn, e1n), writes=("wgt_tm",))
            sch.op("tensor", lambda e, ch=ch, pm=pm: e.transpose(
                pm[0:7, 0:128], wgt_tm[:, bass.ts(ch, 7)], cx.id128_f[:, :]),
                reads=("wgt_tm", "id128_f"), writes=(pmn,))
            sch.op("scalar", lambda e, ch=ch, pm=pm: e.copy(
                wgt_fm[:, ch * 128:(ch + 1) * 128], pm[0:7, 0:128]),
                reads=(pmn,), writes=("wgt_fm",))

        for half in range(2):
            th0 = half * TH
            for i in range(NCH + 2):
                if i >= 2:
                    stageC(half, i - 2)
                if 1 <= i <= NCH:
                    stageB(half, i - 1)
                if i < NCH:
                    stageA(half, i)

            pending_down = [None]
            step = [0]
            for ex in range(NE):
                for q in range(4):
                    par = (ex * 4 + q) % 2
                    gb, ub, db = gbuf[par], ubuf[par], dbuf[par]
                    sg = hbm["moe_g"][ex * D:(ex + 1) * D,
                                      q * 512:(q + 1) * 512].rearrange(
                        "(k p) o -> p k o", p=128)
                    su = hbm["moe_u"][ex * D:(ex + 1) * D,
                                      q * 512:(q + 1) * 512].rearrange(
                        "(k p) o -> p k o", p=128)
                    sd = hbm["moe_d"][ex * IM + q * 512:ex * IM + (q + 1) * 512,
                                      :].rearrange("(k p) o -> p k o", p=128)
                    sch.op("gpsimd", lambda e, d=t3(gb, 8), s=sg: e.dma_start(d, s),
                           writes=(gb.name,), dma=True)
                    sch.op("gpsimd", lambda e, d=t3(ub, 8), s=su: e.dma_start(d, s),
                           writes=(ub.name,), dma=True)
                    sch.op("gpsimd", lambda e, d=t3(db, 4), s=sd: e.dma_start(d, s),
                           writes=(db.name,), dma=True)
                    for t in range(4):
                        sp_ = step[0] % 2
                        hqc, hqn = hq2[sp_], "hq%d" % sp_
                        gsc, gsn2 = gs2[sp_], "gs%d" % sp_
                        step[0] += 1
                        if ex > 0:
                            sch.op("tensor", lambda e, ex=ex, t=t: e.matmul(
                                pwb[:, :], ehsel[:, bass.ts(ex - 1, 128)],
                                wgt_fm[:, t * 512:(t + 1) * 512],
                                start=True, stop=True),
                                reads=("ehsel", "wgt_fm"), writes=("pwb",))
                            sch.op("scalar", lambda e: e.copy(wbc[:, :], pwb[:, :]),
                                   reads=("pwb",), writes=("wbc",))
                        for mi in range(4):
                            pgi = pg[mi % 2]
                            pgn = "pgm%d" % (mi % 2)
                            pui = pu[mi % 2]
                            pun = "pum%d" % (mi % 2)
                            sch.mm_group(pgi[:, :],
                                         [(gb[:, k * 512 + mi * 128:k * 512 + (mi + 1) * 128],
                                           m_in[:, k * TH + t * 512:k * TH + (t + 1) * 512])
                                          for k in range(8)],
                                         reads=(gb.name, "m_in"), writes=(pgn,))
                            sch.op("scalar", lambda e, pgi=pgi, gsc=gsc: e.activation(
                                gsc[:, :], pgi[:, :], AF.Silu),
                                reads=(pgn,), writes=(gsn2,))
                            sch.mm_group(pui[:, :],
                                         [(ub[:, k * 512 + mi * 128:k * 512 + (mi + 1) * 128],
                                           m_in[:, k * TH + t * 512:k * TH + (t + 1) * 512])
                                          for k in range(8)],
                                         reads=(ub.name, "m_in"), writes=(pun,))
                            if ex > 0:
                                sch.op("vector", lambda e, pui=pui, gsc=gsc: e.tensor_tensor(
                                    gsc[:, :], gsc[:, :], pui[:, :], OP.mult),
                                    reads=(gsn2, pun), writes=(gsn2,))
                                sch.op("vector", lambda e, mi=mi, hqc=hqc, gsc=gsc:
                                       e.tensor_tensor(
                                           hqc[:, bass.ts(mi, 512)], gsc[:, :], wbc[:, :],
                                           OP.mult),
                                       reads=(gsn2, "wbc"), writes=(hqn,))
                            else:
                                sch.op("vector", lambda e, mi=mi, pui=pui, hqc=hqc, gsc=gsc:
                                       e.tensor_tensor(
                                           hqc[:, bass.ts(mi, 512)], gsc[:, :], pui[:, :],
                                           OP.mult),
                                       reads=(gsn2, pun), writes=(hqn,))

                        def mk_down(db=db, hqc=hqc, hqn=hqn, t=t):
                            def emit():
                                for m2 in range(8):
                                    pdi = pd[m2 % 2]
                                    pdn = "pdm%d" % (m2 % 2)
                                    sch.mm_group(pdi[:, :],
                                                 [(db[:, k * D + m2 * 128:k * D + (m2 + 1) * 128],
                                                   hqc[:, bass.ts(k, 512)]) for k in range(4)],
                                                 reads=(db.name, hqn), writes=(pdn,))
                                    sch.op("vector", lambda e, m2=m2, t=t, pdi=pdi:
                                           e.tensor_tensor(
                                               acc[:, m2 * TH + t * 512:m2 * TH + (t + 1) * 512],
                                               acc[:, m2 * TH + t * 512:m2 * TH + (t + 1) * 512],
                                               pdi[:, :], OP.add),
                                           reads=("acc", pdn), writes=("acc",))
                            return emit
                        if pending_down[0] is not None:
                            pending_down[0]()
                        pending_down[0] = mk_down()
            if pending_down[0] is not None:
                pending_down[0]()
                pending_down[0] = None
            sch.op("sync", lambda e, th0=th0: e.dma_start(
                fm(cx.out_fm, 8, th0, TH), t3(acc, 8)),
                reads=("acc",), dma=True)


def kernel(**inputs):
    in_maps = _stage(inputs)
    nc = build_program(dev=False, upto="FM")
    res = run_bass_kernel_spmd(nc, in_maps, core_ids=list(range(B)))
    out = np.empty((B, T, D), np.float32)
    for b in range(B):
        out[b] = np.asarray(res.results[b]["out_fm"]).T
    return out
